# version 10
# baseline (speedup 1.0000x reference)
import contextlib
import math

import numpy as np
import ml_dtypes

import concourse.bass as bass
import concourse.mybir as mybir
from concourse.bass_utils import run_bass_kernel_spmd

F32 = mybir.dt.float32
BF16 = mybir.dt.bfloat16
AF = mybir.ActivationFunctionType
ALU = mybir.AluOpType
AX = mybir.AxisListType

L = 4096
D = 2048
NCH = 16
NH = 16
E = 16
NT = 32
ISC = 1.0 / math.sqrt(128.0)
NEG = -1.0e9

DEBUG = False


class Chain:
    def __init__(self, ctx, nc, name):
        self.h = ctx.enter_context(nc.semaphore(name))
        self.name = name
        self.n = 0
        self.maxwait = 0

    def inc(self, ins, k=1):
        ins.then_inc(self.h, k)
        self.n += k
        return self.n

    def wait(self, eng, tgt=None):
        if tgt is None:
            tgt = self.n
        if tgt <= 0:
            return
        self.maxwait = max(self.maxwait, tgt)
        eng.wait_ge(self.h, tgt)

    def check(self):
        assert self.maxwait <= self.n, \
            f"chain {self.name}: wait {self.maxwait} > final {self.n} (hang!)"


class DmaPool:

    def __init__(self, ctx, nc, name, n=6):
        self.lanes = [Chain(ctx, nc, f"{name}{i}") for i in range(n)]
        self.idx = 0

    def group(self, eng):
        ln = self.lanes[self.idx % len(self.lanes)]
        self.idx += 1
        if ln.n > 0:
            eng.wait_ge(ln.h, ln.n)
        return ln

    def wait_mark(self, eng, mark):
        ln, v = mark
        ln.wait(eng, v)

    def wait_all(self, eng):
        for ln in self.lanes:
            ln.wait(eng, ln.n)

    def check(self):
        for ln in self.lanes:
            ln.check()


def fmap(ap):
    return ap.rearrange("(j p) n -> p j n", p=128)


_SS_OPS = frozenset([
    "tensor_tensor", "tensor_scalar", "tensor_copy", "tensor_add",
    "tensor_sub", "tensor_mul", "tensor_max", "tensor_scalar_add",
    "tensor_scalar_sub", "tensor_scalar_mul", "tensor_scalar_max",
    "tensor_scalar_min", "tensor_relu", "tensor_reduce", "reduce_max",
    "reduce_sum", "scalar_tensor_tensor", "tensor_tensor_scan",
    "reciprocal", "memset", "memzero", "select", "activation",
    "tensor_single_scalar",
])


class SelfSync:

    def __init__(self, eng, chain):
        self._e = eng
        self._c = chain

    def __getattr__(self, name):
        attr = getattr(self._e, name)
        if name in _SS_OPS:
            def wrapped(*a, **k):
                self._c.wait(self._e)
                ins = attr(*a, **k)
                self._c.inc(ins)
                return ins
            return wrapped
        return attr


def build_program():
    nc = bass.Bass("TRN2", target_bir_lowering=False,
                   dynamic_dma_scratch_size=16384)

    x_fm = nc.dram_tensor("x_fm", [D, L], F32, kind="ExternalInput").ap()
    wT = {k: nc.dram_tensor(k, [D, D], F32, kind="ExternalInput").ap()
          for k in ["wqT", "wkT", "wvT", "wuT", "woT"]}
    pallh = {k: nc.dram_tensor(k, [D, D], F32, kind="ExternalInput").ap()
             for k in ["pallq", "pallk", "pallv"]}
    oproj_h = nc.dram_tensor("oprojh", [128, E, D], F32, kind="ExternalInput").ap()
    sim_h = nc.dram_tensor("sim", [D, E], F32, kind="ExternalInput").ap()
    gates_h = nc.dram_tensor("gates", [E, 1], F32, kind="ExternalInput").ap()
    z_fm = nc.dram_tensor("z_fm", [D, L], F32, kind="ExternalOutput").ap()

    SCR = "ExternalOutput" if DEBUG else "Internal"
    qbf_s = nc.dram_tensor("qbf_s", [D, L], BF16, kind=SCR).ap()
    oub_s = nc.dram_tensor("oub_s", [D, L], BF16, kind=SCR).ap()
    y32_s = nc.dram_tensor("y32_s", [D, L], F32, kind=SCR).ap()
    ybf_s = nc.dram_tensor("ybf_s", [D, L], BF16, kind=SCR).ap()
    dbg_arena = nc.dram_tensor("dbg_arena", [D, D], BF16, kind=SCR).ap() if DEBUG else None

    identb_h = nc.inline_tensor(np.eye(128, dtype=ml_dtypes.bfloat16), "c_identb").ap()
    _idf = np.zeros((48, 16), dtype=np.float32)
    _idf[0:16] = np.eye(16); _idf[32:48] = np.eye(16)
    idf32_h = nc.inline_tensor(_idf, "c_idf32").ap()
    causal_np = np.where(np.arange(128)[:, None] >= np.arange(128)[None, :],
                         0.0, NEG).astype(ml_dtypes.bfloat16)
    causal_h = nc.inline_tensor(causal_np, "c_causal").ap()
    onesb_h = nc.inline_tensor(np.ones((128, 1), dtype=ml_dtypes.bfloat16), "c_onesb").ap()
    o1x128_h = nc.inline_tensor(np.ones((1, 128), dtype=np.float32), "c_o1x128").ap()
    o1x16_h = nc.inline_tensor(np.ones((1, 16), dtype=np.float32), "c_o1x16").ap()
    selcol_np = np.zeros((128, 16, 16), dtype=ml_dtypes.bfloat16)
    for h in range(16):
        selcol_np[:, h, h] = 1.0
    selcol_h = nc.inline_tensor(selcol_np.reshape(128, 256), "c_selcol").ap()
    selrow_np = np.zeros((16, 8, 128), dtype=ml_dtypes.bfloat16)
    for h in range(8):
        selrow_np[h, h, :] = 1.0
    selrow_h = nc.inline_tensor(selrow_np.reshape(16, 1024), "c_selrow").ap()
    cv_h = nc.inline_tensor(
        np.broadcast_to(np.array([1e-5], np.float32), (128, 1)).copy(), "c_cv").ap()

    with contextlib.ExitStack() as A:
        sbA = lambda n, s, d=F32: A.enter_context(nc.sbuf_tensor(n, s, d))
        identb = sbA("identb", [128, 128], BF16)
        idf32 = sbA("idf32", [48, 16])
        causal = sbA("causal", [128, 128], BF16)
        onesb = sbA("onesb", [128, 1], BF16)
        onesf = sbA("onesf", [128, 1])
        o1x128 = sbA("o1x128", [1, 128])
        o1x16 = sbA("o1x16", [1, 16])
        selcol = sbA("selcol", [128, 16, 16], BF16)
        selrow = A.enter_context(
            nc.sbuf_tensor("selrow", [16, 8, 128], BF16))
        constv = sbA("constv", [128, 1])
        nc.const_aps.register(F32, 1e-5, constv[:, 0:1])
        arena0 = sbA("arena0", [128, NCH, D], BF16)
        wosum = sbA("wosum", [128, NCH], BF16)
        osum = sbA("osum", [128, E], BF16)
        wosumf = sbA("wosumf", [128, NCH])
        osumf = sbA("osumf", [128, E])
        simn = sbA("simn", [128, NCH, E])
        negcs = sbA("negcs", [1, E])
        sigG = sbA("sigG", [E, 1])
        wrt = sbA("wrt", [128, NT, E])
        wpr = sbA("wpr", [128, NT, E])
        Lsum = sbA("Lsum", [128, NT])
        rt16 = sbA("rt16", [128, 5, 64])
        rtc = sbA("rtc", [128, 12])

        SD = DmaPool(A, nc, "SD", 6)
        SW = DmaPool(A, nc, "SW", 2)
        SM = DmaPool(A, nc, "SM", 4)
        C1 = Chain(A, nc, "C1")
        C2 = Chain(A, nc, "C2")
        C3 = Chain(A, nc, "C3")
        C4 = Chain(A, nc, "C4")
        C5 = Chain(A, nc, "C5")
        SBC = Chain(A, nc, "SBC")
        chains = [SD, SW, SM, C1, C2, C3, C4, C5, SBC]

        block = A.enter_context(nc.Block())
        ENGS = ["tensor", "vector", "scalar", "gpsimd", "sync"]
        Te, Ge, Ye = block.tensor, block.gpsimd, block.sync
        Ve = lambda f: block.vector(lambda e: f(SelfSync(e, C3)))
        Se = lambda f: block.scalar(lambda e: f(SelfSync(e, C2)))

        def barrier():
            def pre(e):
                for ch in (C1, C2, C3, C4, C5):
                    ch.wait(e)
                for pool in (SD, SW, SM):
                    pool.wait_all(e)
                SBC.inc(e.nop())
            for en in ENGS:
                getattr(block, en)(pre)
            tgt = SBC.n
            for en in ENGS:
                getattr(block, en)(lambda e, tgt=tgt: e.wait_ge(SBC.h, tgt))

        def load_arena(e, arena, src, pool):
            ln = pool.group(e)
            for g in range(4):
                ins = e.dma_start(arena[:, 4 * g:4 * g + 4, :],
                                  fmap(src)[:, 4 * g:4 * g + 4, :])
                ln.inc(ins, 16)
            return (ln, ln.n)

        M = {}

        def cwait(e, mark):
            ch, v = mark
            ch.wait(e, v)

        def phase0(warena1):
            def y0(e):
                ln = SM.group(e)
                for dst, src in [
                        (identb[:], identb_h[:]), (idf32[:], idf32_h[:]),
                        (causal[:], causal_h[:]), (onesb[:], onesb_h[:]),
                        (o1x128[:], o1x128_h[:]), (o1x16[:], o1x16_h[:]),
                        (selcol[:].rearrange("p a b -> p (a b)"), selcol_h[:]),
                        (selrow[:].rearrange("p a b -> p (a b)"), selrow_h[:]),
                        (constv[:], cv_h[:]), (sigG[:], gates_h[:]),
                        (simn[:, :, :],
                         sim_h.rearrange("(a p) b -> p a b", p=128))]:
                    ln.inc(e.dma_start(dst, src), 16)
                M[("setup",)] = (ln, ln.n)
            Ye(y0)

            def g0(e):
                M[("wk",)] = load_arena(e, arena0, wT["wkT"], SW)
                M[("wv",)] = load_arena(e, warena1, wT["wvT"], SW)
            Ge(g0)

            def v0(e):
                SM.wait_mark(e, M[("setup",)])
                e.tensor_copy(onesf[:], onesb[:])
            Ve(v0)

            def s0(e):
                C3.wait(e)
                e.activation(sigG[:], sigG[:], AF.Exp, scale=-1.0)
            Se(s0)

            def v1(e):
                C2.wait(e)
                e.tensor_scalar(sigG[:], sigG[:], 1.0, None, ALU.add)
                e.reciprocal(sigG[:], sigG[:])
            Ve(v1)

            with nc.psum_tensor("p0a", [1, 16], F32) as p0a, \
                 nc.psum_tensor("p0b", [128, 16], F32) as p0b, \
                 nc.sbuf_tensor("simsq", [128, 16], F32) as simsq:
                for j in range(NCH):
                    def vq(e, j=j):
                        C3.wait(e)
                        if j > 0:
                            C1.wait(e)
                        e.tensor_tensor(simsq[:], simn[:, j, :],
                                        simn[:, j, :], ALU.mult)
                    Ve(vq)

                    def tq(e, j=j):
                        C3.wait(e)
                        ins = e.matmul(p0a[:], onesf[:], simsq[:],
                                       start=(j == 0), stop=(j == NCH - 1))
                        C1.inc(ins)
                    Te(tq)

                def s1(e):
                    C1.wait(e)
                    e.activation(simsq[0:1, 0:16], p0a[:], AF.Sqrt)
                Se(s1)

                def v2(e):
                    C2.wait(e)
                    e.tensor_scalar(simsq[0:1, 0:16], simsq[0:1, 0:16],
                                    1e-12, None, ALU.max)
                    e.reciprocal(simsq[0:1, 0:16], simsq[0:1, 0:16])
                Ve(v2)

                def t1(e):
                    C3.wait(e)
                    ins = e.matmul(p0b[:], o1x128[:], simsq[0:1, 0:16],
                                   start=True, stop=True)
                    C1.inc(ins)
                Te(t1)

                def v3(e):
                    C1.wait(e)
                    e.tensor_copy(simsq[:], p0b[:])
                    for j in range(NCH):
                        e.tensor_tensor(simn[:, j, :], simn[:, j, :],
                                        simsq[:], ALU.mult)
                Ve(v3)

                for j in range(NCH):
                    def tr(e, j=j):
                        C3.wait(e)
                        ins = e.matmul(p0a[:], onesf[:], simn[:, j, :],
                                       start=(j == 0), stop=(j == NCH - 1))
                        if j == NCH - 1:
                            C1.inc(ins)
                    Te(tr)

                def v4(e):
                    C1.wait(e)
                    e.tensor_scalar(negcs[:], p0a[:], -1.0, None, ALU.mult)
                Ve(v4)

        def phase1(warena1, KVb, xbf2b):
            T = 512
            nchunks = L // T
            KVacc = xbf2b[:].rearrange("p a b -> p (a b)").bitcast(F32) \
                [:, 0:D]
            with nc.sbuf_tensor("xbf1", [128, 2, NCH, T], BF16) as xbf, \
                 nc.sbuf_tensor("ktm", [128, D], BF16) as ktm, \
                 nc.sbuf_tensor("vtm", [128, D], BF16) as vtm, \
                 nc.psum_tensor("pKV", [128, D], F32) as pKV, \
                 nc.psum_tensor("pP1", [128, D], F32) as pP1:
                b1, b2, b3, b4 = C1.n, C2.n, C3.n, C4.n
                xld = {}

                def loadx(e, t):
                    ln = SD.group(e)
                    for grp in range(4):
                        ins = e.dma_start(xbf[:, t % 2, 4 * grp:4 * grp + 4, :],
                                          fmap(x_fm)[:, 4 * grp:4 * grp + 4,
                                                     t * T:(t + 1) * T])
                        ln.inc(ins, 16)
                    xld[t] = (ln, ln.n)

                Ge(lambda e: (loadx(e, 0), loadx(e, 1)))

                for t in range(nchunks):
                    if t >= 2:
                        def gl(e, t=t):
                            C1.wait(e, b1 + 16 * (t - 1))
                            loadx(e, t)
                        Ge(gl)
                    for tau in range(4):
                        tile = t * 4 + tau

                        def tf(e, t=t, tile=tile, tau=tau):
                            SD.wait_mark(e, xld[t])
                            if tile == 0:
                                SW.wait_mark(e, M[("wk",)])
                                SW.wait_mark(e, M[("wv",)])
                            for which, wsrc in [(0, arena0), (1, warena1)]:
                                for half in range(2):
                                    g = 4 * tile + 2 * which + half
                                    if g >= 2:
                                        C2.wait(e, b2 + g - 1)
                                    ins = None
                                    for j in range(NCH):
                                        for nn in range(2):
                                            col = half * 1024 + nn * 512
                                            ins = e.matmul(
                                                pP1[:, col:col + 512],
                                                xbf[:, t % 2, j,
                                                    tau * 128:(tau + 1) * 128],
                                                wsrc[:, j, col:col + 512],
                                                start=(j == 0), stop=(j == NCH - 1))
                                    C1.inc(ins)
                            C2.wait(e, b2 + 4 * tile + 4)
                            if tile > 0:
                                C3.wait(e, b3 + tile)
                            ins = None
                            for h in range(NH):
                                ins = e.matmul(
                                    pKV[:, h * 128:(h + 1) * 128],
                                    ktm[:, h * 128:(h + 1) * 128],
                                    vtm[:, h * 128:(h + 1) * 128],
                                    start=True, stop=True)
                            C4.inc(ins)
                        Te(tf)

                        def vkv(e, tile=tile):
                            C4.wait(e, b4 + tile + 1)
                            if tile == 0:
                                e.tensor_copy(KVacc[:], pKV[:])
                            else:
                                e.tensor_tensor(KVacc[:], KVacc[:], pKV[:],
                                                ALU.add)
                        Ve(vkv)

                        def sf(e, tile=tile):
                            for which, dst in [(0, ktm), (1, vtm)]:
                                for half in range(2):
                                    C1.wait(e, b1 + 4 * tile + 2 * which + half + 1)
                                    if which == 0 and half == 0 and tile > 0:
                                        C4.wait(e, b4 + tile)
                                    func = AF.Relu if which == 0 else AF.Copy
                                    sc = ISC if which == 0 else 1.0
                                    e.activation(
                                        dst[:, half * 1024:(half + 1) * 1024],
                                        pP1[:, half * 1024:(half + 1) * 1024],
                                        func, scale=sc)
                        Se(sf)

                def skv(e):
                    C3.wait(e, b3 + NT)
                    e.activation(KVb[:].rearrange("p a b -> p (a b)"),
                                 KVacc[:], AF.Copy)
                Se(skv)

        def phase2a(warena1):
            T = 512
            nchunks = L // T

            def gw(e):
                M[("wq",)] = load_arena(e, arena0, wT["wqT"], SW)
                M[("wu",)] = load_arena(e, warena1, wT["wuT"], SW)
            Ge(gw)

            with nc.sbuf_tensor("xbf2a", [128, 2, NCH, T], BF16) as xbf, \
                 nc.sbuf_tensor("qst", [128, 2, 8, T], BF16) as qst, \
                 nc.psum_tensor("pQ", [128, 4, T], F32) as pQ:
                b1, b2 = C1.n, C2.n
                xld = {}

                def loadx(e, t):
                    ln = SD.group(e)
                    for grp in range(4):
                        ins = e.dma_start(xbf[:, t % 2, 4 * grp:4 * grp + 4, :],
                                          fmap(x_fm)[:, 4 * grp:4 * grp + 4,
                                                     t * T:(t + 1) * T])
                        ln.inc(ins, 16)
                    xld[t] = (ln, ln.n)

                Ge(lambda e: (loadx(e, 0), loadx(e, 1)))

                for t in range(nchunks):
                    if t >= 2:
                        def gl(e, t=t):
                            C1.wait(e, b1 + 16 * (t - 1))
                            loadx(e, t)
                        Ge(gl)

                    def tf(e, t=t):
                        SD.wait_mark(e, xld[t])
                        if t == 0:
                            SW.wait_mark(e, M[("wq",)])
                        for s in range(16):
                            g = 16 * t + s
                            if g >= 4:
                                C2.wait(e, b2 + g - 3)
                            ins = None
                            for j in range(NCH):
                                ins = e.matmul(pQ[:, s % 4, :],
                                               arena0[:, j, s * 128:(s + 1) * 128],
                                               xbf[:, t % 2, j, :],
                                               start=(j == 0), stop=(j == NCH - 1))
                            C1.inc(ins)
                    Te(tf)

                    def sf(e, t=t):
                        for s in range(16):
                            g = 16 * t + s
                            hglob = 2 * t + s // 8
                            if s % 8 == 0 and hglob >= 2:
                                SM.wait_mark(e, M[("qstore", hglob - 2)])
                            C1.wait(e, b1 + g + 1)
                            e.activation(qst[:, hglob % 2, s % 8, :],
                                         pQ[:, s % 4, :],
                                         AF.Relu, scale=ISC)
                    Se(sf)

                    for hh_ in range(2):
                        def yf(e, t=t, hh_=hh_):
                            hglob = 2 * t + hh_
                            C2.wait(e, b2 + 8 * (hglob + 1))
                            ln = SM.group(e)
                            for grp in range(2):
                                ln.inc(e.dma_start(
                                    fmap(qbf_s)[:, 8 * hh_ + 4 * grp:
                                                8 * hh_ + 4 * grp + 4,
                                                t * T:(t + 1) * T],
                                    qst[:, hglob % 2, 4 * grp:4 * grp + 4, :]),
                                    16)
                            M[("qstore", hglob)] = (ln, ln.n)
                        Ye(yf)
                if DEBUG:
                    def ydbg(e):
                        C1.wait(e)
                        ln = SM.group(e)
                        ln.inc(e.dma_start(
                            dbg_arena.rearrange("(j p) e -> p j e", p=128),
                            arena0[:, :, :]), 16)
                    Ye(ydbg)

        def phase2b(warena1, KVb, xbf):
            T = 512
            nchunks = L // T

            with nc.sbuf_tensor("qc", [128, 8, T], BF16) as qc, \
                 nc.sbuf_tensor("uf", [128, 8, T], F32) as uf, \
                 nc.sbuf_tensor("ofp", [128, 2, T], F32) as ofp, \
                 nc.sbuf_tensor("oust", [128, 8, T], BF16) as oust, \
                 nc.sbuf_tensor("osq", [128, 2, T], BF16) as osq, \
                 nc.sbuf_tensor("invh", [8, T], F32) as invh, \
                 nc.sbuf_tensor("invhb", [8, T], BF16) as invhb, \
                 nc.psum_tensor("pU", [128, 2, T], F32) as pU, \
                 nc.psum_tensor("pO", [128, 2, T], F32) as pO, \
                 nc.psum_tensor("pN", [8, T], F32) as pN, \
                 nc.psum_tensor("pB", [128, 2, T], F32) as pB:
                xld, qld = {}, {}

                def loadx(e, t):
                    ln = SD.group(e)
                    for grp in range(4):
                        ins = e.dma_start(xbf[:, 4 * grp:4 * grp + 4, :],
                                          fmap(x_fm)[:, 4 * grp:4 * grp + 4,
                                                     t * T:(t + 1) * T])
                        ln.inc(ins, 16)
                    xld[t] = (ln, ln.n)

                def loadq(e, t, H):
                    ln = SD.group(e)
                    ins = e.dma_start(qc[:, :, :],
                                      fmap(qbf_s)[:, 8 * H:8 * H + 8,
                                                  t * T:(t + 1) * T])
                    ln.inc(ins, 16)
                    qld[(t, H)] = (ln, ln.n)

                def gpre(e):
                    loadx(e, 0)
                Ge(gpre)
                barrier()

                def gw(e):
                    loadq(e, 0, 0)
                    M[("wo",)] = load_arena(e, arena0, wT["woT"], SW)
                Ge(gw)

                for t in range(nchunks):
                    for H in range(2):
                        hh = 2 * t + H
                        bU1, bU2 = C1.n, C2.n

                        def tU(e, t=t, H=H, hh=hh, bU2=bU2):
                            SD.wait_mark(e, xld[t])
                            if hh == 0:
                                SW.wait_mark(e, M[("wu",)])
                            for s8 in range(8):
                                s = 8 * H + s8
                                if s8 >= 2:
                                    C2.wait(e, bU2 + s8 - 1)
                                elif hh > 0:
                                    C3.wait(e, M[("vB", hh - 1)])
                                ins = None
                                for j in range(NCH):
                                    ins = e.matmul(
                                        pU[:, s8 % 2, :],
                                        warena1[:, j, s * 128:(s + 1) * 128],
                                        xbf[:, j, :],
                                        start=(j == 0), stop=(j == NCH - 1))
                                C1.inc(ins)
                        Te(tU)
                        if H == 1:
                            M[("ux", t)] = C1.n

                        def sU(e, bU1=bU1):
                            for s8 in range(8):
                                C1.wait(e, bU1 + s8 + 1)
                                e.activation(uf[:, s8, :], pU[:, s8 % 2, :],
                                             AF.Silu)
                        Se(sU)

                        bO1, bO2, bO5, bO3 = C1.n, C2.n, C5.n, C3.n

                        def tO(e, t=t, H=H, hh=hh, bO2=bO2):
                            SD.wait_mark(e, qld[(t, H)])
                            for h8 in range(8):
                                if h8 >= 2:
                                    C2.wait(e, bO2 + 2 * h8 - 2)
                                elif hh > 0:
                                    C2.wait(e, M[("ocp", hh - 1)])
                                ins = e.matmul(pO[:, h8 % 2, :],
                                               KVb[:, 8 * H + h8, :],
                                               qc[:, h8, :],
                                               start=True, stop=True)
                                C1.inc(ins)
                                if h8 >= 1:
                                    hprev = h8 - 1
                                    if hprev == 0 and hh > 0:
                                        C2.wait(e, M[("inv", hh - 1)])
                                    C2.wait(e, bO2 + 2 * hprev + 1)
                                    h = 8 * H + hprev
                                    ins = e.matmul(
                                        pN[:], selcol[:, h, 8 * H:8 * H + 8],
                                        osq[:, hprev % 2, :],
                                        start=(hprev == 0), stop=False)
                                    C5.inc(ins)
                            C2.wait(e, bO2 + 15)
                            ins = e.matmul(pN[:], selcol[:, 8 * H + 7,
                                                         8 * H:8 * H + 8],
                                           osq[:, 1, :], start=False, stop=True)
                            C5.inc(ins)
                        Te(tO)
                        M[("oq", t, H)] = C1.n

                        def sO(e, hh=hh, bO1=bO1, bO5=bO5, bO3=bO3):
                            for h8 in range(8):
                                C1.wait(e, bO1 + h8 + 1)
                                if h8 >= 2:
                                    C5.wait(e, bO5 + h8 - 1)
                                    C3.wait(e, bO3 + h8 - 1)
                                elif hh > 0:
                                    C5.wait(e, bO5)
                                    C3.wait(e, bO3)
                                e.activation(osq[:, h8 % 2, :], pO[:, h8 % 2, :],
                                             AF.Square)
                                e.activation(ofp[:, h8 % 2, :],
                                             pO[:, h8 % 2, :], AF.Copy)
                        Se(sO)
                        M[("ocp", hh)] = C2.n

                        def vOU(e, bO2=bO2):
                            for h8 in range(8):
                                C2.wait(e, bO2 + 2 * h8 + 2)
                                e.tensor_tensor(uf[:, h8, :],
                                                ofp[:, h8 % 2, :],
                                                uf[:, h8, :], ALU.mult)
                        Ve(vOU)

                        def sI(e, hh=hh, bO5=bO5):
                            C5.wait(e, bO5 + 8)
                            if hh > 0:
                                C1.wait(e, M[("invmm", hh - 1)])
                            e.activation(invh[:], pN[:], AF.Sqrt,
                                         scale=1.0 / 128.0)
                        Se(sI)
                        M[("inv", hh)] = C2.n

                        def vI(e, hh=hh):
                            C2.wait(e, M[("inv", hh)])
                            e.tensor_scalar(invh[:], invh[:], 1e-12, None, ALU.max)
                            e.reciprocal(invh[:], invh[:])
                            e.tensor_copy(invhb[:], invh[:])
                        Ve(vI)
                        vimark = C3.n

                        bB1, bB3 = C1.n, C3.n

                        def tB(e, vimark=vimark, bB3=bB3):
                            C3.wait(e, vimark)
                            for h8 in range(8):
                                if h8 >= 2:
                                    C3.wait(e, bB3 + h8 - 1)
                                ins = e.matmul(pB[:, h8 % 2, :],
                                               selrow[0:8, h8, :], invhb[:],
                                               start=True, stop=True)
                                C1.inc(ins)
                        Te(tB)
                        M[("invmm", hh)] = C1.n

                        def vB(e, hh=hh, bB1=bB1):
                            for h8 in range(8):
                                C1.wait(e, bB1 + h8 + 1)
                                if h8 == 0 and hh >= 1:
                                    SM.wait_mark(e, M[("oust", hh - 1)])
                                e.tensor_tensor(oust[:, h8, :],
                                                uf[:, h8, :],
                                                pB[:, h8 % 2, :], ALU.mult)
                        Ve(vB)
                        M[("vB", hh)] = C3.n

                        def yS(e, t=t, H=H, hh=hh):
                            C3.wait(e, M[("vB", hh)])
                            ln = SM.group(e)
                            ins = e.dma_start(
                                fmap(oub_s)[:, 8 * H:8 * H + 8, t * T:(t + 1) * T],
                                oust[:, :, :])
                            ln.inc(ins, 16)
                            M[("oust", hh)] = (ln, ln.n)
                        Ye(yS)

                        def gq(e, t=t, H=H):
                            C1.wait(e, M[("oq", t, H)])
                            if H == 0:
                                loadq(e, t, 1)
                            else:
                                if t + 1 < nchunks:
                                    C1.wait(e, M[("ux", t)])
                                    loadx(e, t + 1)
                                    loadq(e, t + 1, 0)
                        Ge(gq)

        def phase2c(oarena, kfm, qfm, vtm, aosb):
            T = 512
            nchunks = L // T

            abuf = oarena[:].rearrange("p a b -> p (a b)").bitcast(F32) \
                .rearrange("p (u j n) -> p u j n", u=2, j=NCH)
            kfv = kfm[:].bitcast(F32)
            r16u = kfv[0:48, 0:512]
            rmu = kfv[0:1, 512:1024]
            rvar = kfv[0:1, 1024:1536]
            rsinv = kfv[0:1, 1536:2048]
            vtv = vtm[:].rearrange("p a b -> p (a b)").bitcast(F32)
            rf = vtv[0:1, 0:512]
            rsq = vtv[0:1, 512:1024]
            ab4ba = vtm[:].rearrange("p a b -> p (a b)")[:, 2048:4096] \
                .rearrange("p (a b) -> p a b", a=4)
            asq4a = qfm[:, 0:2048].rearrange("p (a b) -> p a b", a=4)
            _ou01 = aosb[:].rearrange("p a b -> p (a b)") \
                .rearrange("p (s j n) -> p s j n", s=2, j=4)
            _ou2 = qfm[:, 2048:4096].rearrange("p (j n) -> p j n", j=4)

            def ouv(slot):
                return _ou01[:, slot, :, :] if slot < 2 else _ou2

            with nc.sbuf_tensor("xs", [128, 6, 2, T], F32) as xs, \
                 nc.sbuf_tensor("asq4b", [128, 4, T], BF16) as asq4b, \
                 nc.sbuf_tensor("ab4bb", [128, 4, T], BF16) as ab4bb, \
                 nc.psum_tensor("pW", [128, 4, T], F32) as pW, \
                 nc.psum_tensor("pStat", [1, T], F32) as pStat, \
                 nc.psum_tensor("pR", [16, T], F32) as pR, \
                 nc.psum_tensor("pX", [128, T], F32) as pX, \
                 nc.psum_tensor("pY", [128, T], F32) as pY:

                def loadou(e, t, qf):
                    slot = (4 * t + qf) % 3
                    for h in range(2):
                        ln = SD.group(e)
                        ins = e.dma_start(
                            ouv(slot)[:, 2 * h:2 * h + 2, :],
                            fmap(oub_s)[:, 4 * qf + 2 * h:4 * qf + 2 * h + 2,
                                        t * T:(t + 1) * T])
                        ln.inc(ins, 16)
                        M[("ou", t, qf, h)] = (ln, ln.n)

                def loadxp(e, t, pg):
                    slot = (8 * t + pg) % 6
                    for r in range(2):
                        ln = SD.group(e)
                        ins = e.dma_start(
                            xs[:, slot, r, :],
                            fmap(x_fm)[:, 2 * pg + r:2 * pg + r + 1,
                                       t * T:(t + 1) * T]
                            .rearrange("p o n -> p (o n)"))
                        ln.inc(ins, 16)
                        M[("xp", t, pg, r)] = (ln, ln.n)

                def ginit(e):
                    SM.wait_all(e)
                    loadou(e, 0, 0)
                    loadou(e, 0, 1)
                    loadou(e, 0, 2)
                    for pg in range(4):
                        loadxp(e, 0, pg)
                Ge(ginit)

                def xs4(t, qf):
                    s0 = (8 * t + 2 * qf) % 6
                    return xs[:, s0:s0 + 2, :, :].rearrange(
                        "p a b n -> p (a b) n")

                def emit_group(t, qf, sg):
                    slot = (4 * t + qf) % 3

                    def tqg(e, t=t, qf=qf, sg=sg, slot=slot):
                        if (t, qf, sg) == (0, 0, 0):
                            SW.wait_mark(e, M[("wo",)])
                        elif sg > 0:
                            cwait(e, M[("va", t, qf, sg - 1)])
                        elif qf > 0:
                            cwait(e, M[("va", t, qf - 1, 3)])
                        else:
                            cwait(e, M[("va", t - 1, 3, 3)])
                        if sg == 0:
                            SD.wait_mark(e, M[("ou", t, qf, 0)])
                            SD.wait_mark(e, M[("ou", t, qf, 1)])
                        ins = None
                        for i in range(4):
                            s = 4 * sg + i
                            for j in range(4 * qf, 4 * qf + 4):
                                ins = e.matmul(
                                    pW[:, i, :],
                                    arena0[:, j, s * 128:(s + 1) * 128],
                                    ouv(slot)[:, j - 4 * qf, :],
                                    start=(j == 4 * qf),
                                    stop=(j == 4 * qf + 3))
                        C1.inc(ins)
                    Te(tqg)
                    M[("tg", t, qf, sg)] = C1.n

                    ab4 = abuf[:, t % 2, 4 * sg:4 * sg + 4, :]

                    def vag(e, t=t, qf=qf, sg=sg, ab4=ab4):
                        C1.wait(e, M[("tg", t, qf, sg)])
                        if qf == 0 and t >= 2:
                            SM.wait_mark(e, M[("ystore", t - 2)])
                            if sg == 0:
                                C1.wait(e, M[("tsgm", t - 2, 3)])
                        if sg == qf:
                            for pg in (2 * qf, 2 * qf + 1):
                                SD.wait_mark(e, M[("xp", t, pg, 0)])
                                SD.wait_mark(e, M[("xp", t, pg, 1)])
                        pW4 = pW[:, :, :]
                        if qf == 0:
                            if sg == 0:
                                e.tensor_tensor(ab4, pW4, xs4(t, 0), ALU.add)
                            else:
                                e.tensor_copy(ab4, pW4)
                        else:
                            e.tensor_tensor(ab4, ab4, pW4, ALU.add)
                            if sg == qf:
                                e.tensor_tensor(ab4, ab4, xs4(t, qf),
                                                ALU.add)
                    Ve(vag)
                    M[("va", t, qf, sg)] = (C3, C3.n)

                def emit_stats(t, sg):
                    if True:
                        asq4 = asq4a if sg % 2 == 0 else asq4b[:, :, :]
                        ab4b = ab4ba if sg % 2 == 0 else ab4bb[:, :, :]

                        def ssg(e, t=t, sg=sg, asq4=asq4, ab4b=ab4b):
                            cwait(e, M[("va", t, 3, sg)])
                            if sg >= 2:
                                C1.wait(e, M[("tsgm", t, sg - 2)])
                            elif t > 0:
                                C1.wait(e, M[("tsgm", t - 1, sg + 2)])
                            e.activation(asq4,
                                         abuf[:, t % 2, 4 * sg:4 * sg + 4, :],
                                         AF.Square)
                            e.activation(ab4b,
                                         abuf[:, t % 2, 4 * sg:4 * sg + 4, :],
                                         AF.Copy)
                        Se(ssg)

                        def tsgm(e, t=t, sg=sg):
                            C2.wait(e)
                            if sg >= 2:
                                C3.wait(e, M[("vsq", t, sg - 2)])
                            elif t > 0:
                                C3.wait(e, M[("ln1", t - 1)])
                                C3.wait(e, M[("vud", t - 1)])
                                if sg == 0:
                                    C3.wait(e, M[("vr3", t - 1)])
                            dst = pX if sg % 2 == 0 else pY
                            asq4 = asq4a if sg % 2 == 0 else asq4b[:, :, :]
                            ab4b = ab4ba if sg % 2 == 0 else ab4bb[:, :, :]
                            ins = None
                            for i in range(4):
                                ins = e.matmul(dst[0:1, :], onesb[:],
                                               asq4[:, i, :],
                                               start=(i == 0), stop=(i == 3))
                            for i in range(4):
                                ins = e.matmul(pStat[0:1, :], onesb[:],
                                               ab4b[:, i, :],
                                               start=(sg == 0 and i == 0),
                                               stop=(sg == 3 and i == 3))
                            for i in range(4):
                                s = 4 * sg + i
                                ins = e.matmul(pR[:], simn[:, s, :],
                                               abuf[:, t % 2, s, :],
                                               start=(s == 0), stop=False)
                            C1.inc(ins)
                        Te(tsgm)
                        M[("tsgm", t, sg)] = C1.n

                        def vsqg(e, t=t, sg=sg):
                            C1.wait(e, M[("tsgm", t, sg)])
                            src_ = pX if sg % 2 == 0 else pY
                            if sg == 0:
                                e.tensor_copy(rsq, src_[0:1, :])
                            else:
                                e.tensor_tensor(rsq, rsq, src_[0:1, :],
                                                ALU.add)
                        Ve(vsqg)
                        M[("vsq", t, sg)] = C3.n

                def emit_gld(t, qf):
                    def gld(e, t=t, qf=qf):
                        C1.wait(e, M[("tg", t, qf, 0)])
                        P = 4 * t + qf
                        if P >= 1:
                            tp, qp = divmod(P - 1, 4)
                            cwait(e, M[("va", tp, qp, 3)])
                        if P + 2 < 4 * nchunks:
                            t2, q2 = divmod(P + 2, 4)
                            loadou(e, t2, q2)
                            loadxp(e, t2, 2 * q2)
                            loadxp(e, t2, 2 * q2 + 1)
                    Ge(gld)

                def emit_gpq(qf):
                    def gpq(e, qf=qf):
                        C1.wait(e, M[("tg", nchunks - 1, qf, 3)])
                        for j in range(4 * qf, 4 * qf + 4):
                            ln = (SW if j % 2 == 0 else SD).group(e)
                            ins = e.dma_start(
                                arena0[:, j, :],
                                fmap(pallh["pallq"])[:, j:j + 1, :]
                                .rearrange("p o n -> p (o n)"))
                            ln.inc(ins, 16)
                            M.setdefault(("pqm",), []).append((ln, ln.n))
                    Ge(gpq)

                def emit_vr(t):
                    def vr(e, t=t):
                        C1.wait(e, M[("tsgm", t, 3)])
                        e.tensor_scalar(rmu, pStat[0:1, :], 1.0 / D, None,
                                        ALU.mult)
                        e.tensor_tensor(rf, rmu, rmu, ALU.mult)
                        e.tensor_scalar(rvar, rsq, 1.0 / D, None, ALU.mult)
                        e.tensor_tensor(rvar, rvar, rf, ALU.subtract)
                    Ve(vr)
                    M[("vrd", t)] = C3.n

                    def sr(e):
                        C3.wait(e)
                        e.activation(rsinv, rvar, AF.Sqrt, bias=1e-5)
                        e.activation(rf, rvar, AF.Sqrt, scale=float(D))
                    Se(sr)

                    def vr2(e):
                        C2.wait(e)
                        e.reciprocal(rsinv, rsinv)
                        e.tensor_tensor(rf, rf, rsinv, ALU.mult)
                        e.tensor_scalar(rf, rf, 1e-12, None, ALU.max)
                        e.reciprocal(rf, rf)
                        e.tensor_tensor(rf, rsinv, rf, ALU.mult)
                    Ve(vr2)
                    M[("vr2", t)] = C3.n

                def emit_tb(t):
                    def tb(e, t=t):
                        C3.wait(e, M[("vr2", t)])
                        e.matmul(pX[:], o1x128[:], rmu, start=True, stop=True)
                        ins = e.matmul(pY[:], o1x128[:], rsinv,
                                       start=True, stop=True)
                        C1.inc(ins)
                    Te(tb)
                    M[("tb", t)] = C1.n

                def emit_vy(t, g):
                    def vy(e, t=t, g=g):
                        if g == 0:
                            C1.wait(e, M[("tb", t)])
                        ab4 = abuf[:, t % 2, 4 * g:4 * g + 4, :]
                        bcX = pX[:].rearrange("p (o b) -> p o b", o=1) \
                                   .broadcast_to([128, 4, T])
                        bcY = pY[:].rearrange("p (o b) -> p o b", o=1) \
                                   .broadcast_to([128, 4, T])
                        e.tensor_tensor(ab4, ab4, bcX, ALU.subtract)
                        e.tensor_tensor(ab4, ab4, bcY, ALU.mult)
                    Ve(vy)
                    if g == 3:
                        M[("ln1", t)] = C3.n

                def emit_ys(t):
                    def ys(e, t=t):
                        C3.wait(e, M[("ln1", t)])
                        ln = SM.group(e)
                        for grp in range(4):
                            ln.inc(e.dma_start(
                                fmap(y32_s)[:, 4 * grp:4 * grp + 4,
                                            t * T:(t + 1) * T],
                                abuf[:, t % 2, 4 * grp:4 * grp + 4, :]), 16)
                        M[("ystore", t)] = (ln, ln.n)
                    Ye(ys)

                def emit_tr2(t):
                    def tr2(e, t=t):
                        C3.wait(e, M[("ln1", t)])
                        e.matmul(pR[:], negcs[:], rmu, start=False, stop=True)
                        ins = e.matmul(pX[0:16, :], o1x16[:], rf,
                                       start=True, stop=True)
                        C1.inc(ins)
                    Te(tr2)
                    M[("tr2", t)] = C1.n

                def emit_vr3(t):
                    def vr3(e, t=t):
                        C1.wait(e, M[("tr2", t)])
                        e.tensor_copy(r16u[0:16, :], pX[0:16, :])
                        e.tensor_tensor(r16u[32:48, :], pR[:], r16u[0:16, :],
                                        ALU.mult)
                        e.tensor_scalar(r16u[32:48, :], r16u[32:48, :],
                                        sigG[:], None, ALU.subtract)
                    Ve(vr3)
                    M[("vr3", t)] = C3.n

                def emit_tt(t):
                    def tt(e, t=t):
                        C3.wait(e, M[("vr3", t)])
                        ins = None
                        for u in range(4):
                            ins = e.transpose(
                                pX[:, u * 16:(u + 1) * 16],
                                r16u[32:48, u * 128:(u + 1) * 128],
                                idf32[32:48, :])
                        C1.inc(ins)
                    Te(tt)
                    M[("ttm", t)] = C1.n

                def emit_vu(t):
                    def vu(e, t=t):
                        C1.wait(e, M[("ttm", t)])
                        d3 = lambda ap: ap.rearrange("p (a b) -> p a b", a=4)
                        bc = lambda col: col.rearrange("p (a o) -> p a o", o=1) \
                                            .broadcast_to([128, 4, 16])
                        lt = d3(rt16[:, 0, :])
                        e.tensor_copy(lt, d3(pX[:, 0:64]))
                        m1 = rtc[:, 0:4]
                        e.reduce_max(m1.rearrange("p (a o) -> p a o", o=1), lt,
                                     axis=AX.X)
                        ismax = d3(rt16[:, 1, :])
                        e.tensor_tensor(ismax, lt, bc(m1), ALU.is_ge)
                        l2m = d3(rt16[:, 2, :])
                        e.scalar_tensor_tensor(l2m, ismax, -2e9, lt,
                                               ALU.mult, ALU.add)
                        m2 = rtc[:, 4:8]
                        e.reduce_max(m2.rearrange("p (a o) -> p a o", o=1), l2m,
                                     axis=AX.X)
                        fb = d3(rt16[:, 1, :])
                        e.tensor_tensor(fb, lt, bc(m2), ALU.is_ge)
                        inact = rtc[:, 8:12]
                        e.tensor_single_scalar(inact, m1, 0.0, ALU.is_le)
                        mpos = d3(rt16[:, 3, :])
                        e.tensor_single_scalar(mpos, lt, 0.0, ALU.is_gt)
                        dmask = d3(rt16[:, 4, :])
                        e.tensor_tensor(dmask, fb, mpos, ALU.subtract)
                        tmask = d3(rt16[:, 2, :])
                        e.tensor_tensor(tmask, dmask, bc(inact), ALU.mult)
                        mask = d3(rt16[:, 1, :])
                        e.tensor_tensor(mask, tmask, mpos, ALU.add)
                        gated = d3(rt16[:, 2, :])
                        e.tensor_single_scalar(gated, lt, 0.0, ALU.max)
                        t1_ = d3(rt16[:, 3, :])
                        e.tensor_tensor(t1_, gated, mask, ALU.mult)
                        t2_ = d3(rt16[:, 4, :])
                        e.tensor_scalar(t2_, mask, -1.0, 1e9, ALU.add, ALU.mult)
                        marg = d3(rt16[:, 3, :])
                        e.tensor_tensor(marg, t1_, t2_, ALU.add)
                    Ve(vu)
                    M[("vud", t)] = C3.n

                    def su(e):
                        C3.wait(e)
                        e.activation(rt16[:, 0, :].rearrange(
                                         "p (a b) -> p a b", a=4),
                                     rt16[:, 3, :].rearrange(
                                         "p (a b) -> p a b", a=4),
                                     AF.Exp)
                    Se(su)

                def emit_vu2(t):
                    def vu2(e, t=t):
                        C2.wait(e)
                        d3 = lambda ap: ap.rearrange("p (a b) -> p a b", a=4)
                        bc = lambda col: col.rearrange("p (a o) -> p a o", o=1) \
                                            .broadcast_to([128, 4, 16])
                        p_ = d3(rt16[:, 0, :])
                        S_ = rtc[:, 0:4]
                        e.reduce_sum(S_.rearrange("p (a o) -> p a o", o=1), p_,
                                     axis=AX.X)
                        e.reciprocal(S_, S_)
                        t3 = d3(rt16[:, 2, :])
                        e.tensor_tensor(t3, p_, d3(rt16[:, 1, :]), ALU.mult)
                        e.tensor_tensor(
                            wrt[:, 4 * t:4 * t + 4, :], t3, bc(S_), ALU.mult)
                    Ve(vu2)
                    M[("vu2", t)] = C3.n

                for t in range(nchunks):
                    for qf in range(4):
                        for sg in range(4):
                            emit_group(t, qf, sg)
                            if t > 0:
                                p = t - 1
                                if qf == 0:
                                    emit_stats(p, sg)
                                elif qf == 1:
                                    if sg == 0:
                                        emit_vr(p)
                                    elif sg == 1:
                                        emit_tb(p)
                                    elif sg == 2:
                                        emit_vy(p, 0)
                                        emit_vy(p, 1)
                                    else:
                                        emit_vy(p, 2)
                                        emit_vy(p, 3)
                                        emit_ys(p)
                                elif qf == 2:
                                    if sg == 0:
                                        emit_tr2(p)
                                        emit_vr3(p)
                                    elif sg == 1:
                                        emit_tt(p)
                                        emit_vu(p)
                                    elif sg == 2:
                                        emit_vu2(p)
                        emit_gld(t, qf)
                        if t == nchunks - 1:
                            emit_gpq(qf)
                tl = nchunks - 1
                for sg in range(4):
                    emit_stats(tl, sg)
                emit_vr(tl)
                emit_tb(tl)
                for g in range(4):
                    emit_vy(tl, g)
                emit_ys(tl)
                emit_tr2(tl)
                emit_vr3(tl)
                emit_tt(tl)
                emit_vu(tl)
                emit_vu2(tl)

        def phase3a(vtm, qfm, kfm, oarena, aosb):
            T = 256
            nchunks = L // T
            srcarena = [arena0, oarena, arena0]
            with contextlib.ExitStack() as EE:
                ybf = EE.enter_context(
                    nc.sbuf_tensor("ybf", [128, 2, NCH, T], BF16))
                qtm = EE.enter_context(
                    nc.sbuf_tensor("qtm", [128, NT, 128], BF16))
                ktm2 = EE.enter_context(
                    nc.sbuf_tensor("ktm2", [128, NT, 128], BF16))
                wstage = EE.enter_context(
                    nc.sbuf_tensor("wstage", [128, 8, 128], BF16))
                ybf2 = aosb[:].rearrange("p a b -> p (a b)").rearrange(
                    "p (j n) -> p j n", j=NCH)

                def ybv(gc):
                    sl = gc % 3
                    return ybf[:, sl, :, :] if sl < 2 else ybf2

                def loady3(e, gc):
                    tc = gc % 16
                    ln = SD.group(e)
                    ins = e.dma_start(
                        ybv(gc),
                        fmap(y32_s)[:, :, tc * T:(tc + 1) * T])
                    ln.inc(ins, 16)
                    M[("ybg", gc)] = (ln, ln.n)

                accf = rt16[:, 0:2, :].rearrange("p a b -> p (a b)")
                accf2 = rt16[:, 2:4, :].rearrange("p a b -> p (a b)")
                with nc.psum_tensor("pPE", [128, 2, D], F32) as pPE:
                    def gwk(e):
                        M[("pld", 1)] = load_arena(e, oarena,
                                                   pallh["pallk"], SW)
                        SM.wait_all(e)
                        for gc0 in range(3):
                            loady3(e, gc0)
                    Ge(gwk)
                    for pi, (wname, dst) in enumerate(
                            [("pallq", qtm), ("pallk", ktm2), ("pallv", vtm)]):
                        wsrc = srcarena[pi]
                        for t in range(nchunks):
                            gc = 16 * pi + t

                            def gl(e, gc=gc):
                                if gc + 3 < 48:
                                    C1.wait(e, M[("mmg", gc)])
                                    loady3(e, gc + 3)

                            for tau in range(2):
                                g = 2 * t + tau

                                def tm_(e, pi=pi, t=t, tau=tau, g=g, gc=gc,
                                        wsrc=wsrc):
                                    SD.wait_mark(e, M[("ybg", gc)])
                                    if g == 0:
                                        if pi == 0:
                                            for mk in M[("pqm",)]:
                                                SD.wait_mark(e, mk)
                                        else:
                                            SW.wait_mark(e, M[("pld", pi)])
                                    if g >= 2:
                                        C3.wait(e, M[("acc", pi, g - 2)])
                                    elif pi > 0:
                                        C3.wait(e, M[("accend", pi - 1)])
                                    ins = None
                                    for j in range(NCH):
                                        for nn in range(4):
                                            ins = e.matmul(
                                                pPE[:, g % 2,
                                                    nn * 512:(nn + 1) * 512],
                                                ybv(gc)[:, j,
                                                        tau * 128:(tau + 1) * 128],
                                                wsrc[:, j,
                                                     nn * 512:(nn + 1) * 512],
                                                start=(j == 0),
                                                stop=(j == NCH - 1))
                                    C1.inc(ins)
                                Te(tm_)
                                if tau == 1:
                                    M[("mmg", gc)] = C1.n

                                def vm(e, pi=pi, g=g, dst=dst):
                                    C1.wait(e)
                                    for hf, acc_ in ((0, accf), (1, accf2)):
                                        e.tensor_tensor(
                                            wstage[:, :, :],
                                            pPE[:, g % 2,
                                                hf * 1024:(hf + 1) * 1024]
                                            .rearrange("p (a b) -> p a b", a=8),
                                            wrt[:, g, 8 * hf:8 * hf + 8]
                                            .rearrange("p (a o) -> p a o", o=1)
                                            .broadcast_to([128, 8, 128]),
                                            ALU.mult)
                                        e.reduce_sum(
                                            acc_.rearrange("p (b o) -> p b o",
                                                           o=1),
                                            wstage[:, :, :].rearrange(
                                                "p a b -> p b a"),
                                            axis=AX.X)
                                    e.tensor_tensor(accf, accf, accf2, ALU.add)
                                    e.tensor_copy(dst[:, g, :], accf)
                                Ve(vm)
                                M[("acc", pi, g)] = C3.n
                            Ge(gl)
                        M[("mix", pi)] = C1.n
                        M[("accend", pi)] = C3.n

                        if pi == 0:
                            def gv(e):
                                C1.wait(e, M[("mix", 0)])
                                M[("pld", 2)] = load_arena(
                                    e, arena0, pallh["pallv"], SW)
                            Ge(gv)
                        elif pi == 1:
                            def go2(e):
                                C1.wait(e, M[("mix", 1)])
                                ln = SW.group(e)
                                for g4 in range(4):
                                    ins = e.dma_start(
                                        oarena[:, 4 * g4:4 * g4 + 4, :],
                                        oproj_h[:, 4 * g4:4 * g4 + 4, :])
                                    ln.inc(ins, 16)
                                M[("oproj",)] = (ln, ln.n)
                            Ge(go2)

                with nc.psum_tensor("pT3", [128, 512], BF16) as pT3:
                    for src_, dstf in [(qtm, qfm), (ktm2, kfm)]:
                        for gg in range(8):
                            def tt2(e, src_=src_, gg=gg):
                                C3.wait(e)
                                ins = None
                                for u in range(4):
                                    g = 4 * gg + u
                                    ins = e.transpose(
                                        pT3[:, u * 128:(u + 1) * 128],
                                        src_[:, g, :], identb[:])
                                C1.inc(ins)
                            Te(tt2)

                            def vt2(e, dstf=dstf, gg=gg):
                                C1.wait(e)
                                e.tensor_copy(
                                    dstf[:, gg * 512:(gg + 1) * 512], pT3[:])
                            Ve(vt2)

        def phase3b(vtm, qfm, kfm, aosb, oarena):
            def vos(e):
                SW.wait_mark(e, M[("oproj",)])
                for ei in range(E):
                    e.reduce_sum(osumf[:, ei:ei + 1], oarena[:, ei, :],
                                 axis=AX.X)
                e.tensor_copy(osum[:], osumf[:])
            Ve(vos)

            with nc.sbuf_tensor("Ssb", [128, L], F32) as Ssb, \
                 nc.sbuf_tensor("Pb", [128, 2, L], BF16) as Pb, \
                 nc.sbuf_tensor("PTs", [128, 512], BF16) as PTs, \
                 nc.sbuf_tensor("negm", [128, 1], F32) as negm, \
                 nc.psum_tensor("pS", [128, 2, 512], F32) as pS, \
                 nc.psum_tensor("pPT", [128, 512], BF16) as pPT, \
                 nc.psum_tensor("pAO", [128, 128], F32) as pAO:
                b5 = C5.n
                gkb = [0]
                sevd = {}

                def emit_strip_front(i):
                    Lk = (i + 1) * 128
                    nkb = (Lk + 511) // 512
                    gk0 = gkb[0]

                    b3c, b2c = C3.n, C2.n

                    def tS(e, i=i, Lk=Lk, nkb=nkb, gk0=gk0,
                           b3c=b3c, b2c=b2c):
                        for kb in range(nkb):
                            g = gk0 + kb + 1
                            if kb >= 2:
                                kp = kb - 2
                                if kp % 2 == 0:
                                    C3.wait(e, b3c + kp // 2 + 1)
                                else:
                                    C2.wait(e, b2c + (kp - 1) // 2 + 1)
                            elif g > 2:
                                cwait(e, sevd[g - 2])
                            N = min(512, Lk - kb * 512)
                            ins = e.matmul(pS[:, g % 2, 0:N],
                                           qfm[:, i * 128:(i + 1) * 128],
                                           kfm[:, kb * 512:kb * 512 + N],
                                           start=True, stop=True)
                            C5.inc(ins)
                    Te(tS)
                    gkb[0] += nkb

                    def sSv(e, i=i, Lk=Lk, nkb=nkb, gk0=gk0):
                        if i > 0:
                            C2.wait(e, M[("exp", i - 1)])
                        for kb in range(0, nkb, 2):
                            g = gk0 + kb + 1
                            C5.wait(e, b5 + g)
                            N = min(512, Lk - kb * 512)
                            e.tensor_scalar(Ssb[:, kb * 512:kb * 512 + N],
                                            pS[:, g % 2, 0:N],
                                            ISC, None, ALU.mult)
                            sevd[g] = (C3, C3.n)
                    Ve(sSv)

                    def sSs(e, i=i, Lk=Lk, nkb=nkb, gk0=gk0):
                        for kb in range(1, nkb, 2):
                            g = gk0 + kb + 1
                            C5.wait(e, b5 + g)
                            N = min(512, Lk - kb * 512)
                            e.activation(Ssb[:, kb * 512:kb * 512 + N],
                                         pS[:, g % 2, 0:N],
                                         AF.Copy, scale=ISC)
                            sevd[g] = (C2, C2.n)
                    Se(sSs)

                    def vS(e, i=i, Lk=Lk):
                        C2.wait(e)
                        e.tensor_tensor(Ssb[:, i * 128:(i + 1) * 128],
                                        Ssb[:, i * 128:(i + 1) * 128],
                                        causal[:], ALU.add)
                        e.reduce_max(negm[:], Ssb[:, 0:Lk], axis=AX.X,
                                     negate=True)
                    Ve(vS)
                    M[("vS", i)] = C3.n

                    def sE(e, i=i, Lk=Lk):
                        C3.wait(e)
                        if i >= 2:
                            C1.wait(e, M[("tTall", i - 2)])
                        e.activation(Pb[:, i % 2, 0:Lk], Ssb[:, 0:Lk], AF.Exp,
                                     bias=negm[:],
                                     accum_out=Lsum[:, i:i + 1])
                    Se(sE)
                    M[("exp", i)] = C2.n

                def emit_strip_back(i):
                    ngrp = (i + 4) // 4
                    for g in range(ngrp):
                        j0 = 4 * g
                        j1 = min(i, 4 * g + 3)

                        def tT(e, i=i, j0=j0, j1=j1):
                            C2.wait(e, M[("exp", i)])
                            if ("ptev",) in M:
                                C3.wait(e, M[("ptev",)])
                            ins = None
                            for j in range(j0, j1 + 1):
                                ins = e.transpose(
                                    pPT[:, (j - j0) * 128:(j - j0 + 1) * 128],
                                    Pb[:, i % 2, j * 128:(j + 1) * 128],
                                    identb[:])
                            C1.inc(ins)
                        Te(tT)
                        tmark = C1.n

                        def vT(e, j0=j0, j1=j1, tmark=tmark):
                            C1.wait(e, tmark)
                            if ("pvmm",) in M:
                                C1.wait(e, M[("pvmm",)])
                            e.tensor_copy(PTs[:, 0:(j1 - j0 + 1) * 128],
                                          pPT[:, 0:(j1 - j0 + 1) * 128])
                        Ve(vT)
                        M[("ptev",)] = C3.n

                        def tV(e, i=i, j0=j0, j1=j1):
                            C3.wait(e, M[("ptev",)])
                            if j0 == 0 and i > 0:
                                C2.wait(e, M[("aoev", i - 1)])
                            ins = None
                            for j in range(j0, j1 + 1):
                                ins = e.matmul(
                                    pAO[:],
                                    PTs[:, (j - j0) * 128:(j - j0 + 1) * 128],
                                    vtm[:, j, :],
                                    start=(j == 0), stop=(j == i))
                            C1.inc(ins)
                        Te(tV)
                        M[("pvmm",)] = C1.n

                    def sA(e, i=i):
                        C1.wait(e, M[("pvmm",)])
                        e.activation(aosb[:, i, :], pAO[:], AF.Copy)
                    Se(sA)
                    M[("aoev", i)] = C2.n
                    M[("tTall", i)] = C1.n

                for i in range(NT):
                    emit_strip_front(i)
                    if i > 0:
                        emit_strip_back(i - 1)
                emit_strip_back(NT - 1)

                def vw(e):
                    C2.wait(e, M[("aoev", NT - 1)])
                    e.reciprocal(Lsum[:], Lsum[:])
                    for g in range(NT):
                        e.tensor_scalar(wpr[:, g, :], wrt[:, g, :],
                                        Lsum[:, g:g + 1], None, ALU.mult)
                Ve(vw)
                M[("wpr",)] = C3.n

        def phase3c(aosb, oarena, kfm, qfm, vtm):
            T = 256
            nchunks = L // T
            with nc.sbuf_tensor("a2", [128, 2, NCH, T], F32) as a2, \
                 nc.sbuf_tensor("asq2", [128, 4, T], BF16) as asq2, \
                 nc.psum_tensor("pO2", [128, 4, 512], F32) as pO2, \
                 nc.psum_tensor("pS2", [1, 512], F32) as pS2, \
                 nc.psum_tensor("pB1", [128, 512], F32) as pB1, \
                 nc.psum_tensor("pB2", [128, 512], F32) as pB2, \
                 nc.psum_tensor("pWT", [128, 1024], BF16) as pWT:

                wstA = kfm[:].rearrange("p (e t c) -> p e t c", e=E, t=2)
                waoF = qfm[:].rearrange("p (e c) -> p e c", e=E)
                _rows = vtm[:].rearrange("p a b -> p (a b)").bitcast(F32)
                ys = _rows[:, 0:1024].rearrange("p (a b c) -> p a b c",
                                                a=2, b=2)
                qmu = _rows[0:1, 1024:1280]
                qvar = _rows[0:1, 1280:1536]
                qsq = _rows[0:1, 1536:1792]

                def loady(e, t, pg):
                    ln = SD.group(e)
                    ins = e.dma_start(
                        ys[:, pg % 2, :, :],
                        fmap(y32_s)[:, 2 * pg:2 * pg + 2, t * T:(t + 1) * T])
                    ln.inc(ins, 16)
                    M[("ys", t, pg)] = (ln, ln.n)

                def ginit(e):
                    SM.wait_all(e)
                    loady(e, 0, 0)
                    loady(e, 0, 1)
                Ge(ginit)

                def emit_vscA(t):
                    def vscA(e, t=t):
                        if t == 0:
                            C3.wait(e, M[("wpr",)])
                        else:
                            C1.wait(e, M[("o2mm", t - 1)])
                            C1.wait(e, M[("wtrA", t - 1)])
                        e.tensor_tensor(
                            wstA[:, :, :, :],
                            aosb[:, 2 * t:2 * t + 2, :].rearrange(
                                "p (o t) c -> p o t c", o=1)
                            .broadcast_to([128, E, 2, 128]),
                            wpr[:, 2 * t:2 * t + 2, :].rearrange(
                                "p t e -> p e t").rearrange(
                                "p e (t o) -> p e t o", o=1)
                            .broadcast_to([128, E, 2, 128]),
                            ALU.mult)
                    Ve(vscA)
                    M[("wscA", t)] = C3.n

                def emit_wao(t):
                    for g4 in range(4):
                        def ttr(e, t=t, g4=g4):
                            C3.wait(e, M[("wscA", t)])
                            if (t, g4) != (0, 0):
                                C3.wait(e, M[("wevA",)])
                            ins = None
                            for k in range(8):
                                ei, tau = (8 * g4 + k) // 2, (8 * g4 + k) % 2
                                ins = e.transpose(
                                    pWT[:, k * 128:(k + 1) * 128],
                                    wstA[:, ei, tau, :], identb[:])
                            C1.inc(ins)
                        Te(ttr)
                        M[("wtr", t, g4)] = C1.n

                        def vev(e, t=t, g4=g4):
                            C1.wait(e, M[("wtr", t, g4)])
                            if t > 0:
                                C1.wait(e, M[("o2mm", t - 1)])
                            e.tensor_copy(
                                waoF[:, :, :].rearrange(
                                    "p e (t c) -> p e t c", t=2)
                                [:, 4 * g4:4 * g4 + 4, :, :],
                                pWT[:].rearrange(
                                    "p (e t c) -> p e t c", e=4, t=2))
                        Ve(vev)
                        M[("wevA",)] = C3.n
                    M[("wtrA", t)] = C1.n
                    M[("wev", t, E - 1)] = C3.n

                def emit_out2(t):
                    bQ3 = C3.n
                    for q in range(4):
                        def tq(e, t=t, q=q, bQ3=bQ3):
                            C3.wait(e, M[("wev", t, E - 1)])
                            if q > 0:
                                C3.wait(e, bQ3 + q)
                            ins = None
                            for ei in range(E):
                                for s in range(4 * q, 4 * q + 4):
                                    ins = e.matmul(
                                        pO2[:, s % 4, 0:T],
                                        oarena[:, ei, s * 128:(s + 1) * 128],
                                        waoF[:, ei, :],
                                        start=(ei == 0), stop=(ei == E - 1))
                                if q == 0:
                                    ins = e.matmul(
                                        pS2[0:1, 0:T], osum[:, ei:ei + 1],
                                        waoF[:, ei, :],
                                        start=(ei == 0), stop=(ei == E - 1))
                            C1.inc(ins)
                        Te(tq)
                        M[("o2", t, q)] = C1.n

                        def vvQ(e, t=t, q=q):
                            C1.wait(e, M[("o2", t, q)])
                            if q == 0 and t >= 2:
                                SM.wait_mark(e, M[("zstore", t - 2)])
                            SD.wait_mark(e, M[("ys", t, 2 * q)])
                            SD.wait_mark(e, M[("ys", t, 2 * q + 1)])
                            e.tensor_tensor(
                                a2[:, t % 2, 4 * q:4 * q + 4, :],
                                pO2[:, :, 0:T],
                                ys[:, :, :, :].rearrange("p a b c -> p (a b) c"),
                                ALU.add)
                        Ve(vvQ)

                        def ssQ(e, t=t, q=q):
                            C3.wait(e)
                            e.activation(asq2[:, :, :],
                                         a2[:, t % 2, 4 * q:4 * q + 4, :],
                                         AF.Square)
                        Se(ssQ)

                        def tsQ(e, t=t, q=q):
                            C2.wait(e)
                            C3.wait(e)
                            if q == 0 and t > 0:
                                C3.wait(e, M[("vz", t - 1)])
                            e.matmul(pB1[0:1, 0:2 * T], onesb[:],
                                     asq2[:, 0:2, :].rearrange(
                                         "p a b -> p (a b)"),
                                     start=True, stop=True)
                            ins = e.matmul(pB2[0:1, 0:2 * T], onesb[:],
                                           asq2[:, 2:4, :].rearrange(
                                               "p a b -> p (a b)"),
                                           start=True, stop=True)
                            C1.inc(ins)
                        Te(tsQ)

                        def vsqQ(e, t=t, q=q):
                            C1.wait(e)
                            if q == 0:
                                e.tensor_copy(qsq[:], pB1[0:1, 0:T])
                            else:
                                e.tensor_tensor(qsq[:], qsq[:],
                                                pB1[0:1, 0:T], ALU.add)
                            e.tensor_tensor(qsq[:], qsq[:], pB1[0:1, T:2 * T],
                                            ALU.add)
                            e.tensor_tensor(qsq[:], qsq[:], pB2[0:1, 0:T],
                                            ALU.add)
                            e.tensor_tensor(qsq[:], qsq[:], pB2[0:1, T:2 * T],
                                            ALU.add)
                        Ve(vsqQ)

                        def gly(e, t=t, q=q):
                            C1.wait(e)
                            if q < 3:
                                loady(e, t, 2 * q + 2)
                                loady(e, t, 2 * q + 3)
                            elif t + 1 < nchunks:
                                loady(e, t + 1, 0)
                                loady(e, t + 1, 1)
                        Ge(gly)
                    M[("o2mm", t)] = C1.n

                def emit_rows(t):
                    def vr(e, t=t):
                        C1.wait(e)
                        if t > 0:
                            C1.wait(e, M[("bc", t - 1)])
                        e.tensor_scalar(qvar[:], qsq[:], 1.0 / D, None,
                                        ALU.mult)
                        e.tensor_scalar(qmu[:], pS2[0:1, 0:T], 1.0 / D, None,
                                        ALU.mult)
                        e.tensor_tensor(qsq[:], qmu[:], qmu[:], ALU.mult)
                        e.tensor_tensor(qvar[:], qvar[:], qsq[:],
                                        ALU.subtract)
                    Ve(vr)

                    def sr(e, t=t):
                        C3.wait(e)
                        e.activation(qvar[:], qvar[:], AF.Sqrt, bias=1e-5)
                    Se(sr)

                    def vr2(e, t=t):
                        C2.wait(e)
                        e.reciprocal(qvar[:], qvar[:])
                    Ve(vr2)
                    M[("rows", t)] = C3.n

                def emit_finish(t):
                    def tb(e, t=t):
                        C3.wait(e, M[("rows", t)])
                        e.matmul(pB1[:, 0:T], o1x128[:], qmu[:], start=True,
                                 stop=True)
                        ins = e.matmul(pB2[:, 0:T], o1x128[:], qvar[:],
                                       start=True, stop=True)
                        C1.inc(ins)
                        M[("bc", t)] = C1.n
                    Te(tb)

                    def vz(e, t=t):
                        C1.wait(e, M[("bc", t)])
                        bc1 = pB1[:, 0:T].rearrange("p (o b) -> p o b", o=1) \
                                         .broadcast_to([128, NCH, T])
                        bc2 = pB2[:, 0:T].rearrange("p (o b) -> p o b", o=1) \
                                         .broadcast_to([128, NCH, T])
                        e.tensor_tensor(a2[:, t % 2, :, :], a2[:, t % 2, :, :],
                                        bc1, ALU.subtract)
                        e.tensor_tensor(a2[:, t % 2, :, :], a2[:, t % 2, :, :],
                                        bc2, ALU.mult)
                    Ve(vz)
                    M[("vz", t)] = C3.n
                    M[("z", t)] = C3.n

                    def yz(e, t=t):
                        C3.wait(e, M[("z", t)])
                        ln = SM.group(e)
                        for grp in range(4):
                            ln.inc(e.dma_start(
                                fmap(z_fm)[:, 4 * grp:4 * grp + 4,
                                           t * T:(t + 1) * T],
                                a2[:, t % 2, 4 * grp:4 * grp + 4, :]), 16)
                        M[("zstore", t)] = (ln, ln.n)
                    Ye(yz)

                emit_vscA(0)
                emit_wao(0)
                for t in range(nchunks):
                    emit_out2(t)
                    if t + 1 < nchunks:
                        emit_vscA(t + 1)
                    emit_rows(t)
                    if t + 1 < nchunks:
                        emit_wao(t + 1)
                    emit_finish(t)

                def yfin(e):
                    SM.wait_all(e)
                    SD.wait_all(e)
                Ye(yfin)

        with contextlib.ExitStack() as B:
            warena1 = B.enter_context(
                nc.sbuf_tensor("warena1", [128, NCH, D], BF16))
            KVb = B.enter_context(nc.sbuf_tensor("KVb", [128, NH, 128], BF16))
            xbf2b = B.enter_context(
                nc.sbuf_tensor("xbf2b", [128, NCH, 512], BF16))
            phase0(warena1)
            barrier()
            phase1(warena1, KVb, xbf2b)
            barrier()
            phase2a(warena1)
            phase2b(warena1, KVb, xbf2b)
            barrier()

        with contextlib.ExitStack() as DD:
            vtm = DD.enter_context(nc.sbuf_tensor("vtm3", [128, NT, 128], BF16))
            aosb = DD.enter_context(nc.sbuf_tensor("aosb", [128, NT, 128], BF16))
            qfm = DD.enter_context(nc.sbuf_tensor("qfm", [128, L], BF16))
            kfm = DD.enter_context(nc.sbuf_tensor("kfm", [128, L], BF16))
            oarena = DD.enter_context(nc.sbuf_tensor("oarena", [128, E, D], BF16))
            phase2c(oarena, kfm, qfm, vtm, aosb)
            barrier()
            phase3a(vtm, qfm, kfm, oarena, aosb)
            barrier()
            phase3b(vtm, qfm, kfm, aosb, oarena)
            barrier()
            phase3c(aosb, oarena, kfm, qfm, vtm)

        for c in chains:
            c.check()
    return nc


_NC_CACHE = {}


def _get_program():
    key = ("prog", DEBUG)
    if key not in _NC_CACHE:
        _NC_CACHE[key] = build_program()
    return _NC_CACHE[key]


def kernel(L0_attn_x, L0_attn_wq, L0_attn_wk, L0_attn_wv, L0_attn_wu, L0_attn_wo,
           L0_ffn_sim_matrix, L0_ffn_gates, L0_ffn_q_proj, L0_ffn_k_proj,
           L0_ffn_v_proj, L0_ffn_o_proj):
    nc_prog = _get_program()
    c = np.ascontiguousarray
    shared = {
        "wqT": c(L0_attn_wq.T), "wkT": c(L0_attn_wk.T), "wvT": c(L0_attn_wv.T),
        "wuT": c(L0_attn_wu.T), "woT": c(L0_attn_wo.T),
        "pallq": c(L0_ffn_q_proj.transpose(1, 0, 2).reshape(D, D)),
        "pallk": c(L0_ffn_k_proj.transpose(1, 0, 2).reshape(D, D)),
        "pallv": c(L0_ffn_v_proj.transpose(1, 0, 2).reshape(D, D)),
        "oprojh": c(L0_ffn_o_proj.transpose(1, 0, 2)),
        "sim": c(L0_ffn_sim_matrix),
        "gates": c(L0_ffn_gates.reshape(E, 1)),
    }
    in_maps = []
    for b in range(8):
        m = dict(shared)
        m["x_fm"] = c(L0_attn_x[b].T)
        in_maps.append(m)
    res = run_bass_kernel_spmd(nc_prog, in_maps, core_ids=list(range(8)))
    out = np.stack([res.results[b]["z_fm"].T for b in range(8)], axis=0)
    return np.ascontiguousarray(out.astype(np.float32))


# version 11
# speedup vs baseline: 1.1865x; 1.1865x over previous
import contextlib
import math

import numpy as np
import ml_dtypes

import concourse.bass as bass
import concourse.mybir as mybir
from concourse.bass_utils import run_bass_kernel_spmd

F32 = mybir.dt.float32
BF16 = mybir.dt.bfloat16
AF = mybir.ActivationFunctionType
ALU = mybir.AluOpType
AX = mybir.AxisListType

L = 4096
D = 2048
NCH = 16
NH = 16
E = 16
NT = 32
ISC = 1.0 / math.sqrt(128.0)
NEG = -1.0e9

DEBUG = False


class Chain:
    def __init__(self, ctx, nc, name):
        self.h = ctx.enter_context(nc.semaphore(name))
        self.name = name
        self.n = 0
        self.maxwait = 0

    def inc(self, ins, k=1):
        ins.then_inc(self.h, k)
        self.n += k
        return self.n

    def wait(self, eng, tgt=None):
        if tgt is None:
            tgt = self.n
        if tgt <= 0:
            return
        self.maxwait = max(self.maxwait, tgt)
        eng.wait_ge(self.h, tgt)

    def check(self):
        assert self.maxwait <= self.n, \
            f"chain {self.name}: wait {self.maxwait} > final {self.n} (hang!)"


class DmaPool:

    def __init__(self, ctx, nc, name, n=6):
        self.lanes = [Chain(ctx, nc, f"{name}{i}") for i in range(n)]
        self.idx = 0

    def group(self, eng):
        ln = self.lanes[self.idx % len(self.lanes)]
        self.idx += 1
        if ln.n > 0:
            eng.wait_ge(ln.h, ln.n)
        return ln

    def wait_mark(self, eng, mark):
        ln, v = mark
        ln.wait(eng, v)

    def wait_all(self, eng):
        for ln in self.lanes:
            ln.wait(eng, ln.n)

    def check(self):
        for ln in self.lanes:
            ln.check()


def fmap(ap):
    return ap.rearrange("(j p) n -> p j n", p=128)


_SS_OPS = frozenset([
    "tensor_tensor", "tensor_scalar", "tensor_copy", "tensor_add",
    "tensor_sub", "tensor_mul", "tensor_max", "tensor_scalar_add",
    "tensor_scalar_sub", "tensor_scalar_mul", "tensor_scalar_max",
    "tensor_scalar_min", "tensor_relu", "tensor_reduce", "reduce_max",
    "reduce_sum", "scalar_tensor_tensor", "tensor_tensor_scan",
    "reciprocal", "memset", "memzero", "select", "activation",
    "tensor_single_scalar",
])


class SelfSync:

    def __init__(self, eng, chain):
        self._e = eng
        self._c = chain

    def __getattr__(self, name):
        attr = getattr(self._e, name)
        if name in _SS_OPS:
            def wrapped(*a, **k):
                self._c.wait(self._e)
                ins = attr(*a, **k)
                self._c.inc(ins)
                return ins
            return wrapped
        return attr


def build_program():
    nc = bass.Bass("TRN2", target_bir_lowering=False,
                   dynamic_dma_scratch_size=16384)

    x_fm = nc.dram_tensor("x_fm", [D, L], F32, kind="ExternalInput").ap()
    wT = {k: nc.dram_tensor(k, [D, D], F32, kind="ExternalInput").ap()
          for k in ["wqT", "wkT", "wvT", "wuT", "woT"]}
    pallh = {k: nc.dram_tensor(k, [D, D], F32, kind="ExternalInput").ap()
             for k in ["pallq", "pallk", "pallv"]}
    oproj_h = nc.dram_tensor("oprojh", [128, E, D], F32, kind="ExternalInput").ap()
    sim_h = nc.dram_tensor("sim", [D, E], F32, kind="ExternalInput").ap()
    gates_h = nc.dram_tensor("gates", [E, 1], F32, kind="ExternalInput").ap()
    z_fm = nc.dram_tensor("z_fm", [D, L], F32, kind="ExternalOutput").ap()

    SCR = "ExternalOutput" if DEBUG else "Internal"
    qbf_s = nc.dram_tensor("qbf_s", [D, L], BF16, kind=SCR).ap()
    oub_s = nc.dram_tensor("oub_s", [D, L], BF16, kind=SCR).ap()
    y32_s = nc.dram_tensor("y32_s", [D, L], F32, kind=SCR).ap()
    ybf_s = nc.dram_tensor("ybf_s", [D, L], BF16, kind=SCR).ap()
    dbg_arena = nc.dram_tensor("dbg_arena", [D, D], BF16, kind=SCR).ap() if DEBUG else None

    identb_h = nc.inline_tensor(np.eye(128, dtype=ml_dtypes.bfloat16), "c_identb").ap()
    _idf = np.zeros((48, 16), dtype=np.float32)
    _idf[0:16] = np.eye(16); _idf[32:48] = np.eye(16)
    idf32_h = nc.inline_tensor(_idf, "c_idf32").ap()
    causal_np = np.where(np.arange(128)[:, None] >= np.arange(128)[None, :],
                         0.0, NEG).astype(ml_dtypes.bfloat16)
    causal_h = nc.inline_tensor(causal_np, "c_causal").ap()
    onesb_h = nc.inline_tensor(np.ones((128, 1), dtype=ml_dtypes.bfloat16), "c_onesb").ap()
    o1x128_h = nc.inline_tensor(np.ones((1, 128), dtype=np.float32), "c_o1x128").ap()
    o1x16_h = nc.inline_tensor(np.ones((1, 16), dtype=np.float32), "c_o1x16").ap()
    selcol_np = np.zeros((128, 16, 16), dtype=ml_dtypes.bfloat16)
    for h in range(16):
        selcol_np[:, h, h] = 1.0
    selcol_h = nc.inline_tensor(selcol_np.reshape(128, 256), "c_selcol").ap()
    selrow_np = np.zeros((16, 8, 128), dtype=ml_dtypes.bfloat16)
    for h in range(8):
        selrow_np[h, h, :] = 1.0
    selrow_h = nc.inline_tensor(selrow_np.reshape(16, 1024), "c_selrow").ap()
    cv_h = nc.inline_tensor(
        np.broadcast_to(np.array([1e-5], np.float32), (128, 1)).copy(), "c_cv").ap()

    with contextlib.ExitStack() as A:
        sbA = lambda n, s, d=F32: A.enter_context(nc.sbuf_tensor(n, s, d))
        identb = sbA("identb", [128, 128], BF16)
        idf32 = sbA("idf32", [48, 16])
        causal = sbA("causal", [128, 128], BF16)
        onesb = sbA("onesb", [128, 1], BF16)
        onesf = sbA("onesf", [128, 1])
        o1x128 = sbA("o1x128", [1, 128])
        o1x16 = sbA("o1x16", [1, 16])
        selcol = sbA("selcol", [128, 16, 16], BF16)
        selrow = A.enter_context(
            nc.sbuf_tensor("selrow", [16, 8, 128], BF16))
        constv = sbA("constv", [128, 1])
        nc.const_aps.register(F32, 1e-5, constv[:, 0:1])
        arena0 = sbA("arena0", [128, NCH, D], BF16)
        wosum = sbA("wosum", [128, NCH], BF16)
        osum = sbA("osum", [128, E], BF16)
        wosumf = sbA("wosumf", [128, NCH])
        osumf = sbA("osumf", [128, E])
        simn = sbA("simn", [128, NCH, E])
        negcs = sbA("negcs", [1, E])
        sigG = sbA("sigG", [E, 1])
        wrt = sbA("wrt", [128, NT, E])
        wpr = sbA("wpr", [128, NT, E])
        Lsum = sbA("Lsum", [128, NT])
        rt16 = sbA("rt16", [128, 5, 64])
        rtc = sbA("rtc", [128, 12])

        SD = DmaPool(A, nc, "SD", 6)
        SW = DmaPool(A, nc, "SW", 2)
        SM = DmaPool(A, nc, "SM", 4)
        C1 = Chain(A, nc, "C1")
        C2 = Chain(A, nc, "C2")
        C3 = Chain(A, nc, "C3")
        C4 = Chain(A, nc, "C4")
        C5 = Chain(A, nc, "C5")
        SBC = Chain(A, nc, "SBC")
        chains = [SD, SW, SM, C1, C2, C3, C4, C5, SBC]

        block = A.enter_context(nc.Block())
        ENGS = ["tensor", "vector", "scalar", "gpsimd", "sync"]
        Te, Ge, Ye = block.tensor, block.gpsimd, block.sync
        Ve = lambda f: block.vector(lambda e: f(SelfSync(e, C3)))
        Se = lambda f: block.scalar(lambda e: f(SelfSync(e, C2)))

        def barrier():
            def pre(e):
                for ch in (C1, C2, C3, C4, C5):
                    ch.wait(e)
                for pool in (SD, SW, SM):
                    pool.wait_all(e)
                SBC.inc(e.nop())
            for en in ENGS:
                getattr(block, en)(pre)
            tgt = SBC.n
            for en in ENGS:
                getattr(block, en)(lambda e, tgt=tgt: e.wait_ge(SBC.h, tgt))

        def load_arena(e, arena, src, pool):
            ln = pool.group(e)
            for g in range(4):
                ins = e.dma_start(arena[:, 4 * g:4 * g + 4, :],
                                  fmap(src)[:, 4 * g:4 * g + 4, :])
                ln.inc(ins, 16)
            return (ln, ln.n)

        M = {}

        def cwait(e, mark):
            ch, v = mark
            ch.wait(e, v)

        def phase0(warena1):
            def y0(e):
                ln = SM.group(e)
                for dst, src in [
                        (identb[:], identb_h[:]), (idf32[:], idf32_h[:]),
                        (causal[:], causal_h[:]), (onesb[:], onesb_h[:]),
                        (o1x128[:], o1x128_h[:]), (o1x16[:], o1x16_h[:]),
                        (selcol[:].rearrange("p a b -> p (a b)"), selcol_h[:]),
                        (selrow[:].rearrange("p a b -> p (a b)"), selrow_h[:]),
                        (constv[:], cv_h[:]), (sigG[:], gates_h[:]),
                        (simn[:, :, :],
                         sim_h.rearrange("(a p) b -> p a b", p=128))]:
                    ln.inc(e.dma_start(dst, src), 16)
                M[("setup",)] = (ln, ln.n)
            Ye(y0)

            def g0(e):
                M[("wk",)] = load_arena(e, arena0, wT["wkT"], SW)
                M[("wv",)] = load_arena(e, warena1, wT["wvT"], SW)
            Ge(g0)

            def v0(e):
                SM.wait_mark(e, M[("setup",)])
                e.tensor_copy(onesf[:], onesb[:])
            Ve(v0)

            def s0(e):
                C3.wait(e)
                e.activation(sigG[:], sigG[:], AF.Exp, scale=-1.0)
            Se(s0)

            def v1(e):
                C2.wait(e)
                e.tensor_scalar(sigG[:], sigG[:], 1.0, None, ALU.add)
                e.reciprocal(sigG[:], sigG[:])
            Ve(v1)

            with nc.psum_tensor("p0a", [1, 16], F32) as p0a, \
                 nc.psum_tensor("p0b", [128, 16], F32) as p0b, \
                 nc.sbuf_tensor("simsq", [128, 16], F32) as simsq:
                for j in range(NCH):
                    def vq(e, j=j):
                        C3.wait(e)
                        if j > 0:
                            C1.wait(e)
                        e.tensor_tensor(simsq[:], simn[:, j, :],
                                        simn[:, j, :], ALU.mult)
                    Ve(vq)

                    def tq(e, j=j):
                        C3.wait(e)
                        ins = e.matmul(p0a[:], onesf[:], simsq[:],
                                       start=(j == 0), stop=(j == NCH - 1))
                        C1.inc(ins)
                    Te(tq)

                def s1(e):
                    C1.wait(e)
                    e.activation(simsq[0:1, 0:16], p0a[:], AF.Sqrt)
                Se(s1)

                def v2(e):
                    C2.wait(e)
                    e.tensor_scalar(simsq[0:1, 0:16], simsq[0:1, 0:16],
                                    1e-12, None, ALU.max)
                    e.reciprocal(simsq[0:1, 0:16], simsq[0:1, 0:16])
                Ve(v2)

                def t1(e):
                    C3.wait(e)
                    ins = e.matmul(p0b[:], o1x128[:], simsq[0:1, 0:16],
                                   start=True, stop=True)
                    C1.inc(ins)
                Te(t1)

                def v3(e):
                    C1.wait(e)
                    e.tensor_copy(simsq[:], p0b[:])
                    for j in range(NCH):
                        e.tensor_tensor(simn[:, j, :], simn[:, j, :],
                                        simsq[:], ALU.mult)
                Ve(v3)

                for j in range(NCH):
                    def tr(e, j=j):
                        C3.wait(e)
                        ins = e.matmul(p0a[:], onesf[:], simn[:, j, :],
                                       start=(j == 0), stop=(j == NCH - 1))
                        if j == NCH - 1:
                            C1.inc(ins)
                    Te(tr)

                def v4(e):
                    C1.wait(e)
                    e.tensor_scalar(negcs[:], p0a[:], -1.0, None, ALU.mult)
                Ve(v4)

        def phase1(warena1, KVb, xbf2b):
            T = 512
            nchunks = L // T
            KVacc = xbf2b[:].rearrange("p a b -> p (a b)").bitcast(F32) \
                [:, 0:D]
            with nc.sbuf_tensor("xbf1", [128, 2, NCH, T], BF16) as xbf, \
                 nc.sbuf_tensor("ktm", [128, D], BF16) as ktm, \
                 nc.sbuf_tensor("vtm", [128, D], BF16) as vtm, \
                 nc.psum_tensor("pKV", [128, D], F32) as pKV, \
                 nc.psum_tensor("pP1", [128, D], F32) as pP1:
                b1, b2, b3, b4 = C1.n, C2.n, C3.n, C4.n
                xld = {}

                def loadx(e, t):
                    ln = SD.group(e)
                    for grp in range(4):
                        ins = e.dma_start(xbf[:, t % 2, 4 * grp:4 * grp + 4, :],
                                          fmap(x_fm)[:, 4 * grp:4 * grp + 4,
                                                     t * T:(t + 1) * T])
                        ln.inc(ins, 16)
                    xld[t] = (ln, ln.n)

                Ge(lambda e: (loadx(e, 0), loadx(e, 1)))

                for t in range(nchunks):
                    if t >= 2:
                        def gl(e, t=t):
                            C1.wait(e, b1 + 16 * (t - 1))
                            loadx(e, t)
                        Ge(gl)
                    for tau in range(4):
                        tile = t * 4 + tau

                        def tf(e, t=t, tile=tile, tau=tau):
                            SD.wait_mark(e, xld[t])
                            if tile == 0:
                                SW.wait_mark(e, M[("wk",)])
                                SW.wait_mark(e, M[("wv",)])
                            for which, wsrc in [(0, arena0), (1, warena1)]:
                                for half in range(2):
                                    g = 4 * tile + 2 * which + half
                                    if g >= 2:
                                        C2.wait(e, b2 + g - 1)
                                    ins = None
                                    for j in range(NCH):
                                        for nn in range(2):
                                            col = half * 1024 + nn * 512
                                            ins = e.matmul(
                                                pP1[:, col:col + 512],
                                                xbf[:, t % 2, j,
                                                    tau * 128:(tau + 1) * 128],
                                                wsrc[:, j, col:col + 512],
                                                start=(j == 0), stop=(j == NCH - 1))
                                    C1.inc(ins)
                            C2.wait(e, b2 + 4 * tile + 4)
                            if tile > 0:
                                C3.wait(e, b3 + tile)
                            ins = None
                            for h in range(NH):
                                ins = e.matmul(
                                    pKV[:, h * 128:(h + 1) * 128],
                                    ktm[:, h * 128:(h + 1) * 128],
                                    vtm[:, h * 128:(h + 1) * 128],
                                    start=True, stop=True)
                            C4.inc(ins)
                        Te(tf)

                        def vkv(e, tile=tile):
                            C4.wait(e, b4 + tile + 1)
                            if tile == 0:
                                e.tensor_copy(KVacc[:], pKV[:])
                            else:
                                e.tensor_tensor(KVacc[:], KVacc[:], pKV[:],
                                                ALU.add)
                        Ve(vkv)

                        def sf(e, tile=tile):
                            for which, dst in [(0, ktm), (1, vtm)]:
                                for half in range(2):
                                    C1.wait(e, b1 + 4 * tile + 2 * which + half + 1)
                                    if which == 0 and half == 0 and tile > 0:
                                        C4.wait(e, b4 + tile)
                                    func = AF.Relu if which == 0 else AF.Copy
                                    sc = ISC if which == 0 else 1.0
                                    e.activation(
                                        dst[:, half * 1024:(half + 1) * 1024],
                                        pP1[:, half * 1024:(half + 1) * 1024],
                                        func, scale=sc)
                        Se(sf)

                def skv(e):
                    C3.wait(e, b3 + NT)
                    e.activation(KVb[:].rearrange("p a b -> p (a b)"),
                                 KVacc[:], AF.Copy)
                Se(skv)

        def phase2a(warena1):
            T = 512
            nchunks = L // T

            def gw(e):
                M[("wq",)] = load_arena(e, arena0, wT["wqT"], SW)
                M[("wu",)] = load_arena(e, warena1, wT["wuT"], SW)
            Ge(gw)

            with nc.sbuf_tensor("xbf2a", [128, 2, NCH, T], BF16) as xbf, \
                 nc.sbuf_tensor("qst", [128, 2, 8, T], BF16) as qst, \
                 nc.psum_tensor("pQ", [128, 4, T], F32) as pQ:
                b1, b2 = C1.n, C2.n
                xld = {}

                def loadx(e, t):
                    ln = SD.group(e)
                    for grp in range(4):
                        ins = e.dma_start(xbf[:, t % 2, 4 * grp:4 * grp + 4, :],
                                          fmap(x_fm)[:, 4 * grp:4 * grp + 4,
                                                     t * T:(t + 1) * T])
                        ln.inc(ins, 16)
                    xld[t] = (ln, ln.n)

                Ge(lambda e: (loadx(e, 0), loadx(e, 1)))

                for t in range(nchunks):
                    if t >= 2:
                        def gl(e, t=t):
                            C1.wait(e, b1 + 16 * (t - 1))
                            loadx(e, t)
                        Ge(gl)

                    def tf(e, t=t):
                        SD.wait_mark(e, xld[t])
                        if t == 0:
                            SW.wait_mark(e, M[("wq",)])
                        for s in range(16):
                            g = 16 * t + s
                            if g >= 4:
                                C2.wait(e, b2 + g - 3)
                            ins = None
                            for j in range(NCH):
                                ins = e.matmul(pQ[:, s % 4, :],
                                               arena0[:, j, s * 128:(s + 1) * 128],
                                               xbf[:, t % 2, j, :],
                                               start=(j == 0), stop=(j == NCH - 1))
                            C1.inc(ins)
                    Te(tf)

                    def sf(e, t=t):
                        for s in range(16):
                            g = 16 * t + s
                            hglob = 2 * t + s // 8
                            if s % 8 == 0 and hglob >= 2:
                                SM.wait_mark(e, M[("qstore", hglob - 2)])
                            C1.wait(e, b1 + g + 1)
                            e.activation(qst[:, hglob % 2, s % 8, :],
                                         pQ[:, s % 4, :],
                                         AF.Relu, scale=ISC)
                    Se(sf)

                    for hh_ in range(2):
                        def yf(e, t=t, hh_=hh_):
                            hglob = 2 * t + hh_
                            C2.wait(e, b2 + 8 * (hglob + 1))
                            ln = SM.group(e)
                            for grp in range(2):
                                ln.inc(e.dma_start(
                                    fmap(qbf_s)[:, 8 * hh_ + 4 * grp:
                                                8 * hh_ + 4 * grp + 4,
                                                t * T:(t + 1) * T],
                                    qst[:, hglob % 2, 4 * grp:4 * grp + 4, :]),
                                    16)
                            M[("qstore", hglob)] = (ln, ln.n)
                        Ye(yf)
                if DEBUG:
                    def ydbg(e):
                        C1.wait(e)
                        ln = SM.group(e)
                        ln.inc(e.dma_start(
                            dbg_arena.rearrange("(j p) e -> p j e", p=128),
                            arena0[:, :, :]), 16)
                    Ye(ydbg)

        def phase2b(warena1, KVb, xbf):
            T = 512
            nchunks = L // T

            with nc.sbuf_tensor("qc", [128, 8, T], BF16) as qc, \
                 nc.sbuf_tensor("uf", [128, 8, T], F32) as uf, \
                 nc.sbuf_tensor("ofp", [128, 2, T], F32) as ofp, \
                 nc.sbuf_tensor("oust", [128, 8, T], BF16) as oust, \
                 nc.sbuf_tensor("osq", [128, 2, T], BF16) as osq, \
                 nc.sbuf_tensor("invh", [8, T], F32) as invh, \
                 nc.sbuf_tensor("invhb", [8, T], BF16) as invhb, \
                 nc.psum_tensor("pU", [128, 2, T], F32) as pU, \
                 nc.psum_tensor("pO", [128, 2, T], F32) as pO, \
                 nc.psum_tensor("pN", [8, T], F32) as pN, \
                 nc.psum_tensor("pB", [128, 2, T], F32) as pB:
                xld, qld = {}, {}

                def loadx(e, t):
                    ln = SD.group(e)
                    for grp in range(4):
                        ins = e.dma_start(xbf[:, 4 * grp:4 * grp + 4, :],
                                          fmap(x_fm)[:, 4 * grp:4 * grp + 4,
                                                     t * T:(t + 1) * T])
                        ln.inc(ins, 16)
                    xld[t] = (ln, ln.n)

                def loadq(e, t, H):
                    ln = SD.group(e)
                    ins = e.dma_start(qc[:, :, :],
                                      fmap(qbf_s)[:, 8 * H:8 * H + 8,
                                                  t * T:(t + 1) * T])
                    ln.inc(ins, 16)
                    qld[(t, H)] = (ln, ln.n)

                def gpre(e):
                    loadx(e, 0)
                Ge(gpre)
                barrier()

                def gw(e):
                    loadq(e, 0, 0)
                    M[("wo",)] = load_arena(e, arena0, wT["woT"], SW)
                Ge(gw)

                for t in range(nchunks):
                    for H in range(2):
                        hh = 2 * t + H
                        bU1, bU2 = C1.n, C2.n

                        def tU(e, t=t, H=H, hh=hh, bU2=bU2):
                            SD.wait_mark(e, xld[t])
                            if hh == 0:
                                SW.wait_mark(e, M[("wu",)])
                            for s8 in range(8):
                                s = 8 * H + s8
                                if s8 >= 2:
                                    C2.wait(e, bU2 + s8 - 1)
                                elif hh > 0:
                                    C3.wait(e, M[("vB", hh - 1)])
                                ins = None
                                for j in range(NCH):
                                    ins = e.matmul(
                                        pU[:, s8 % 2, :],
                                        warena1[:, j, s * 128:(s + 1) * 128],
                                        xbf[:, j, :],
                                        start=(j == 0), stop=(j == NCH - 1))
                                C1.inc(ins)
                        Te(tU)
                        if H == 1:
                            M[("ux", t)] = C1.n

                        def sU(e, bU1=bU1):
                            for s8 in range(8):
                                C1.wait(e, bU1 + s8 + 1)
                                e.activation(uf[:, s8, :], pU[:, s8 % 2, :],
                                             AF.Silu)
                        Se(sU)

                        bO1, bO2, bO5, bO3 = C1.n, C2.n, C5.n, C3.n

                        def tO(e, t=t, H=H, hh=hh, bO2=bO2):
                            SD.wait_mark(e, qld[(t, H)])
                            for h8 in range(8):
                                if h8 >= 2:
                                    C2.wait(e, bO2 + 2 * h8 - 2)
                                elif hh > 0:
                                    C2.wait(e, M[("ocp", hh - 1)])
                                ins = e.matmul(pO[:, h8 % 2, :],
                                               KVb[:, 8 * H + h8, :],
                                               qc[:, h8, :],
                                               start=True, stop=True)
                                C1.inc(ins)
                                if h8 >= 1:
                                    hprev = h8 - 1
                                    if hprev == 0 and hh > 0:
                                        C2.wait(e, M[("inv", hh - 1)])
                                    C2.wait(e, bO2 + 2 * hprev + 1)
                                    h = 8 * H + hprev
                                    ins = e.matmul(
                                        pN[:], selcol[:, h, 8 * H:8 * H + 8],
                                        osq[:, hprev % 2, :],
                                        start=(hprev == 0), stop=False)
                                    C5.inc(ins)
                            C2.wait(e, bO2 + 15)
                            ins = e.matmul(pN[:], selcol[:, 8 * H + 7,
                                                         8 * H:8 * H + 8],
                                           osq[:, 1, :], start=False, stop=True)
                            C5.inc(ins)
                        Te(tO)
                        M[("oq", t, H)] = C1.n

                        def sO(e, hh=hh, bO1=bO1, bO5=bO5, bO3=bO3):
                            for h8 in range(8):
                                C1.wait(e, bO1 + h8 + 1)
                                if h8 >= 2:
                                    C5.wait(e, bO5 + h8 - 1)
                                    C3.wait(e, bO3 + h8 - 1)
                                elif hh > 0:
                                    C5.wait(e, bO5)
                                    C3.wait(e, bO3)
                                e.activation(osq[:, h8 % 2, :], pO[:, h8 % 2, :],
                                             AF.Square)
                                e.activation(ofp[:, h8 % 2, :],
                                             pO[:, h8 % 2, :], AF.Copy)
                        Se(sO)
                        M[("ocp", hh)] = C2.n

                        def vOU(e, bO2=bO2):
                            for h8 in range(8):
                                C2.wait(e, bO2 + 2 * h8 + 2)
                                e.tensor_tensor(uf[:, h8, :],
                                                ofp[:, h8 % 2, :],
                                                uf[:, h8, :], ALU.mult)
                        Ve(vOU)

                        def sI(e, hh=hh, bO5=bO5):
                            C5.wait(e, bO5 + 8)
                            if hh > 0:
                                C1.wait(e, M[("invmm", hh - 1)])
                            e.activation(invh[:], pN[:], AF.Sqrt,
                                         scale=1.0 / 128.0)
                        Se(sI)
                        M[("inv", hh)] = C2.n

                        def vI(e, hh=hh):
                            C2.wait(e, M[("inv", hh)])
                            e.tensor_scalar(invh[:], invh[:], 1e-12, None, ALU.max)
                            e.reciprocal(invh[:], invh[:])
                            e.tensor_copy(invhb[:], invh[:])
                        Ve(vI)
                        vimark = C3.n

                        bB1, bB3 = C1.n, C3.n

                        def tB(e, vimark=vimark, bB3=bB3):
                            C3.wait(e, vimark)
                            for h8 in range(8):
                                if h8 >= 2:
                                    C3.wait(e, bB3 + h8 - 1)
                                ins = e.matmul(pB[:, h8 % 2, :],
                                               selrow[0:8, h8, :], invhb[:],
                                               start=True, stop=True)
                                C1.inc(ins)
                        Te(tB)
                        M[("invmm", hh)] = C1.n

                        def vB(e, hh=hh, bB1=bB1):
                            for h8 in range(8):
                                C1.wait(e, bB1 + h8 + 1)
                                if h8 == 0 and hh >= 1:
                                    SM.wait_mark(e, M[("oust", hh - 1)])
                                e.tensor_tensor(oust[:, h8, :],
                                                uf[:, h8, :],
                                                pB[:, h8 % 2, :], ALU.mult)
                        Ve(vB)
                        M[("vB", hh)] = C3.n

                        def yS(e, t=t, H=H, hh=hh):
                            C3.wait(e, M[("vB", hh)])
                            ln = SM.group(e)
                            ins = e.dma_start(
                                fmap(oub_s)[:, 8 * H:8 * H + 8, t * T:(t + 1) * T],
                                oust[:, :, :])
                            ln.inc(ins, 16)
                            M[("oust", hh)] = (ln, ln.n)
                        Ye(yS)

                        def gq(e, t=t, H=H):
                            C1.wait(e, M[("oq", t, H)])
                            if H == 0:
                                loadq(e, t, 1)
                            else:
                                if t + 1 < nchunks:
                                    C1.wait(e, M[("ux", t)])
                                    loadx(e, t + 1)
                                    loadq(e, t + 1, 0)
                        Ge(gq)

        def phase2c(oarena, kfm, qfm, vtm, aosb):
            T = 512
            nchunks = L // T

            abuf = oarena[:].rearrange("p a b -> p (a b)").bitcast(F32) \
                .rearrange("p (u j n) -> p u j n", u=2, j=NCH)
            kfv = kfm[:].bitcast(F32)
            r16u = kfv[0:48, 0:512]
            rmu = kfv[0:1, 512:1024]
            rvar = kfv[0:1, 1024:1536]
            rsinv = kfv[0:1, 1536:2048]
            vtv = vtm[:].rearrange("p a b -> p (a b)").bitcast(F32)
            rf = vtv[0:1, 0:512]
            rsq = vtv[0:1, 512:1024]
            ab4ba = vtm[:].rearrange("p a b -> p (a b)")[:, 2048:4096] \
                .rearrange("p (a b) -> p a b", a=4)
            asq4a = qfm[:, 0:2048].rearrange("p (a b) -> p a b", a=4)
            _ou01 = aosb[:].rearrange("p a b -> p (a b)") \
                .rearrange("p (s j n) -> p s j n", s=2, j=4)
            _ou2 = qfm[:, 2048:4096].rearrange("p (j n) -> p j n", j=4)

            def ouv(slot):
                return _ou01[:, slot, :, :] if slot < 2 else _ou2

            with nc.sbuf_tensor("xs", [128, 6, 2, T], F32) as xs, \
                 nc.sbuf_tensor("asq4b", [128, 4, T], BF16) as asq4b, \
                 nc.sbuf_tensor("ab4bb", [128, 4, T], BF16) as ab4bb, \
                 nc.psum_tensor("pW", [128, 4, T], F32) as pW, \
                 nc.psum_tensor("pStat", [1, T], F32) as pStat, \
                 nc.psum_tensor("pR", [16, T], F32) as pR, \
                 nc.psum_tensor("pX", [128, T], F32) as pX, \
                 nc.psum_tensor("pY", [128, T], F32) as pY:

                def loadou(e, t, qf):
                    slot = (4 * t + qf) % 3
                    for h in range(2):
                        ln = SD.group(e)
                        ins = e.dma_start(
                            ouv(slot)[:, 2 * h:2 * h + 2, :],
                            fmap(oub_s)[:, 4 * qf + 2 * h:4 * qf + 2 * h + 2,
                                        t * T:(t + 1) * T])
                        ln.inc(ins, 16)
                        M[("ou", t, qf, h)] = (ln, ln.n)

                def loadxp(e, t, pg):
                    slot = (8 * t + pg) % 6
                    for r in range(2):
                        ln = SD.group(e)
                        ins = e.dma_start(
                            xs[:, slot, r, :],
                            fmap(x_fm)[:, 2 * pg + r:2 * pg + r + 1,
                                       t * T:(t + 1) * T]
                            .rearrange("p o n -> p (o n)"))
                        ln.inc(ins, 16)
                        M[("xp", t, pg, r)] = (ln, ln.n)

                def ginit(e):
                    SM.wait_all(e)
                    loadou(e, 0, 0)
                    loadou(e, 0, 1)
                    loadou(e, 0, 2)
                    for pg in range(4):
                        loadxp(e, 0, pg)
                Ge(ginit)

                def xs4(t, qf):
                    s0 = (8 * t + 2 * qf) % 6
                    return xs[:, s0:s0 + 2, :, :].rearrange(
                        "p a b n -> p (a b) n")

                def emit_group(t, qf, sg):
                    slot = (4 * t + qf) % 3

                    def tqg(e, t=t, qf=qf, sg=sg, slot=slot):
                        if (t, qf, sg) == (0, 0, 0):
                            SW.wait_mark(e, M[("wo",)])
                        elif sg > 0:
                            cwait(e, M[("va", t, qf, sg - 1)])
                        elif qf > 0:
                            cwait(e, M[("va", t, qf - 1, 3)])
                        else:
                            cwait(e, M[("va", t - 1, 3, 3)])
                        if sg == 0:
                            SD.wait_mark(e, M[("ou", t, qf, 0)])
                            SD.wait_mark(e, M[("ou", t, qf, 1)])
                        ins = None
                        for i in range(4):
                            s = 4 * sg + i
                            for j in range(4 * qf, 4 * qf + 4):
                                ins = e.matmul(
                                    pW[:, i, :],
                                    arena0[:, j, s * 128:(s + 1) * 128],
                                    ouv(slot)[:, j - 4 * qf, :],
                                    start=(j == 4 * qf),
                                    stop=(j == 4 * qf + 3))
                        C1.inc(ins)
                    Te(tqg)
                    M[("tg", t, qf, sg)] = C1.n

                    ab4 = abuf[:, t % 2, 4 * sg:4 * sg + 4, :]

                    def vag(e, t=t, qf=qf, sg=sg, ab4=ab4):
                        C1.wait(e, M[("tg", t, qf, sg)])
                        if qf == 0 and t >= 2:
                            SM.wait_mark(e, M[("ystore", t - 2)])
                            if sg == 0:
                                C1.wait(e, M[("tsgm", t - 2, 3)])
                        if sg == qf:
                            for pg in (2 * qf, 2 * qf + 1):
                                SD.wait_mark(e, M[("xp", t, pg, 0)])
                                SD.wait_mark(e, M[("xp", t, pg, 1)])
                        pW4 = pW[:, :, :]
                        if qf == 0:
                            if sg == 0:
                                e.tensor_tensor(ab4, pW4, xs4(t, 0), ALU.add)
                            else:
                                e.tensor_copy(ab4, pW4)
                        else:
                            e.tensor_tensor(ab4, ab4, pW4, ALU.add)
                            if sg == qf:
                                e.tensor_tensor(ab4, ab4, xs4(t, qf),
                                                ALU.add)
                    Ve(vag)
                    M[("va", t, qf, sg)] = (C3, C3.n)

                def emit_stats(t, sg):
                    if True:
                        asq4 = asq4a if sg % 2 == 0 else asq4b[:, :, :]
                        ab4b = ab4ba if sg % 2 == 0 else ab4bb[:, :, :]

                        def vcp(e, t=t, sg=sg, ab4b=ab4b):
                            if t > 0:
                                C1.wait(e, M[("tsgm", t - 1, sg)])
                            e.tensor_copy(ab4b,
                                          abuf[:, t % 2, 4 * sg:4 * sg + 4, :])
                        Ve(vcp)
                        M[("vcp", t, sg)] = C3.n

                        def ssg(e, t=t, sg=sg, asq4=asq4):
                            cwait(e, M[("va", t, 3, sg)])
                            if sg >= 2:
                                C1.wait(e, M[("tsgm", t, sg - 2)])
                            elif t > 0:
                                C1.wait(e, M[("tsgm", t - 1, sg + 2)])
                            e.activation(asq4,
                                         abuf[:, t % 2, 4 * sg:4 * sg + 4, :],
                                         AF.Square)
                        Se(ssg)

                        def tsgm(e, t=t, sg=sg):
                            C2.wait(e)
                            if sg >= 2:
                                C3.wait(e, M[("vsq", t, sg - 2)])
                            elif t > 0:
                                C3.wait(e, M[("ln1", t - 1)])
                                C3.wait(e, M[("vud", t - 1)])
                                if sg == 0:
                                    C3.wait(e, M[("vr3", t - 1)])
                            dst = pX if sg % 2 == 0 else pY
                            asq4 = asq4a if sg % 2 == 0 else asq4b[:, :, :]
                            ab4b = ab4ba if sg % 2 == 0 else ab4bb[:, :, :]
                            ins = None
                            for i in range(4):
                                ins = e.matmul(dst[0:1, :], onesb[:],
                                               asq4[:, i, :],
                                               start=(i == 0), stop=(i == 3))
                            C3.wait(e, M[("vcp", t, sg)])
                            for i in range(4):
                                ins = e.matmul(pStat[0:1, :], onesb[:],
                                               ab4b[:, i, :],
                                               start=(sg == 0 and i == 0),
                                               stop=(sg == 3 and i == 3))
                            for i in range(4):
                                s = 4 * sg + i
                                ins = e.matmul(pR[:], simn[:, s, :],
                                               abuf[:, t % 2, s, :],
                                               start=(s == 0), stop=False)
                            C1.inc(ins)
                        Te(tsgm)
                        M[("tsgm", t, sg)] = C1.n

                        def vsqg(e, t=t, sg=sg):
                            C1.wait(e, M[("tsgm", t, sg)])
                            src_ = pX if sg % 2 == 0 else pY
                            if sg == 0:
                                e.tensor_copy(rsq, src_[0:1, :])
                            else:
                                e.tensor_tensor(rsq, rsq, src_[0:1, :],
                                                ALU.add)
                        Ve(vsqg)
                        M[("vsq", t, sg)] = C3.n

                def emit_gld(t, qf):
                    def gld(e, t=t, qf=qf):
                        C1.wait(e, M[("tg", t, qf, 0)])
                        P = 4 * t + qf
                        if P >= 1:
                            tp, qp = divmod(P - 1, 4)
                            cwait(e, M[("va", tp, qp, 3)])
                        if P + 2 < 4 * nchunks:
                            t2, q2 = divmod(P + 2, 4)
                            loadou(e, t2, q2)
                            loadxp(e, t2, 2 * q2)
                            loadxp(e, t2, 2 * q2 + 1)
                    Ge(gld)

                def emit_gpq(qf):
                    def gpq(e, qf=qf):
                        C1.wait(e, M[("tg", nchunks - 1, qf, 3)])
                        for j in range(4 * qf, 4 * qf + 4):
                            ln = (SW if j % 2 == 0 else SD).group(e)
                            ins = e.dma_start(
                                arena0[:, j, :],
                                fmap(pallh["pallq"])[:, j:j + 1, :]
                                .rearrange("p o n -> p (o n)"))
                            ln.inc(ins, 16)
                            M.setdefault(("pqm",), []).append((ln, ln.n))
                    Ge(gpq)

                def emit_vr(t):
                    def vr(e, t=t):
                        C1.wait(e, M[("tsgm", t, 3)])
                        e.tensor_scalar(rmu, pStat[0:1, :], 1.0 / D, None,
                                        ALU.mult)
                        e.tensor_tensor(rf, rmu, rmu, ALU.mult)
                        e.tensor_scalar(rvar, rsq, 1.0 / D, None, ALU.mult)
                        e.tensor_tensor(rvar, rvar, rf, ALU.subtract)
                    Ve(vr)
                    M[("vrd", t)] = C3.n

                    def sr(e):
                        C3.wait(e)
                        e.activation(rsinv, rvar, AF.Sqrt, bias=1e-5)
                        e.activation(rf, rvar, AF.Sqrt, scale=float(D))
                    Se(sr)

                    def vr2(e):
                        C2.wait(e)
                        e.reciprocal(rsinv, rsinv)
                        e.tensor_tensor(rf, rf, rsinv, ALU.mult)
                        e.tensor_scalar(rf, rf, 1e-12, None, ALU.max)
                        e.reciprocal(rf, rf)
                        e.tensor_tensor(rf, rsinv, rf, ALU.mult)
                    Ve(vr2)
                    M[("vr2", t)] = C3.n

                def emit_tb(t):
                    def tb(e, t=t):
                        C3.wait(e, M[("vr2", t)])
                        e.matmul(pX[:], o1x128[:], rmu, start=True, stop=True)
                        ins = e.matmul(pY[:], o1x128[:], rsinv,
                                       start=True, stop=True)
                        C1.inc(ins)
                    Te(tb)
                    M[("tb", t)] = C1.n

                def emit_vy(t, g):
                    def vy(e, t=t, g=g):
                        if g == 0:
                            C1.wait(e, M[("tb", t)])
                        ab4 = abuf[:, t % 2, 4 * g:4 * g + 4, :]
                        bcX = pX[:].rearrange("p (o b) -> p o b", o=1) \
                                   .broadcast_to([128, 4, T])
                        bcY = pY[:].rearrange("p (o b) -> p o b", o=1) \
                                   .broadcast_to([128, 4, T])
                        e.tensor_tensor(ab4, ab4, bcX, ALU.subtract)
                        e.tensor_tensor(ab4, ab4, bcY, ALU.mult)
                    Ve(vy)
                    if g == 3:
                        M[("ln1", t)] = C3.n

                def emit_ys(t):
                    def ys(e, t=t):
                        C3.wait(e, M[("ln1", t)])
                        ln = SM.group(e)
                        for grp in range(4):
                            ln.inc(e.dma_start(
                                fmap(y32_s)[:, 4 * grp:4 * grp + 4,
                                            t * T:(t + 1) * T],
                                abuf[:, t % 2, 4 * grp:4 * grp + 4, :]), 16)
                        M[("ystore", t)] = (ln, ln.n)
                    Ye(ys)

                def emit_tr2(t):
                    def tr2(e, t=t):
                        C3.wait(e, M[("ln1", t)])
                        e.matmul(pR[:], negcs[:], rmu, start=False, stop=True)
                        ins = e.matmul(pX[0:16, :], o1x16[:], rf,
                                       start=True, stop=True)
                        C1.inc(ins)
                    Te(tr2)
                    M[("tr2", t)] = C1.n

                def emit_vr3(t):
                    def vr3(e, t=t):
                        C1.wait(e, M[("tr2", t)])
                        e.tensor_copy(r16u[0:16, :], pX[0:16, :])
                        e.tensor_tensor(r16u[32:48, :], pR[:], r16u[0:16, :],
                                        ALU.mult)
                        e.tensor_scalar(r16u[32:48, :], r16u[32:48, :],
                                        sigG[:], None, ALU.subtract)
                    Ve(vr3)
                    M[("vr3", t)] = C3.n

                def emit_tt(t):
                    def tt(e, t=t):
                        C3.wait(e, M[("vr3", t)])
                        ins = None
                        for u in range(4):
                            ins = e.transpose(
                                pX[:, u * 16:(u + 1) * 16],
                                r16u[32:48, u * 128:(u + 1) * 128],
                                idf32[32:48, :])
                        C1.inc(ins)
                    Te(tt)
                    M[("ttm", t)] = C1.n

                def emit_vu(t):
                    def vu(e, t=t):
                        C1.wait(e, M[("ttm", t)])
                        d3 = lambda ap: ap.rearrange("p (a b) -> p a b", a=4)
                        bc = lambda col: col.rearrange("p (a o) -> p a o", o=1) \
                                            .broadcast_to([128, 4, 16])
                        lt = d3(rt16[:, 0, :])
                        e.tensor_copy(lt, d3(pX[:, 0:64]))
                        m1 = rtc[:, 0:4]
                        e.reduce_max(m1.rearrange("p (a o) -> p a o", o=1), lt,
                                     axis=AX.X)
                        ismax = d3(rt16[:, 1, :])
                        e.tensor_tensor(ismax, lt, bc(m1), ALU.is_ge)
                        l2m = d3(rt16[:, 2, :])
                        e.scalar_tensor_tensor(l2m, ismax, -2e9, lt,
                                               ALU.mult, ALU.add)
                        m2 = rtc[:, 4:8]
                        e.reduce_max(m2.rearrange("p (a o) -> p a o", o=1), l2m,
                                     axis=AX.X)
                        fb = d3(rt16[:, 1, :])
                        e.tensor_tensor(fb, lt, bc(m2), ALU.is_ge)
                        inact = rtc[:, 8:12]
                        e.tensor_single_scalar(inact, m1, 0.0, ALU.is_le)
                        mpos = d3(rt16[:, 3, :])
                        e.tensor_single_scalar(mpos, lt, 0.0, ALU.is_gt)
                        dmask = d3(rt16[:, 4, :])
                        e.tensor_tensor(dmask, fb, mpos, ALU.subtract)
                        tmask = d3(rt16[:, 2, :])
                        e.tensor_tensor(tmask, dmask, bc(inact), ALU.mult)
                        mask = d3(rt16[:, 1, :])
                        e.tensor_tensor(mask, tmask, mpos, ALU.add)
                        gated = d3(rt16[:, 2, :])
                        e.tensor_single_scalar(gated, lt, 0.0, ALU.max)
                        t1_ = d3(rt16[:, 3, :])
                        e.tensor_tensor(t1_, gated, mask, ALU.mult)
                        t2_ = d3(rt16[:, 4, :])
                        e.tensor_scalar(t2_, mask, -1.0, 1e9, ALU.add, ALU.mult)
                        marg = d3(rt16[:, 3, :])
                        e.tensor_tensor(marg, t1_, t2_, ALU.add)
                    Ve(vu)
                    M[("vud", t)] = C3.n

                    def su(e):
                        C3.wait(e)
                        e.activation(rt16[:, 0, :].rearrange(
                                         "p (a b) -> p a b", a=4),
                                     rt16[:, 3, :].rearrange(
                                         "p (a b) -> p a b", a=4),
                                     AF.Exp)
                    Se(su)

                def emit_vu2(t):
                    def vu2(e, t=t):
                        C2.wait(e)
                        d3 = lambda ap: ap.rearrange("p (a b) -> p a b", a=4)
                        bc = lambda col: col.rearrange("p (a o) -> p a o", o=1) \
                                            .broadcast_to([128, 4, 16])
                        p_ = d3(rt16[:, 0, :])
                        S_ = rtc[:, 0:4]
                        e.reduce_sum(S_.rearrange("p (a o) -> p a o", o=1), p_,
                                     axis=AX.X)
                        e.reciprocal(S_, S_)
                        t3 = d3(rt16[:, 2, :])
                        e.tensor_tensor(t3, p_, d3(rt16[:, 1, :]), ALU.mult)
                        e.tensor_tensor(
                            wrt[:, 4 * t:4 * t + 4, :], t3, bc(S_), ALU.mult)
                    Ve(vu2)
                    M[("vu2", t)] = C3.n

                for t in range(nchunks):
                    for qf in range(4):
                        for sg in range(4):
                            emit_group(t, qf, sg)
                            if t > 0:
                                p = t - 1
                                if qf == 0:
                                    emit_stats(p, sg)
                                elif qf == 1:
                                    if sg == 0:
                                        emit_vr(p)
                                    elif sg == 1:
                                        emit_tb(p)
                                    elif sg == 2:
                                        emit_vy(p, 0)
                                        emit_vy(p, 1)
                                    else:
                                        emit_vy(p, 2)
                                        emit_vy(p, 3)
                                        emit_ys(p)
                                elif qf == 2:
                                    if sg == 0:
                                        emit_tr2(p)
                                        emit_vr3(p)
                                    elif sg == 1:
                                        emit_tt(p)
                                        emit_vu(p)
                                    elif sg == 2:
                                        emit_vu2(p)
                        emit_gld(t, qf)
                        if t == nchunks - 1:
                            emit_gpq(qf)
                tl = nchunks - 1
                for sg in range(4):
                    emit_stats(tl, sg)
                emit_vr(tl)
                emit_tb(tl)
                for g in range(4):
                    emit_vy(tl, g)
                emit_ys(tl)
                emit_tr2(tl)
                emit_vr3(tl)
                emit_tt(tl)
                emit_vu(tl)
                emit_vu2(tl)

        def phase3a(vtm, qfm, kfm, oarena, aosb):
            T = 256
            nchunks = L // T
            srcarena = [arena0, oarena, arena0]
            with contextlib.ExitStack() as EE:
                ybf = EE.enter_context(
                    nc.sbuf_tensor("ybf", [128, 2, NCH, T], BF16))
                qtm = EE.enter_context(
                    nc.sbuf_tensor("qtm", [128, NT, 128], BF16))
                ktm2 = EE.enter_context(
                    nc.sbuf_tensor("ktm2", [128, NT, 128], BF16))
                wstage = EE.enter_context(
                    nc.sbuf_tensor("wstage", [128, 8, 128], BF16))
                ybf2 = aosb[:].rearrange("p a b -> p (a b)").rearrange(
                    "p (j n) -> p j n", j=NCH)

                def ybv(gc):
                    sl = gc % 3
                    return ybf[:, sl, :, :] if sl < 2 else ybf2

                def loady3(e, gc):
                    tc = gc % 16
                    ln = SD.group(e)
                    ins = e.dma_start(
                        ybv(gc),
                        fmap(y32_s)[:, :, tc * T:(tc + 1) * T])
                    ln.inc(ins, 16)
                    M[("ybg", gc)] = (ln, ln.n)

                accf = rt16[:, 0:2, :].rearrange("p a b -> p (a b)")
                accf2 = rt16[:, 2:4, :].rearrange("p a b -> p (a b)")
                with nc.psum_tensor("pPE", [128, 2, D], F32) as pPE:
                    def gwk(e):
                        M[("pld", 1)] = load_arena(e, oarena,
                                                   pallh["pallk"], SW)
                        SM.wait_all(e)
                        for gc0 in range(3):
                            loady3(e, gc0)
                    Ge(gwk)
                    for pi, (wname, dst) in enumerate(
                            [("pallq", qtm), ("pallk", ktm2), ("pallv", vtm)]):
                        wsrc = srcarena[pi]
                        for t in range(nchunks):
                            gc = 16 * pi + t

                            def gl(e, gc=gc):
                                if gc + 3 < 48:
                                    C1.wait(e, M[("mmg", gc)])
                                    loady3(e, gc + 3)

                            for tau in range(2):
                                g = 2 * t + tau

                                def tm_(e, pi=pi, t=t, tau=tau, g=g, gc=gc,
                                        wsrc=wsrc):
                                    SD.wait_mark(e, M[("ybg", gc)])
                                    if g == 0:
                                        if pi == 0:
                                            for mk in M[("pqm",)]:
                                                SD.wait_mark(e, mk)
                                        else:
                                            SW.wait_mark(e, M[("pld", pi)])
                                    if g >= 2:
                                        C3.wait(e, M[("acc", pi, g - 2)])
                                    elif pi > 0:
                                        C3.wait(e, M[("accend", pi - 1)])
                                    ins = None
                                    for j in range(NCH):
                                        for nn in range(4):
                                            ins = e.matmul(
                                                pPE[:, g % 2,
                                                    nn * 512:(nn + 1) * 512],
                                                ybv(gc)[:, j,
                                                        tau * 128:(tau + 1) * 128],
                                                wsrc[:, j,
                                                     nn * 512:(nn + 1) * 512],
                                                start=(j == 0),
                                                stop=(j == NCH - 1))
                                    C1.inc(ins)
                                Te(tm_)
                                if tau == 1:
                                    M[("mmg", gc)] = C1.n

                                def vm(e, pi=pi, g=g, dst=dst):
                                    C1.wait(e)
                                    for hf, acc_ in ((0, accf), (1, accf2)):
                                        e.tensor_tensor(
                                            wstage[:, :, :],
                                            pPE[:, g % 2,
                                                hf * 1024:(hf + 1) * 1024]
                                            .rearrange("p (a b) -> p a b", a=8),
                                            wrt[:, g, 8 * hf:8 * hf + 8]
                                            .rearrange("p (a o) -> p a o", o=1)
                                            .broadcast_to([128, 8, 128]),
                                            ALU.mult)
                                        e.reduce_sum(
                                            acc_.rearrange("p (b o) -> p b o",
                                                           o=1),
                                            wstage[:, :, :].rearrange(
                                                "p a b -> p b a"),
                                            axis=AX.X)
                                    e.tensor_tensor(accf, accf, accf2, ALU.add)
                                    e.tensor_copy(dst[:, g, :], accf)
                                Ve(vm)
                                M[("acc", pi, g)] = C3.n
                            Ge(gl)
                        M[("mix", pi)] = C1.n
                        M[("accend", pi)] = C3.n

                        if pi == 0:
                            def gv(e):
                                C1.wait(e, M[("mix", 0)])
                                M[("pld", 2)] = load_arena(
                                    e, arena0, pallh["pallv"], SW)
                            Ge(gv)
                        elif pi == 1:
                            def go2(e):
                                C1.wait(e, M[("mix", 1)])
                                ln = SW.group(e)
                                for g4 in range(4):
                                    ins = e.dma_start(
                                        oarena[:, 4 * g4:4 * g4 + 4, :],
                                        oproj_h[:, 4 * g4:4 * g4 + 4, :])
                                    ln.inc(ins, 16)
                                M[("oproj",)] = (ln, ln.n)
                            Ge(go2)

                with nc.psum_tensor("pT3", [128, 512], BF16) as pT3:
                    for src_, dstf in [(qtm, qfm), (ktm2, kfm)]:
                        for gg in range(8):
                            def tt2(e, src_=src_, gg=gg):
                                C3.wait(e)
                                ins = None
                                for u in range(4):
                                    g = 4 * gg + u
                                    ins = e.transpose(
                                        pT3[:, u * 128:(u + 1) * 128],
                                        src_[:, g, :], identb[:])
                                C1.inc(ins)
                            Te(tt2)

                            def vt2(e, dstf=dstf, gg=gg):
                                C1.wait(e)
                                e.tensor_copy(
                                    dstf[:, gg * 512:(gg + 1) * 512], pT3[:])
                            Ve(vt2)

        def phase3b(vtm, qfm, kfm, aosb, oarena):
            def vos(e):
                SW.wait_mark(e, M[("oproj",)])
                for ei in range(E):
                    e.reduce_sum(osumf[:, ei:ei + 1], oarena[:, ei, :],
                                 axis=AX.X)
                e.tensor_copy(osum[:], osumf[:])
            Ve(vos)

            with nc.sbuf_tensor("Ssb", [128, L], F32) as Ssb, \
                 nc.sbuf_tensor("Pb", [128, 2, L], BF16) as Pb, \
                 nc.sbuf_tensor("PTs", [128, 512], BF16) as PTs, \
                 nc.sbuf_tensor("negm", [128, 1], F32) as negm, \
                 nc.psum_tensor("pS", [128, 2, 512], F32) as pS, \
                 nc.psum_tensor("pPT", [128, 512], BF16) as pPT, \
                 nc.psum_tensor("pAO", [128, 128], F32) as pAO:
                b5 = C5.n
                gkb = [0]
                sevd = {}

                def emit_strip_front(i):
                    Lk = (i + 1) * 128
                    nkb = (Lk + 511) // 512
                    gk0 = gkb[0]

                    b3c, b2c = C3.n, C2.n

                    def tS(e, i=i, Lk=Lk, nkb=nkb, gk0=gk0,
                           b3c=b3c, b2c=b2c):
                        for kb in range(nkb):
                            g = gk0 + kb + 1
                            if kb >= 2:
                                kp = kb - 2
                                if kp % 2 == 0:
                                    C3.wait(e, b3c + kp // 2 + 1)
                                else:
                                    C2.wait(e, b2c + (kp - 1) // 2 + 1)
                            elif g > 2:
                                cwait(e, sevd[g - 2])
                            N = min(512, Lk - kb * 512)
                            ins = e.matmul(pS[:, g % 2, 0:N],
                                           qfm[:, i * 128:(i + 1) * 128],
                                           kfm[:, kb * 512:kb * 512 + N],
                                           start=True, stop=True)
                            C5.inc(ins)
                    Te(tS)
                    gkb[0] += nkb

                    def sSv(e, i=i, Lk=Lk, nkb=nkb, gk0=gk0):
                        if i > 0:
                            C2.wait(e, M[("exp", i - 1)])
                        for kb in range(0, nkb, 2):
                            g = gk0 + kb + 1
                            C5.wait(e, b5 + g)
                            N = min(512, Lk - kb * 512)
                            e.tensor_scalar(Ssb[:, kb * 512:kb * 512 + N],
                                            pS[:, g % 2, 0:N],
                                            ISC, None, ALU.mult)
                            sevd[g] = (C3, C3.n)
                    Ve(sSv)

                    def sSs(e, i=i, Lk=Lk, nkb=nkb, gk0=gk0):
                        for kb in range(1, nkb, 2):
                            g = gk0 + kb + 1
                            C5.wait(e, b5 + g)
                            N = min(512, Lk - kb * 512)
                            e.activation(Ssb[:, kb * 512:kb * 512 + N],
                                         pS[:, g % 2, 0:N],
                                         AF.Copy, scale=ISC)
                            sevd[g] = (C2, C2.n)
                    Se(sSs)

                    def vS(e, i=i, Lk=Lk):
                        C2.wait(e)
                        e.tensor_tensor(Ssb[:, i * 128:(i + 1) * 128],
                                        Ssb[:, i * 128:(i + 1) * 128],
                                        causal[:], ALU.add)
                        e.reduce_max(negm[:], Ssb[:, 0:Lk], axis=AX.X,
                                     negate=True)
                    Ve(vS)
                    M[("vS", i)] = C3.n

                    def sE(e, i=i, Lk=Lk):
                        C3.wait(e)
                        if i >= 2:
                            C1.wait(e, M[("tTall", i - 2)])
                        e.activation(Pb[:, i % 2, 0:Lk], Ssb[:, 0:Lk], AF.Exp,
                                     bias=negm[:],
                                     accum_out=Lsum[:, i:i + 1])
                    Se(sE)
                    M[("exp", i)] = C2.n

                def emit_strip_back(i):
                    ngrp = (i + 4) // 4
                    for g in range(ngrp):
                        j0 = 4 * g
                        j1 = min(i, 4 * g + 3)

                        def tT(e, i=i, j0=j0, j1=j1):
                            C2.wait(e, M[("exp", i)])
                            if ("ptev",) in M:
                                C3.wait(e, M[("ptev",)])
                            ins = None
                            for j in range(j0, j1 + 1):
                                ins = e.transpose(
                                    pPT[:, (j - j0) * 128:(j - j0 + 1) * 128],
                                    Pb[:, i % 2, j * 128:(j + 1) * 128],
                                    identb[:])
                            C1.inc(ins)
                        Te(tT)
                        tmark = C1.n

                        def vT(e, j0=j0, j1=j1, tmark=tmark):
                            C1.wait(e, tmark)
                            if ("pvmm",) in M:
                                C1.wait(e, M[("pvmm",)])
                            e.tensor_copy(PTs[:, 0:(j1 - j0 + 1) * 128],
                                          pPT[:, 0:(j1 - j0 + 1) * 128])
                        Ve(vT)
                        M[("ptev",)] = C3.n

                        def tV(e, i=i, j0=j0, j1=j1):
                            C3.wait(e, M[("ptev",)])
                            if j0 == 0 and i > 0:
                                C2.wait(e, M[("aoev", i - 1)])
                            ins = None
                            for j in range(j0, j1 + 1):
                                ins = e.matmul(
                                    pAO[:],
                                    PTs[:, (j - j0) * 128:(j - j0 + 1) * 128],
                                    vtm[:, j, :],
                                    start=(j == 0), stop=(j == i))
                            C1.inc(ins)
                        Te(tV)
                        M[("pvmm",)] = C1.n

                    def sA(e, i=i):
                        C1.wait(e, M[("pvmm",)])
                        e.activation(aosb[:, i, :], pAO[:], AF.Copy)
                    Se(sA)
                    M[("aoev", i)] = C2.n
                    M[("tTall", i)] = C1.n

                for i in range(NT):
                    emit_strip_front(i)
                    if i > 0:
                        emit_strip_back(i - 1)
                emit_strip_back(NT - 1)

                def vw(e):
                    C2.wait(e, M[("aoev", NT - 1)])
                    e.reciprocal(Lsum[:], Lsum[:])
                    for g in range(NT):
                        e.tensor_scalar(wpr[:, g, :], wrt[:, g, :],
                                        Lsum[:, g:g + 1], None, ALU.mult)
                Ve(vw)
                M[("wpr",)] = C3.n

        def phase3c(aosb, oarena, kfm, qfm, vtm):
            T = 256
            nchunks = L // T
            with nc.sbuf_tensor("a2", [128, 2, NCH, T], F32) as a2, \
                 nc.sbuf_tensor("asq2", [128, 4, T], BF16) as asq2, \
                 nc.psum_tensor("pO2", [128, 4, 512], F32) as pO2, \
                 nc.psum_tensor("pS2", [1, 512], F32) as pS2, \
                 nc.psum_tensor("pB1", [128, 512], F32) as pB1, \
                 nc.psum_tensor("pB2", [128, 512], F32) as pB2, \
                 nc.psum_tensor("pWT", [128, 1024], BF16) as pWT:

                wstA = kfm[:].rearrange("p (e t c) -> p e t c", e=E, t=2)
                waoF = qfm[:].rearrange("p (e c) -> p e c", e=E)
                _rows = vtm[:].rearrange("p a b -> p (a b)").bitcast(F32)
                ys = _rows[:, 0:1024].rearrange("p (a b c) -> p a b c",
                                                a=2, b=2)
                qmu = _rows[0:1, 1024:1280]
                qvar = _rows[0:1, 1280:1536]
                qsq = _rows[0:1, 1536:1792]

                def loady(e, t, pg):
                    ln = SD.group(e)
                    ins = e.dma_start(
                        ys[:, pg % 2, :, :],
                        fmap(y32_s)[:, 2 * pg:2 * pg + 2, t * T:(t + 1) * T])
                    ln.inc(ins, 16)
                    M[("ys", t, pg)] = (ln, ln.n)

                def ginit(e):
                    SM.wait_all(e)
                    loady(e, 0, 0)
                    loady(e, 0, 1)
                Ge(ginit)

                def emit_vscA(t):
                    def vscA(e, t=t):
                        if t == 0:
                            C3.wait(e, M[("wpr",)])
                        else:
                            C1.wait(e, M[("o2mm", t - 1)])
                            C1.wait(e, M[("wtrA", t - 1)])
                        e.tensor_tensor(
                            wstA[:, :, :, :],
                            aosb[:, 2 * t:2 * t + 2, :].rearrange(
                                "p (o t) c -> p o t c", o=1)
                            .broadcast_to([128, E, 2, 128]),
                            wpr[:, 2 * t:2 * t + 2, :].rearrange(
                                "p t e -> p e t").rearrange(
                                "p e (t o) -> p e t o", o=1)
                            .broadcast_to([128, E, 2, 128]),
                            ALU.mult)
                    Ve(vscA)
                    M[("wscA", t)] = C3.n

                def emit_wao(t):
                    for g4 in range(4):
                        def ttr(e, t=t, g4=g4):
                            C3.wait(e, M[("wscA", t)])
                            if (t, g4) != (0, 0):
                                C3.wait(e, M[("wevA",)])
                            ins = None
                            for k in range(8):
                                ei, tau = (8 * g4 + k) // 2, (8 * g4 + k) % 2
                                ins = e.transpose(
                                    pWT[:, k * 128:(k + 1) * 128],
                                    wstA[:, ei, tau, :], identb[:])
                            C1.inc(ins)
                        Te(ttr)
                        M[("wtr", t, g4)] = C1.n

                        def vev(e, t=t, g4=g4):
                            C1.wait(e, M[("wtr", t, g4)])
                            if t > 0:
                                C1.wait(e, M[("o2mm", t - 1)])
                            e.tensor_copy(
                                waoF[:, :, :].rearrange(
                                    "p e (t c) -> p e t c", t=2)
                                [:, 4 * g4:4 * g4 + 4, :, :],
                                pWT[:].rearrange(
                                    "p (e t c) -> p e t c", e=4, t=2))
                        Ve(vev)
                        M[("wevA",)] = C3.n
                    M[("wtrA", t)] = C1.n
                    M[("wev", t, E - 1)] = C3.n

                def emit_out2(t):
                    bQ3 = C3.n
                    for q in range(4):
                        def tq(e, t=t, q=q, bQ3=bQ3):
                            C3.wait(e, M[("wev", t, E - 1)])
                            if q > 0:
                                C3.wait(e, bQ3 + q)
                            ins = None
                            for ei in range(E):
                                for s in range(4 * q, 4 * q + 4):
                                    ins = e.matmul(
                                        pO2[:, s % 4, 0:T],
                                        oarena[:, ei, s * 128:(s + 1) * 128],
                                        waoF[:, ei, :],
                                        start=(ei == 0), stop=(ei == E - 1))
                                if q == 0:
                                    ins = e.matmul(
                                        pS2[0:1, 0:T], osum[:, ei:ei + 1],
                                        waoF[:, ei, :],
                                        start=(ei == 0), stop=(ei == E - 1))
                            C1.inc(ins)
                        Te(tq)
                        M[("o2", t, q)] = C1.n

                        def vvQ(e, t=t, q=q):
                            C1.wait(e, M[("o2", t, q)])
                            if q == 0 and t >= 2:
                                SM.wait_mark(e, M[("zstore", t - 2)])
                            SD.wait_mark(e, M[("ys", t, 2 * q)])
                            SD.wait_mark(e, M[("ys", t, 2 * q + 1)])
                            e.tensor_tensor(
                                a2[:, t % 2, 4 * q:4 * q + 4, :],
                                pO2[:, :, 0:T],
                                ys[:, :, :, :].rearrange("p a b c -> p (a b) c"),
                                ALU.add)
                        Ve(vvQ)

                        def ssQ(e, t=t, q=q):
                            e.tensor_tensor(asq2[:, :, :],
                                            a2[:, t % 2, 4 * q:4 * q + 4, :],
                                            a2[:, t % 2, 4 * q:4 * q + 4, :],
                                            ALU.mult)
                        Ve(ssQ)
                        M[("sq3", t, q)] = C3.n

                        def tsQ(e, t=t, q=q):
                            C3.wait(e, M[("sq3", t, q)])
                            if q == 0 and t > 0:
                                C3.wait(e, M[("vz", t - 1)])
                            e.matmul(pB1[0:1, 0:2 * T], onesb[:],
                                     asq2[:, 0:2, :].rearrange(
                                         "p a b -> p (a b)"),
                                     start=True, stop=True)
                            ins = e.matmul(pB2[0:1, 0:2 * T], onesb[:],
                                           asq2[:, 2:4, :].rearrange(
                                               "p a b -> p (a b)"),
                                           start=True, stop=True)
                            C1.inc(ins)
                        Te(tsQ)

                        def vsqQ(e, t=t, q=q):
                            C1.wait(e)
                            if q == 0:
                                e.tensor_copy(qsq[:], pB1[0:1, 0:T])
                            else:
                                e.tensor_tensor(qsq[:], qsq[:],
                                                pB1[0:1, 0:T], ALU.add)
                            e.tensor_tensor(qsq[:], qsq[:], pB1[0:1, T:2 * T],
                                            ALU.add)
                            e.tensor_tensor(qsq[:], qsq[:], pB2[0:1, 0:T],
                                            ALU.add)
                            e.tensor_tensor(qsq[:], qsq[:], pB2[0:1, T:2 * T],
                                            ALU.add)
                        Ve(vsqQ)

                        def gly(e, t=t, q=q):
                            C1.wait(e)
                            if q < 3:
                                loady(e, t, 2 * q + 2)
                                loady(e, t, 2 * q + 3)
                            elif t + 1 < nchunks:
                                loady(e, t + 1, 0)
                                loady(e, t + 1, 1)
                        Ge(gly)
                    M[("o2mm", t)] = C1.n

                def emit_rows(t):
                    def vr(e, t=t):
                        C1.wait(e)
                        if t > 0:
                            C1.wait(e, M[("bc", t - 1)])
                        e.tensor_scalar(qvar[:], qsq[:], 1.0 / D, None,
                                        ALU.mult)
                        e.tensor_scalar(qmu[:], pS2[0:1, 0:T], 1.0 / D, None,
                                        ALU.mult)
                        e.tensor_tensor(qsq[:], qmu[:], qmu[:], ALU.mult)
                        e.tensor_tensor(qvar[:], qvar[:], qsq[:],
                                        ALU.subtract)
                    Ve(vr)

                    def sr(e, t=t):
                        C3.wait(e)
                        e.activation(qvar[:], qvar[:], AF.Sqrt, bias=1e-5)
                    Se(sr)

                    def vr2(e, t=t):
                        C2.wait(e)
                        e.reciprocal(qvar[:], qvar[:])
                    Ve(vr2)
                    M[("rows", t)] = C3.n

                def emit_finish(t):
                    def tb(e, t=t):
                        C3.wait(e, M[("rows", t)])
                        e.matmul(pB1[:, 0:T], o1x128[:], qmu[:], start=True,
                                 stop=True)
                        ins = e.matmul(pB2[:, 0:T], o1x128[:], qvar[:],
                                       start=True, stop=True)
                        C1.inc(ins)
                        M[("bc", t)] = C1.n
                    Te(tb)

                    def vz(e, t=t):
                        C1.wait(e, M[("bc", t)])
                        bc1 = pB1[:, 0:T].rearrange("p (o b) -> p o b", o=1) \
                                         .broadcast_to([128, NCH, T])
                        bc2 = pB2[:, 0:T].rearrange("p (o b) -> p o b", o=1) \
                                         .broadcast_to([128, NCH, T])
                        e.tensor_tensor(a2[:, t % 2, :, :], a2[:, t % 2, :, :],
                                        bc1, ALU.subtract)
                        e.tensor_tensor(a2[:, t % 2, :, :], a2[:, t % 2, :, :],
                                        bc2, ALU.mult)
                    Ve(vz)
                    M[("vz", t)] = C3.n
                    M[("z", t)] = C3.n

                    def yz(e, t=t):
                        C3.wait(e, M[("z", t)])
                        ln = SM.group(e)
                        for grp in range(4):
                            ln.inc(e.dma_start(
                                fmap(z_fm)[:, 4 * grp:4 * grp + 4,
                                           t * T:(t + 1) * T],
                                a2[:, t % 2, 4 * grp:4 * grp + 4, :]), 16)
                        M[("zstore", t)] = (ln, ln.n)
                    Ye(yz)

                emit_vscA(0)
                emit_wao(0)
                for t in range(nchunks):
                    emit_out2(t)
                    if t + 1 < nchunks:
                        emit_vscA(t + 1)
                    emit_rows(t)
                    if t + 1 < nchunks:
                        emit_wao(t + 1)
                    emit_finish(t)

                def yfin(e):
                    SM.wait_all(e)
                    SD.wait_all(e)
                Ye(yfin)

        with contextlib.ExitStack() as B:
            warena1 = B.enter_context(
                nc.sbuf_tensor("warena1", [128, NCH, D], BF16))
            KVb = B.enter_context(nc.sbuf_tensor("KVb", [128, NH, 128], BF16))
            xbf2b = B.enter_context(
                nc.sbuf_tensor("xbf2b", [128, NCH, 512], BF16))
            phase0(warena1)
            barrier()
            phase1(warena1, KVb, xbf2b)
            barrier()
            phase2a(warena1)
            phase2b(warena1, KVb, xbf2b)
            barrier()

        with contextlib.ExitStack() as DD:
            vtm = DD.enter_context(nc.sbuf_tensor("vtm3", [128, NT, 128], BF16))
            aosb = DD.enter_context(nc.sbuf_tensor("aosb", [128, NT, 128], BF16))
            qfm = DD.enter_context(nc.sbuf_tensor("qfm", [128, L], BF16))
            kfm = DD.enter_context(nc.sbuf_tensor("kfm", [128, L], BF16))
            oarena = DD.enter_context(nc.sbuf_tensor("oarena", [128, E, D], BF16))
            phase2c(oarena, kfm, qfm, vtm, aosb)
            barrier()
            phase3a(vtm, qfm, kfm, oarena, aosb)
            barrier()
            phase3b(vtm, qfm, kfm, aosb, oarena)
            barrier()
            phase3c(aosb, oarena, kfm, qfm, vtm)

        for c in chains:
            c.check()
    return nc


_NC_CACHE = {}


def _get_program():
    key = ("prog", DEBUG)
    if key not in _NC_CACHE:
        _NC_CACHE[key] = build_program()
    return _NC_CACHE[key]


def kernel(L0_attn_x, L0_attn_wq, L0_attn_wk, L0_attn_wv, L0_attn_wu, L0_attn_wo,
           L0_ffn_sim_matrix, L0_ffn_gates, L0_ffn_q_proj, L0_ffn_k_proj,
           L0_ffn_v_proj, L0_ffn_o_proj):
    nc_prog = _get_program()
    c = np.ascontiguousarray
    shared = {
        "wqT": c(L0_attn_wq.T), "wkT": c(L0_attn_wk.T), "wvT": c(L0_attn_wv.T),
        "wuT": c(L0_attn_wu.T), "woT": c(L0_attn_wo.T),
        "pallq": c(L0_ffn_q_proj.transpose(1, 0, 2).reshape(D, D)),
        "pallk": c(L0_ffn_k_proj.transpose(1, 0, 2).reshape(D, D)),
        "pallv": c(L0_ffn_v_proj.transpose(1, 0, 2).reshape(D, D)),
        "oprojh": c(L0_ffn_o_proj.transpose(1, 0, 2)),
        "sim": c(L0_ffn_sim_matrix),
        "gates": c(L0_ffn_gates.reshape(E, 1)),
    }
    in_maps = []
    for b in range(8):
        m = dict(shared)
        m["x_fm"] = c(L0_attn_x[b].T)
        in_maps.append(m)
    res = run_bass_kernel_spmd(nc_prog, in_maps, core_ids=list(range(8)))
    out = np.stack([res.results[b]["z_fm"].T for b in range(8)], axis=0)
    return np.ascontiguousarray(out.astype(np.float32))


# version 12
# speedup vs baseline: 1.2085x; 1.0185x over previous
import contextlib
import math

import numpy as np
import ml_dtypes

import concourse.bass as bass
import concourse.mybir as mybir
from concourse.bass_utils import run_bass_kernel_spmd

F32 = mybir.dt.float32
BF16 = mybir.dt.bfloat16
AF = mybir.ActivationFunctionType
ALU = mybir.AluOpType
AX = mybir.AxisListType

L = 4096
D = 2048
NCH = 16
NH = 16
E = 16
NT = 32
ISC = 1.0 / math.sqrt(128.0)
NEG = -1.0e9

DEBUG = False


class Chain:
    def __init__(self, ctx, nc, name):
        self.h = ctx.enter_context(nc.semaphore(name))
        self.name = name
        self.n = 0
        self.maxwait = 0

    def inc(self, ins, k=1):
        ins.then_inc(self.h, k)
        self.n += k
        return self.n

    def wait(self, eng, tgt=None):
        if tgt is None:
            tgt = self.n
        if tgt <= 0:
            return
        self.maxwait = max(self.maxwait, tgt)
        eng.wait_ge(self.h, tgt)

    def check(self):
        assert self.maxwait <= self.n, \
            f"chain {self.name}: wait {self.maxwait} > final {self.n} (hang!)"


class DmaPool:

    def __init__(self, ctx, nc, name, n=6):
        self.lanes = [Chain(ctx, nc, f"{name}{i}") for i in range(n)]
        self.idx = 0

    def group(self, eng):
        ln = self.lanes[self.idx % len(self.lanes)]
        self.idx += 1
        if ln.n > 0:
            eng.wait_ge(ln.h, ln.n)
        return ln

    def wait_mark(self, eng, mark):
        ln, v = mark
        ln.wait(eng, v)

    def wait_all(self, eng):
        for ln in self.lanes:
            ln.wait(eng, ln.n)

    def check(self):
        for ln in self.lanes:
            ln.check()


def fmap(ap):
    return ap.rearrange("(j p) n -> p j n", p=128)


_SS_OPS = frozenset([
    "tensor_tensor", "tensor_scalar", "tensor_copy", "tensor_add",
    "tensor_sub", "tensor_mul", "tensor_max", "tensor_scalar_add",
    "tensor_scalar_sub", "tensor_scalar_mul", "tensor_scalar_max",
    "tensor_scalar_min", "tensor_relu", "tensor_reduce", "reduce_max",
    "reduce_sum", "scalar_tensor_tensor", "tensor_tensor_scan",
    "reciprocal", "memset", "memzero", "select", "activation",
    "tensor_single_scalar",
])


class SelfSync:

    def __init__(self, eng, chain):
        self._e = eng
        self._c = chain

    def __getattr__(self, name):
        attr = getattr(self._e, name)
        if name in _SS_OPS:
            def wrapped(*a, **k):
                self._c.wait(self._e)
                ins = attr(*a, **k)
                self._c.inc(ins)
                return ins
            return wrapped
        return attr


def build_program():
    nc = bass.Bass("TRN2", target_bir_lowering=False,
                   dynamic_dma_scratch_size=16384)

    x_fm = nc.dram_tensor("x_fm", [D, L], F32, kind="ExternalInput").ap()
    wT = {k: nc.dram_tensor(k, [D, D], F32, kind="ExternalInput").ap()
          for k in ["wqT", "wkT", "wvT", "wuT", "woT"]}
    pallh = {k: nc.dram_tensor(k, [D, D], F32, kind="ExternalInput").ap()
             for k in ["pallq", "pallk", "pallv"]}
    oproj_h = nc.dram_tensor("oprojh", [128, E, D], F32, kind="ExternalInput").ap()
    sim_h = nc.dram_tensor("sim", [D, E], F32, kind="ExternalInput").ap()
    gates_h = nc.dram_tensor("gates", [E, 1], F32, kind="ExternalInput").ap()
    z_fm = nc.dram_tensor("z_fm", [D, L], F32, kind="ExternalOutput").ap()

    SCR = "ExternalOutput" if DEBUG else "Internal"
    qbf_s = nc.dram_tensor("qbf_s", [D, L], BF16, kind=SCR).ap()
    oub_s = nc.dram_tensor("oub_s", [D, L], BF16, kind=SCR).ap()
    y32_s = nc.dram_tensor("y32_s", [D, L], F32, kind=SCR).ap()
    ybf_s = nc.dram_tensor("ybf_s", [D, L], BF16, kind=SCR).ap()
    dbg_arena = nc.dram_tensor("dbg_arena", [D, D], BF16, kind=SCR).ap() if DEBUG else None

    identb_h = nc.inline_tensor(np.eye(128, dtype=ml_dtypes.bfloat16), "c_identb").ap()
    _idf = np.zeros((48, 16), dtype=np.float32)
    _idf[0:16] = np.eye(16); _idf[32:48] = np.eye(16)
    idf32_h = nc.inline_tensor(_idf, "c_idf32").ap()
    causal_np = np.where(np.arange(128)[:, None] >= np.arange(128)[None, :],
                         0.0, NEG).astype(ml_dtypes.bfloat16)
    causal_h = nc.inline_tensor(causal_np, "c_causal").ap()
    onesb_h = nc.inline_tensor(np.ones((128, 1), dtype=ml_dtypes.bfloat16), "c_onesb").ap()
    o1x128_h = nc.inline_tensor(np.ones((1, 128), dtype=np.float32), "c_o1x128").ap()
    o1x16_h = nc.inline_tensor(np.ones((1, 16), dtype=np.float32), "c_o1x16").ap()
    selcol_np = np.zeros((128, 16, 16), dtype=ml_dtypes.bfloat16)
    for h in range(16):
        selcol_np[:, h, h] = 1.0
    selcol_h = nc.inline_tensor(selcol_np.reshape(128, 256), "c_selcol").ap()
    selrow_np = np.zeros((16, 8, 128), dtype=ml_dtypes.bfloat16)
    for h in range(8):
        selrow_np[h, h, :] = 1.0
    selrow_h = nc.inline_tensor(selrow_np.reshape(16, 1024), "c_selrow").ap()
    cv_h = nc.inline_tensor(
        np.broadcast_to(np.array([1e-5], np.float32), (128, 1)).copy(), "c_cv").ap()

    with contextlib.ExitStack() as A:
        sbA = lambda n, s, d=F32: A.enter_context(nc.sbuf_tensor(n, s, d))
        identb = sbA("identb", [128, 128], BF16)
        idf32 = sbA("idf32", [48, 16])
        causal = sbA("causal", [128, 128], BF16)
        onesb = sbA("onesb", [128, 1], BF16)
        onesf = sbA("onesf", [128, 1])
        o1x128 = sbA("o1x128", [1, 128])
        o1x16 = sbA("o1x16", [1, 16])
        selcol = sbA("selcol", [128, 16, 16], BF16)
        selrow = A.enter_context(
            nc.sbuf_tensor("selrow", [16, 8, 128], BF16))
        constv = sbA("constv", [128, 1])
        nc.const_aps.register(F32, 1e-5, constv[:, 0:1])
        arena0 = sbA("arena0", [128, NCH, D], BF16)
        wosum = sbA("wosum", [128, NCH], BF16)
        osum = sbA("osum", [128, E], BF16)
        wosumf = sbA("wosumf", [128, NCH])
        osumf = sbA("osumf", [128, E])
        simn = sbA("simn", [128, NCH, E])
        negcs = sbA("negcs", [1, E])
        sigG = sbA("sigG", [E, 1])
        wrt = sbA("wrt", [128, NT, E])
        wpr = sbA("wpr", [128, NT, E])
        Lsum = sbA("Lsum", [128, NT])
        rt16 = sbA("rt16", [128, 5, 64])
        rtc = sbA("rtc", [128, 12])

        SD = DmaPool(A, nc, "SD", 6)
        SW = DmaPool(A, nc, "SW", 2)
        SM = DmaPool(A, nc, "SM", 4)
        C1 = Chain(A, nc, "C1")
        C2 = Chain(A, nc, "C2")
        C3 = Chain(A, nc, "C3")
        C4 = Chain(A, nc, "C4")
        C5 = Chain(A, nc, "C5")
        SBC = Chain(A, nc, "SBC")
        chains = [SD, SW, SM, C1, C2, C3, C4, C5, SBC]

        block = A.enter_context(nc.Block())
        ENGS = ["tensor", "vector", "scalar", "gpsimd", "sync"]
        Te, Ge, Ye = block.tensor, block.gpsimd, block.sync
        Ve = lambda f: block.vector(lambda e: f(SelfSync(e, C3)))
        Se = lambda f: block.scalar(lambda e: f(SelfSync(e, C2)))

        def barrier():
            def pre(e):
                for ch in (C1, C2, C3, C4, C5):
                    ch.wait(e)
                for pool in (SD, SW, SM):
                    pool.wait_all(e)
                SBC.inc(e.nop())
            for en in ENGS:
                getattr(block, en)(pre)
            tgt = SBC.n
            for en in ENGS:
                getattr(block, en)(lambda e, tgt=tgt: e.wait_ge(SBC.h, tgt))

        def load_arena(e, arena, src, pool):
            ln = pool.group(e)
            for g in range(4):
                ins = e.dma_start(arena[:, 4 * g:4 * g + 4, :],
                                  fmap(src)[:, 4 * g:4 * g + 4, :])
                ln.inc(ins, 16)
            return (ln, ln.n)

        M = {}

        def cwait(e, mark):
            ch, v = mark
            ch.wait(e, v)

        def phase0(warena1):
            def y0(e):
                ln = SM.group(e)
                for dst, src in [
                        (identb[:], identb_h[:]), (idf32[:], idf32_h[:]),
                        (causal[:], causal_h[:]), (onesb[:], onesb_h[:]),
                        (o1x128[:], o1x128_h[:]), (o1x16[:], o1x16_h[:]),
                        (selcol[:].rearrange("p a b -> p (a b)"), selcol_h[:]),
                        (selrow[:].rearrange("p a b -> p (a b)"), selrow_h[:]),
                        (constv[:], cv_h[:]), (sigG[:], gates_h[:]),
                        (simn[:, :, :],
                         sim_h.rearrange("(a p) b -> p a b", p=128))]:
                    ln.inc(e.dma_start(dst, src), 16)
                M[("setup",)] = (ln, ln.n)
            Ye(y0)

            def g0(e):
                M[("wk",)] = load_arena(e, arena0, wT["wkT"], SW)
                M[("wv",)] = load_arena(e, warena1, wT["wvT"], SW)
            Ge(g0)

            def v0(e):
                SM.wait_mark(e, M[("setup",)])
                e.tensor_copy(onesf[:], onesb[:])
            Ve(v0)

            def s0(e):
                C3.wait(e)
                e.activation(sigG[:], sigG[:], AF.Exp, scale=-1.0)
            Se(s0)

            def v1(e):
                C2.wait(e)
                e.tensor_scalar(sigG[:], sigG[:], 1.0, None, ALU.add)
                e.reciprocal(sigG[:], sigG[:])
            Ve(v1)

            with nc.psum_tensor("p0a", [1, 16], F32) as p0a, \
                 nc.psum_tensor("p0b", [128, 16], F32) as p0b, \
                 nc.sbuf_tensor("simsq", [128, 16], F32) as simsq:
                for j in range(NCH):
                    def vq(e, j=j):
                        C3.wait(e)
                        if j > 0:
                            C1.wait(e)
                        e.tensor_tensor(simsq[:], simn[:, j, :],
                                        simn[:, j, :], ALU.mult)
                    Ve(vq)

                    def tq(e, j=j):
                        C3.wait(e)
                        ins = e.matmul(p0a[:], onesf[:], simsq[:],
                                       start=(j == 0), stop=(j == NCH - 1))
                        C1.inc(ins)
                    Te(tq)

                def s1(e):
                    C1.wait(e)
                    e.activation(simsq[0:1, 0:16], p0a[:], AF.Sqrt)
                Se(s1)

                def v2(e):
                    C2.wait(e)
                    e.tensor_scalar(simsq[0:1, 0:16], simsq[0:1, 0:16],
                                    1e-12, None, ALU.max)
                    e.reciprocal(simsq[0:1, 0:16], simsq[0:1, 0:16])
                Ve(v2)

                def t1(e):
                    C3.wait(e)
                    ins = e.matmul(p0b[:], o1x128[:], simsq[0:1, 0:16],
                                   start=True, stop=True)
                    C1.inc(ins)
                Te(t1)

                def v3(e):
                    C1.wait(e)
                    e.tensor_copy(simsq[:], p0b[:])
                    for j in range(NCH):
                        e.tensor_tensor(simn[:, j, :], simn[:, j, :],
                                        simsq[:], ALU.mult)
                Ve(v3)

                for j in range(NCH):
                    def tr(e, j=j):
                        C3.wait(e)
                        ins = e.matmul(p0a[:], onesf[:], simn[:, j, :],
                                       start=(j == 0), stop=(j == NCH - 1))
                        if j == NCH - 1:
                            C1.inc(ins)
                    Te(tr)

                def v4(e):
                    C1.wait(e)
                    e.tensor_scalar(negcs[:], p0a[:], -1.0, None, ALU.mult)
                Ve(v4)

        def phase1(warena1, KVb, xbf2b):
            T = 512
            nchunks = L // T
            KVacc = xbf2b[:].rearrange("p a b -> p (a b)").bitcast(F32) \
                [:, 0:D]
            with nc.sbuf_tensor("xbf1", [128, 2, NCH, T], BF16) as xbf, \
                 nc.sbuf_tensor("ktm", [128, D], BF16) as ktm, \
                 nc.sbuf_tensor("vtm", [128, D], BF16) as vtm, \
                 nc.psum_tensor("pKV", [128, D], F32) as pKV, \
                 nc.psum_tensor("pP1", [128, D], F32) as pP1:
                b1, b2, b3, b4 = C1.n, C2.n, C3.n, C4.n
                xld = {}

                def loadx(e, t):
                    ln = SD.group(e)
                    for grp in range(4):
                        ins = e.dma_start(xbf[:, t % 2, 4 * grp:4 * grp + 4, :],
                                          fmap(x_fm)[:, 4 * grp:4 * grp + 4,
                                                     t * T:(t + 1) * T])
                        ln.inc(ins, 16)
                    xld[t] = (ln, ln.n)

                Ge(lambda e: (loadx(e, 0), loadx(e, 1)))

                for t in range(nchunks):
                    if t >= 2:
                        def gl(e, t=t):
                            C1.wait(e, b1 + 16 * (t - 1))
                            loadx(e, t)
                        Ge(gl)
                    for tau in range(4):
                        tile = t * 4 + tau

                        def tf(e, t=t, tile=tile, tau=tau):
                            SD.wait_mark(e, xld[t])
                            if tile == 0:
                                SW.wait_mark(e, M[("wk",)])
                                SW.wait_mark(e, M[("wv",)])
                            for which, wsrc in [(0, arena0), (1, warena1)]:
                                for half in range(2):
                                    g = 4 * tile + 2 * which + half
                                    if g >= 2:
                                        C2.wait(e, b2 + g - 1)
                                    ins = None
                                    for j in range(NCH):
                                        for nn in range(2):
                                            col = half * 1024 + nn * 512
                                            ins = e.matmul(
                                                pP1[:, col:col + 512],
                                                xbf[:, t % 2, j,
                                                    tau * 128:(tau + 1) * 128],
                                                wsrc[:, j, col:col + 512],
                                                start=(j == 0), stop=(j == NCH - 1))
                                    C1.inc(ins)
                            C2.wait(e, b2 + 4 * tile + 4)
                            if tile > 0:
                                C3.wait(e, b3 + tile)
                            ins = None
                            for h in range(NH):
                                ins = e.matmul(
                                    pKV[:, h * 128:(h + 1) * 128],
                                    ktm[:, h * 128:(h + 1) * 128],
                                    vtm[:, h * 128:(h + 1) * 128],
                                    start=True, stop=True)
                            C4.inc(ins)
                        Te(tf)

                        def vkv(e, tile=tile):
                            C4.wait(e, b4 + tile + 1)
                            if tile == 0:
                                e.tensor_copy(KVacc[:], pKV[:])
                            else:
                                e.tensor_tensor(KVacc[:], KVacc[:], pKV[:],
                                                ALU.add)
                        Ve(vkv)

                        def sf(e, tile=tile):
                            for which, dst in [(0, ktm), (1, vtm)]:
                                for half in range(2):
                                    C1.wait(e, b1 + 4 * tile + 2 * which + half + 1)
                                    if which == 0 and half == 0 and tile > 0:
                                        C4.wait(e, b4 + tile)
                                    func = AF.Relu if which == 0 else AF.Copy
                                    sc = ISC if which == 0 else 1.0
                                    e.activation(
                                        dst[:, half * 1024:(half + 1) * 1024],
                                        pP1[:, half * 1024:(half + 1) * 1024],
                                        func, scale=sc)
                        Se(sf)

                def skv(e):
                    C3.wait(e, b3 + NT)
                    e.activation(KVb[:].rearrange("p a b -> p (a b)"),
                                 KVacc[:], AF.Copy)
                Se(skv)

        def phase2a(warena1):
            T = 512
            nchunks = L // T

            def gw(e):
                M[("wq",)] = load_arena(e, arena0, wT["wqT"], SW)
                M[("wu",)] = load_arena(e, warena1, wT["wuT"], SW)
            Ge(gw)

            with nc.sbuf_tensor("xbf2a", [128, 2, NCH, T], BF16) as xbf, \
                 nc.sbuf_tensor("qst", [128, 2, 8, T], BF16) as qst, \
                 nc.psum_tensor("pQ", [128, 4, T], F32) as pQ:
                b1, b2 = C1.n, C2.n
                xld = {}

                def loadx(e, t):
                    ln = SD.group(e)
                    for grp in range(4):
                        ins = e.dma_start(xbf[:, t % 2, 4 * grp:4 * grp + 4, :],
                                          fmap(x_fm)[:, 4 * grp:4 * grp + 4,
                                                     t * T:(t + 1) * T])
                        ln.inc(ins, 16)
                    xld[t] = (ln, ln.n)

                Ge(lambda e: (loadx(e, 0), loadx(e, 1)))

                for t in range(nchunks):
                    if t >= 2:
                        def gl(e, t=t):
                            C1.wait(e, b1 + 16 * (t - 1))
                            loadx(e, t)
                        Ge(gl)

                    def tf(e, t=t):
                        SD.wait_mark(e, xld[t])
                        if t == 0:
                            SW.wait_mark(e, M[("wq",)])
                        for s in range(16):
                            g = 16 * t + s
                            if g >= 4:
                                C2.wait(e, b2 + g - 3)
                            ins = None
                            for j in range(NCH):
                                ins = e.matmul(pQ[:, s % 4, :],
                                               arena0[:, j, s * 128:(s + 1) * 128],
                                               xbf[:, t % 2, j, :],
                                               start=(j == 0), stop=(j == NCH - 1))
                            C1.inc(ins)
                    Te(tf)

                    def sf(e, t=t):
                        for s in range(16):
                            g = 16 * t + s
                            hglob = 2 * t + s // 8
                            if s % 8 == 0 and hglob >= 2:
                                SM.wait_mark(e, M[("qstore", hglob - 2)])
                            C1.wait(e, b1 + g + 1)
                            e.activation(qst[:, hglob % 2, s % 8, :],
                                         pQ[:, s % 4, :],
                                         AF.Relu, scale=ISC)
                    Se(sf)

                    for hh_ in range(2):
                        def yf(e, t=t, hh_=hh_):
                            hglob = 2 * t + hh_
                            C2.wait(e, b2 + 8 * (hglob + 1))
                            ln = SM.group(e)
                            for grp in range(2):
                                ln.inc(e.dma_start(
                                    fmap(qbf_s)[:, 8 * hh_ + 4 * grp:
                                                8 * hh_ + 4 * grp + 4,
                                                t * T:(t + 1) * T],
                                    qst[:, hglob % 2, 4 * grp:4 * grp + 4, :]),
                                    16)
                            M[("qstore", hglob)] = (ln, ln.n)
                        Ye(yf)
                if DEBUG:
                    def ydbg(e):
                        C1.wait(e)
                        ln = SM.group(e)
                        ln.inc(e.dma_start(
                            dbg_arena.rearrange("(j p) e -> p j e", p=128),
                            arena0[:, :, :]), 16)
                    Ye(ydbg)

        def phase2b(warena1, KVb, xbf):
            T = 512
            nchunks = L // T

            with nc.sbuf_tensor("qc", [128, 8, T], BF16) as qc, \
                 nc.sbuf_tensor("uf", [128, 8, T], F32) as uf, \
                 nc.sbuf_tensor("ofp", [128, 2, T], F32) as ofp, \
                 nc.sbuf_tensor("oust", [128, 8, T], BF16) as oust, \
                 nc.sbuf_tensor("osq", [128, 2, T], BF16) as osq, \
                 nc.sbuf_tensor("invh", [8, T], F32) as invh, \
                 nc.sbuf_tensor("invhb", [8, T], BF16) as invhb, \
                 nc.psum_tensor("pU", [128, 2, T], F32) as pU, \
                 nc.psum_tensor("pO", [128, 2, T], F32) as pO, \
                 nc.psum_tensor("pN", [8, T], F32) as pN, \
                 nc.psum_tensor("pB", [128, 2, T], F32) as pB:
                xld, qld = {}, {}

                def loadx(e, t):
                    ln = SD.group(e)
                    for grp in range(4):
                        ins = e.dma_start(xbf[:, 4 * grp:4 * grp + 4, :],
                                          fmap(x_fm)[:, 4 * grp:4 * grp + 4,
                                                     t * T:(t + 1) * T])
                        ln.inc(ins, 16)
                    xld[t] = (ln, ln.n)

                def loadq(e, t, H):
                    ln = SD.group(e)
                    ins = e.dma_start(qc[:, :, :],
                                      fmap(qbf_s)[:, 8 * H:8 * H + 8,
                                                  t * T:(t + 1) * T])
                    ln.inc(ins, 16)
                    qld[(t, H)] = (ln, ln.n)

                def gpre(e):
                    loadx(e, 0)
                Ge(gpre)
                barrier()

                def gw(e):
                    loadq(e, 0, 0)
                    M[("wo",)] = load_arena(e, arena0, wT["woT"], SW)
                Ge(gw)

                for t in range(nchunks):
                    for H in range(2):
                        hh = 2 * t + H
                        bU1, bU2 = C1.n, C2.n

                        def tU(e, t=t, H=H, hh=hh, bU2=bU2):
                            SD.wait_mark(e, xld[t])
                            if hh == 0:
                                SW.wait_mark(e, M[("wu",)])
                            for s8 in range(8):
                                s = 8 * H + s8
                                if s8 >= 2:
                                    C2.wait(e, bU2 + s8 - 1)
                                elif hh > 0:
                                    C3.wait(e, M[("vB", hh - 1)])
                                ins = None
                                for j in range(NCH):
                                    ins = e.matmul(
                                        pU[:, s8 % 2, :],
                                        warena1[:, j, s * 128:(s + 1) * 128],
                                        xbf[:, j, :],
                                        start=(j == 0), stop=(j == NCH - 1))
                                C1.inc(ins)
                        Te(tU)
                        if H == 1:
                            M[("ux", t)] = C1.n

                        def sU(e, bU1=bU1):
                            for s8 in range(8):
                                C1.wait(e, bU1 + s8 + 1)
                                e.activation(uf[:, s8, :], pU[:, s8 % 2, :],
                                             AF.Silu)
                        Se(sU)

                        bO1, bO2, bO5, bO3 = C1.n, C2.n, C5.n, C3.n

                        def tO(e, t=t, H=H, hh=hh, bO2=bO2):
                            SD.wait_mark(e, qld[(t, H)])
                            for h8 in range(8):
                                if h8 >= 2:
                                    C2.wait(e, bO2 + 2 * h8 - 2)
                                elif hh > 0:
                                    C2.wait(e, M[("ocp", hh - 1)])
                                ins = e.matmul(pO[:, h8 % 2, :],
                                               KVb[:, 8 * H + h8, :],
                                               qc[:, h8, :],
                                               start=True, stop=True)
                                C1.inc(ins)
                                if h8 >= 1:
                                    hprev = h8 - 1
                                    if hprev == 0 and hh > 0:
                                        C2.wait(e, M[("inv", hh - 1)])
                                    C2.wait(e, bO2 + 2 * hprev + 1)
                                    h = 8 * H + hprev
                                    ins = e.matmul(
                                        pN[:], selcol[:, h, 8 * H:8 * H + 8],
                                        osq[:, hprev % 2, :],
                                        start=(hprev == 0), stop=False)
                                    C5.inc(ins)
                            C2.wait(e, bO2 + 15)
                            ins = e.matmul(pN[:], selcol[:, 8 * H + 7,
                                                         8 * H:8 * H + 8],
                                           osq[:, 1, :], start=False, stop=True)
                            C5.inc(ins)
                        Te(tO)
                        M[("oq", t, H)] = C1.n

                        def sO(e, hh=hh, bO1=bO1, bO5=bO5, bO3=bO3):
                            for h8 in range(8):
                                C1.wait(e, bO1 + h8 + 1)
                                if h8 >= 2:
                                    C5.wait(e, bO5 + h8 - 1)
                                    C3.wait(e, bO3 + h8 - 1)
                                elif hh > 0:
                                    C5.wait(e, bO5)
                                    C3.wait(e, bO3)
                                e.activation(osq[:, h8 % 2, :], pO[:, h8 % 2, :],
                                             AF.Square)
                                e.activation(ofp[:, h8 % 2, :],
                                             pO[:, h8 % 2, :], AF.Copy)
                        Se(sO)
                        M[("ocp", hh)] = C2.n

                        def vOU(e, bO2=bO2):
                            for h8 in range(8):
                                C2.wait(e, bO2 + 2 * h8 + 2)
                                e.tensor_tensor(uf[:, h8, :],
                                                ofp[:, h8 % 2, :],
                                                uf[:, h8, :], ALU.mult)
                        Ve(vOU)

                        def sI(e, hh=hh, bO5=bO5):
                            C5.wait(e, bO5 + 8)
                            if hh > 0:
                                C1.wait(e, M[("invmm", hh - 1)])
                            e.activation(invh[:], pN[:], AF.Sqrt,
                                         scale=1.0 / 128.0)
                        Se(sI)
                        M[("inv", hh)] = C2.n

                        def vI(e, hh=hh):
                            C2.wait(e, M[("inv", hh)])
                            e.tensor_scalar(invh[:], invh[:], 1e-12, None, ALU.max)
                            e.reciprocal(invh[:], invh[:])
                            e.tensor_copy(invhb[:], invh[:])
                        Ve(vI)
                        vimark = C3.n

                        bB1, bB3 = C1.n, C3.n

                        def tB(e, vimark=vimark, bB3=bB3):
                            C3.wait(e, vimark)
                            for h8 in range(8):
                                if h8 >= 2:
                                    C3.wait(e, bB3 + h8 - 1)
                                ins = e.matmul(pB[:, h8 % 2, :],
                                               selrow[0:8, h8, :], invhb[:],
                                               start=True, stop=True)
                                C1.inc(ins)
                        Te(tB)
                        M[("invmm", hh)] = C1.n

                        def vB(e, hh=hh, bB1=bB1):
                            for h8 in range(8):
                                C1.wait(e, bB1 + h8 + 1)
                                if h8 == 0 and hh >= 1:
                                    SM.wait_mark(e, M[("oust", hh - 1)])
                                e.tensor_tensor(oust[:, h8, :],
                                                uf[:, h8, :],
                                                pB[:, h8 % 2, :], ALU.mult)
                        Ve(vB)
                        M[("vB", hh)] = C3.n

                        def yS(e, t=t, H=H, hh=hh):
                            C3.wait(e, M[("vB", hh)])
                            ln = SM.group(e)
                            ins = e.dma_start(
                                fmap(oub_s)[:, 8 * H:8 * H + 8, t * T:(t + 1) * T],
                                oust[:, :, :])
                            ln.inc(ins, 16)
                            M[("oust", hh)] = (ln, ln.n)
                        Ye(yS)

                        def gq(e, t=t, H=H):
                            C1.wait(e, M[("oq", t, H)])
                            if H == 0:
                                loadq(e, t, 1)
                            else:
                                if t + 1 < nchunks:
                                    C1.wait(e, M[("ux", t)])
                                    loadx(e, t + 1)
                                    loadq(e, t + 1, 0)
                        Ge(gq)

        def phase2c(oarena, kfm, qfm, vtm, aosb):
            T = 512
            nchunks = L // T

            abuf = oarena[:].rearrange("p a b -> p (a b)").bitcast(F32) \
                .rearrange("p (u j n) -> p u j n", u=2, j=NCH)
            kfv = kfm[:].bitcast(F32)
            r16u = kfv[0:48, 0:512]
            rmu = kfv[0:1, 512:1024]
            rvar = kfv[0:1, 1024:1536]
            rsinv = kfv[0:1, 1536:2048]
            vtv = vtm[:].rearrange("p a b -> p (a b)").bitcast(F32)
            rf = vtv[0:1, 0:512]
            rsq = vtv[0:1, 512:1024]
            ab4ba = vtm[:].rearrange("p a b -> p (a b)")[:, 2048:4096] \
                .rearrange("p (a b) -> p a b", a=4)
            asq4a = qfm[:, 0:2048].rearrange("p (a b) -> p a b", a=4)
            _ou01 = aosb[:].rearrange("p a b -> p (a b)") \
                .rearrange("p (s j n) -> p s j n", s=2, j=4)
            _ou2 = qfm[:, 2048:4096].rearrange("p (j n) -> p j n", j=4)

            def ouv(slot):
                return _ou01[:, slot, :, :] if slot < 2 else _ou2

            with nc.sbuf_tensor("xs", [128, 6, 2, T], F32) as xs, \
                 nc.sbuf_tensor("asq4b", [128, 4, T], BF16) as asq4b, \
                 nc.sbuf_tensor("ab4bb", [128, 4, T], BF16) as ab4bb, \
                 nc.psum_tensor("pW", [128, 4, T], F32) as pW, \
                 nc.psum_tensor("pStat", [1, T], F32) as pStat, \
                 nc.psum_tensor("pR", [16, T], F32) as pR, \
                 nc.psum_tensor("pX", [128, T], F32) as pX, \
                 nc.psum_tensor("pY", [128, T], F32) as pY:

                def loadou(e, t, qf):
                    slot = (4 * t + qf) % 3
                    for h in range(2):
                        ln = SD.group(e)
                        ins = e.dma_start(
                            ouv(slot)[:, 2 * h:2 * h + 2, :],
                            fmap(oub_s)[:, 4 * qf + 2 * h:4 * qf + 2 * h + 2,
                                        t * T:(t + 1) * T])
                        ln.inc(ins, 16)
                        M[("ou", t, qf, h)] = (ln, ln.n)

                def loadxp(e, t, pg):
                    slot = (8 * t + pg) % 6
                    for r in range(2):
                        ln = SD.group(e)
                        ins = e.dma_start(
                            xs[:, slot, r, :],
                            fmap(x_fm)[:, 2 * pg + r:2 * pg + r + 1,
                                       t * T:(t + 1) * T]
                            .rearrange("p o n -> p (o n)"))
                        ln.inc(ins, 16)
                        M[("xp", t, pg, r)] = (ln, ln.n)

                def ginit(e):
                    SM.wait_all(e)
                    loadou(e, 0, 0)
                    loadou(e, 0, 1)
                    loadou(e, 0, 2)
                    for pg in range(4):
                        loadxp(e, 0, pg)
                Ge(ginit)

                def xs4(t, qf):
                    s0 = (8 * t + 2 * qf) % 6
                    return xs[:, s0:s0 + 2, :, :].rearrange(
                        "p a b n -> p (a b) n")

                def emit_group(t, qf, sg):
                    slot = (4 * t + qf) % 3

                    def tqg(e, t=t, qf=qf, sg=sg, slot=slot):
                        if (t, qf, sg) == (0, 0, 0):
                            SW.wait_mark(e, M[("wo",)])
                        elif sg > 0:
                            cwait(e, M[("va", t, qf, sg - 1)])
                        elif qf > 0:
                            cwait(e, M[("va", t, qf - 1, 3)])
                        else:
                            cwait(e, M[("va", t - 1, 3, 3)])
                        if sg == 0:
                            SD.wait_mark(e, M[("ou", t, qf, 0)])
                            SD.wait_mark(e, M[("ou", t, qf, 1)])
                        ins = None
                        for i in range(4):
                            s = 4 * sg + i
                            for j in range(4 * qf, 4 * qf + 4):
                                ins = e.matmul(
                                    pW[:, i, :],
                                    arena0[:, j, s * 128:(s + 1) * 128],
                                    ouv(slot)[:, j - 4 * qf, :],
                                    start=(j == 4 * qf),
                                    stop=(j == 4 * qf + 3))
                        C1.inc(ins)
                    Te(tqg)
                    M[("tg", t, qf, sg)] = C1.n

                    ab4 = abuf[:, t % 2, 4 * sg:4 * sg + 4, :]

                    def vag(e, t=t, qf=qf, sg=sg, ab4=ab4):
                        C1.wait(e, M[("tg", t, qf, sg)])
                        if qf == 0 and t >= 2:
                            SM.wait_mark(e, M[("ystore", t - 2)])
                            if sg == 0:
                                C1.wait(e, M[("tsgm", t - 2, 3)])
                        if sg == qf:
                            for pg in (2 * qf, 2 * qf + 1):
                                SD.wait_mark(e, M[("xp", t, pg, 0)])
                                SD.wait_mark(e, M[("xp", t, pg, 1)])
                        pW4 = pW[:, :, :]
                        if qf == 0:
                            if sg == 0:
                                e.tensor_tensor(ab4, pW4, xs4(t, 0), ALU.add)
                            else:
                                e.tensor_copy(ab4, pW4)
                        else:
                            e.tensor_tensor(ab4, ab4, pW4, ALU.add)
                            if sg == qf:
                                e.tensor_tensor(ab4, ab4, xs4(t, qf),
                                                ALU.add)
                    Ve(vag)
                    M[("va", t, qf, sg)] = (C3, C3.n)

                def emit_stats(t, sg):
                    if True:
                        asq4 = asq4a if sg % 2 == 0 else asq4b[:, :, :]
                        ab4b = ab4ba if sg % 2 == 0 else ab4bb[:, :, :]

                        def ssg(e, t=t, sg=sg, asq4=asq4, ab4b=ab4b):
                            cwait(e, M[("va", t, 3, sg)])
                            if sg >= 2:
                                C1.wait(e, M[("tsgm", t, sg - 2)])
                            elif t > 0:
                                C1.wait(e, M[("tsgm", t - 1, sg + 2)])
                            e.activation(asq4,
                                         abuf[:, t % 2, 4 * sg:4 * sg + 4, :],
                                         AF.Square)
                            e.activation(ab4b,
                                         abuf[:, t % 2, 4 * sg:4 * sg + 4, :],
                                         AF.Copy)
                        Se(ssg)

                        def tsgm(e, t=t, sg=sg):
                            C2.wait(e)
                            if sg >= 2:
                                C3.wait(e, M[("vsq", t, sg - 2)])
                            elif t > 0:
                                C3.wait(e, M[("ln1", t - 1)])
                                C3.wait(e, M[("vud", t - 1)])
                                if sg == 0:
                                    C3.wait(e, M[("vr3", t - 1)])
                            dst = pX if sg % 2 == 0 else pY
                            asq4 = asq4a if sg % 2 == 0 else asq4b[:, :, :]
                            ab4b = ab4ba if sg % 2 == 0 else ab4bb[:, :, :]
                            ins = None
                            for i in range(4):
                                ins = e.matmul(dst[0:1, :], onesb[:],
                                               asq4[:, i, :],
                                               start=(i == 0), stop=(i == 3))
                            for i in range(4):
                                ins = e.matmul(pStat[0:1, :], onesb[:],
                                               ab4b[:, i, :],
                                               start=(sg == 0 and i == 0),
                                               stop=(sg == 3 and i == 3))
                            for i in range(4):
                                s = 4 * sg + i
                                ins = e.matmul(pR[:], simn[:, s, :],
                                               abuf[:, t % 2, s, :],
                                               start=(s == 0), stop=False)
                            C1.inc(ins)
                        Te(tsgm)
                        M[("tsgm", t, sg)] = C1.n

                        def vsqg(e, t=t, sg=sg):
                            C1.wait(e, M[("tsgm", t, sg)])
                            src_ = pX if sg % 2 == 0 else pY
                            if sg == 0:
                                e.tensor_copy(rsq, src_[0:1, :])
                            else:
                                e.tensor_tensor(rsq, rsq, src_[0:1, :],
                                                ALU.add)
                        Ve(vsqg)
                        M[("vsq", t, sg)] = C3.n

                def emit_gld(t, qf):
                    def gld(e, t=t, qf=qf):
                        C1.wait(e, M[("tg", t, qf, 0)])
                        P = 4 * t + qf
                        if P >= 1:
                            tp, qp = divmod(P - 1, 4)
                            cwait(e, M[("va", tp, qp, 3)])
                        if P + 2 < 4 * nchunks:
                            t2, q2 = divmod(P + 2, 4)
                            loadou(e, t2, q2)
                            loadxp(e, t2, 2 * q2)
                            loadxp(e, t2, 2 * q2 + 1)
                    Ge(gld)

                def emit_gpq(qf):
                    def gpq(e, qf=qf):
                        C1.wait(e, M[("tg", nchunks - 1, qf, 3)])
                        for j in range(4 * qf, 4 * qf + 4):
                            ln = (SW if j % 2 == 0 else SD).group(e)
                            ins = e.dma_start(
                                arena0[:, j, :],
                                fmap(pallh["pallq"])[:, j:j + 1, :]
                                .rearrange("p o n -> p (o n)"))
                            ln.inc(ins, 16)
                            M.setdefault(("pqm",), []).append((ln, ln.n))
                    Ge(gpq)

                def emit_vr(t):
                    def vr(e, t=t):
                        C1.wait(e, M[("tsgm", t, 3)])
                        e.tensor_scalar(rmu, pStat[0:1, :], 1.0 / D, None,
                                        ALU.mult)
                        e.tensor_tensor(rf, rmu, rmu, ALU.mult)
                        e.tensor_scalar(rvar, rsq, 1.0 / D, None, ALU.mult)
                        e.tensor_tensor(rvar, rvar, rf, ALU.subtract)
                    Ve(vr)
                    M[("vrd", t)] = C3.n

                    def sr(e):
                        C3.wait(e)
                        e.activation(rsinv, rvar, AF.Sqrt, bias=1e-5)
                        e.activation(rf, rvar, AF.Sqrt, scale=float(D))
                    Se(sr)

                    def vr2(e):
                        C2.wait(e)
                        e.reciprocal(rsinv, rsinv)
                        e.tensor_tensor(rf, rf, rsinv, ALU.mult)
                        e.tensor_scalar(rf, rf, 1e-12, None, ALU.max)
                        e.reciprocal(rf, rf)
                        e.tensor_tensor(rf, rsinv, rf, ALU.mult)
                    Ve(vr2)
                    M[("vr2", t)] = C3.n

                def emit_tb(t):
                    def tb(e, t=t):
                        C3.wait(e, M[("vr2", t)])
                        e.matmul(pX[:], o1x128[:], rmu, start=True, stop=True)
                        ins = e.matmul(pY[:], o1x128[:], rsinv,
                                       start=True, stop=True)
                        C1.inc(ins)
                    Te(tb)
                    M[("tb", t)] = C1.n

                def emit_vy(t, g):
                    def vy(e, t=t, g=g):
                        if g == 0:
                            C1.wait(e, M[("tb", t)])
                        ab4 = abuf[:, t % 2, 4 * g:4 * g + 4, :]
                        bcX = pX[:].rearrange("p (o b) -> p o b", o=1) \
                                   .broadcast_to([128, 4, T])
                        bcY = pY[:].rearrange("p (o b) -> p o b", o=1) \
                                   .broadcast_to([128, 4, T])
                        e.tensor_tensor(ab4, ab4, bcX, ALU.subtract)
                        e.tensor_tensor(ab4, ab4, bcY, ALU.mult)
                    Ve(vy)
                    if g == 3:
                        M[("ln1", t)] = C3.n

                def emit_ys(t):
                    def ys(e, t=t):
                        C3.wait(e, M[("ln1", t)])
                        ln = SM.group(e)
                        for grp in range(4):
                            ln.inc(e.dma_start(
                                fmap(y32_s)[:, 4 * grp:4 * grp + 4,
                                            t * T:(t + 1) * T],
                                abuf[:, t % 2, 4 * grp:4 * grp + 4, :]), 16)
                        M[("ystore", t)] = (ln, ln.n)
                    Ye(ys)

                def emit_tr2(t):
                    def tr2(e, t=t):
                        C3.wait(e, M[("ln1", t)])
                        e.matmul(pR[:], negcs[:], rmu, start=False, stop=True)
                        ins = e.matmul(pX[0:16, :], o1x16[:], rf,
                                       start=True, stop=True)
                        C1.inc(ins)
                    Te(tr2)
                    M[("tr2", t)] = C1.n

                def emit_vr3(t):
                    def vr3(e, t=t):
                        C1.wait(e, M[("tr2", t)])
                        e.tensor_copy(r16u[0:16, :], pX[0:16, :])
                        e.tensor_tensor(r16u[32:48, :], pR[:], r16u[0:16, :],
                                        ALU.mult)
                        e.tensor_scalar(r16u[32:48, :], r16u[32:48, :],
                                        sigG[:], None, ALU.subtract)
                    Ve(vr3)
                    M[("vr3", t)] = C3.n

                def emit_tt(t):
                    def tt(e, t=t):
                        C3.wait(e, M[("vr3", t)])
                        ins = None
                        for u in range(4):
                            ins = e.transpose(
                                pX[:, u * 16:(u + 1) * 16],
                                r16u[32:48, u * 128:(u + 1) * 128],
                                idf32[32:48, :])
                        C1.inc(ins)
                    Te(tt)
                    M[("ttm", t)] = C1.n

                def emit_vu(t):
                    def vu(e, t=t):
                        C1.wait(e, M[("ttm", t)])
                        d3 = lambda ap: ap.rearrange("p (a b) -> p a b", a=4)
                        bc = lambda col: col.rearrange("p (a o) -> p a o", o=1) \
                                            .broadcast_to([128, 4, 16])
                        lt = d3(rt16[:, 0, :])
                        e.tensor_copy(lt, d3(pX[:, 0:64]))
                        m1 = rtc[:, 0:4]
                        e.reduce_max(m1.rearrange("p (a o) -> p a o", o=1), lt,
                                     axis=AX.X)
                        ismax = d3(rt16[:, 1, :])
                        e.tensor_tensor(ismax, lt, bc(m1), ALU.is_ge)
                        l2m = d3(rt16[:, 2, :])
                        e.scalar_tensor_tensor(l2m, ismax, -2e9, lt,
                                               ALU.mult, ALU.add)
                        m2 = rtc[:, 4:8]
                        e.reduce_max(m2.rearrange("p (a o) -> p a o", o=1), l2m,
                                     axis=AX.X)
                        fb = d3(rt16[:, 1, :])
                        e.tensor_tensor(fb, lt, bc(m2), ALU.is_ge)
                        inact = rtc[:, 8:12]
                        e.tensor_single_scalar(inact, m1, 0.0, ALU.is_le)
                        mpos = d3(rt16[:, 3, :])
                        e.tensor_single_scalar(mpos, lt, 0.0, ALU.is_gt)
                        dmask = d3(rt16[:, 4, :])
                        e.tensor_tensor(dmask, fb, mpos, ALU.subtract)
                        tmask = d3(rt16[:, 2, :])
                        e.tensor_tensor(tmask, dmask, bc(inact), ALU.mult)
                        mask = d3(rt16[:, 1, :])
                        e.tensor_tensor(mask, tmask, mpos, ALU.add)
                        gated = d3(rt16[:, 2, :])
                        e.tensor_single_scalar(gated, lt, 0.0, ALU.max)
                        t1_ = d3(rt16[:, 3, :])
                        e.tensor_tensor(t1_, gated, mask, ALU.mult)
                        t2_ = d3(rt16[:, 4, :])
                        e.tensor_scalar(t2_, mask, -1.0, 1e9, ALU.add, ALU.mult)
                        marg = d3(rt16[:, 3, :])
                        e.tensor_tensor(marg, t1_, t2_, ALU.add)
                    Ve(vu)
                    M[("vud", t)] = C3.n

                    def su(e):
                        C3.wait(e)
                        e.activation(rt16[:, 0, :].rearrange(
                                         "p (a b) -> p a b", a=4),
                                     rt16[:, 3, :].rearrange(
                                         "p (a b) -> p a b", a=4),
                                     AF.Exp)
                    Se(su)

                def emit_vu2(t):
                    def vu2(e, t=t):
                        C2.wait(e)
                        d3 = lambda ap: ap.rearrange("p (a b) -> p a b", a=4)
                        bc = lambda col: col.rearrange("p (a o) -> p a o", o=1) \
                                            .broadcast_to([128, 4, 16])
                        p_ = d3(rt16[:, 0, :])
                        S_ = rtc[:, 0:4]
                        e.reduce_sum(S_.rearrange("p (a o) -> p a o", o=1), p_,
                                     axis=AX.X)
                        e.reciprocal(S_, S_)
                        t3 = d3(rt16[:, 2, :])
                        e.tensor_tensor(t3, p_, d3(rt16[:, 1, :]), ALU.mult)
                        e.tensor_tensor(
                            wrt[:, 4 * t:4 * t + 4, :], t3, bc(S_), ALU.mult)
                    Ve(vu2)
                    M[("vu2", t)] = C3.n

                for t in range(nchunks):
                    for qf in range(4):
                        for sg in range(4):
                            emit_group(t, qf, sg)
                            if t > 0:
                                p = t - 1
                                if qf == 0:
                                    emit_stats(p, sg)
                                elif qf == 1:
                                    if sg == 0:
                                        emit_vr(p)
                                    elif sg == 1:
                                        emit_tb(p)
                                    elif sg == 2:
                                        emit_vy(p, 0)
                                        emit_vy(p, 1)
                                    else:
                                        emit_vy(p, 2)
                                        emit_vy(p, 3)
                                        emit_ys(p)
                                elif qf == 2:
                                    if sg == 0:
                                        emit_tr2(p)
                                        emit_vr3(p)
                                    elif sg == 1:
                                        emit_tt(p)
                                        emit_vu(p)
                                    elif sg == 2:
                                        emit_vu2(p)
                        emit_gld(t, qf)
                        if t == nchunks - 1:
                            emit_gpq(qf)
                tl = nchunks - 1
                for sg in range(4):
                    emit_stats(tl, sg)
                emit_vr(tl)
                emit_tb(tl)
                for g in range(4):
                    emit_vy(tl, g)
                emit_ys(tl)
                emit_tr2(tl)
                emit_vr3(tl)
                emit_tt(tl)
                emit_vu(tl)
                emit_vu2(tl)

        def phase3a(vtm, qfm, kfm, oarena, aosb):
            T = 256
            nchunks = L // T
            srcarena = [arena0, oarena, arena0]
            with contextlib.ExitStack() as EE:
                ybf = EE.enter_context(
                    nc.sbuf_tensor("ybf", [128, 2, NCH, T], BF16))
                qtm = EE.enter_context(
                    nc.sbuf_tensor("qtm", [128, NT, 128], BF16))
                ktm2 = EE.enter_context(
                    nc.sbuf_tensor("ktm2", [128, NT, 128], BF16))
                wstage = EE.enter_context(
                    nc.sbuf_tensor("wstage", [128, 8, 128], BF16))
                ybf2 = aosb[:].rearrange("p a b -> p (a b)").rearrange(
                    "p (j n) -> p j n", j=NCH)

                def ybv(gc):
                    sl = gc % 3
                    return ybf[:, sl, :, :] if sl < 2 else ybf2

                def loady3(e, gc):
                    tc = gc % 16
                    ln = SD.group(e)
                    ins = e.dma_start(
                        ybv(gc),
                        fmap(y32_s)[:, :, tc * T:(tc + 1) * T])
                    ln.inc(ins, 16)
                    M[("ybg", gc)] = (ln, ln.n)

                accf = rt16[:, 0:2, :].rearrange("p a b -> p (a b)")
                accf2 = rt16[:, 2:4, :].rearrange("p a b -> p (a b)")
                with nc.psum_tensor("pPE", [128, 2, D], F32) as pPE:
                    def gwk(e):
                        M[("pld", 1)] = load_arena(e, oarena,
                                                   pallh["pallk"], SW)
                        SM.wait_all(e)
                        for gc0 in range(3):
                            loady3(e, gc0)
                    Ge(gwk)
                    for pi, (wname, dst) in enumerate(
                            [("pallq", qtm), ("pallk", ktm2), ("pallv", vtm)]):
                        wsrc = srcarena[pi]
                        for t in range(nchunks):
                            gc = 16 * pi + t

                            def gl(e, gc=gc):
                                if gc + 3 < 48:
                                    C1.wait(e, M[("mmg", gc)])
                                    loady3(e, gc + 3)

                            for tau in range(2):
                                g = 2 * t + tau

                                def tm_(e, pi=pi, t=t, tau=tau, g=g, gc=gc,
                                        wsrc=wsrc):
                                    SD.wait_mark(e, M[("ybg", gc)])
                                    if g == 0:
                                        if pi == 0:
                                            for mk in M[("pqm",)]:
                                                SD.wait_mark(e, mk)
                                        else:
                                            SW.wait_mark(e, M[("pld", pi)])
                                    if g >= 2:
                                        C3.wait(e, M[("acc", pi, g - 2)])
                                    elif pi > 0:
                                        C3.wait(e, M[("accend", pi - 1)])
                                    ins = None
                                    for j in range(NCH):
                                        for nn in range(4):
                                            ins = e.matmul(
                                                pPE[:, g % 2,
                                                    nn * 512:(nn + 1) * 512],
                                                ybv(gc)[:, j,
                                                        tau * 128:(tau + 1) * 128],
                                                wsrc[:, j,
                                                     nn * 512:(nn + 1) * 512],
                                                start=(j == 0),
                                                stop=(j == NCH - 1))
                                    C1.inc(ins)
                                Te(tm_)
                                if tau == 1:
                                    M[("mmg", gc)] = C1.n

                                def vm(e, pi=pi, g=g, dst=dst):
                                    C1.wait(e)
                                    for hf, acc_ in ((0, accf), (1, accf2)):
                                        e.tensor_tensor(
                                            wstage[:, :, :],
                                            pPE[:, g % 2,
                                                hf * 1024:(hf + 1) * 1024]
                                            .rearrange("p (a b) -> p a b", a=8),
                                            wrt[:, g, 8 * hf:8 * hf + 8]
                                            .rearrange("p (a o) -> p a o", o=1)
                                            .broadcast_to([128, 8, 128]),
                                            ALU.mult)
                                        e.reduce_sum(
                                            acc_.rearrange("p (b o) -> p b o",
                                                           o=1),
                                            wstage[:, :, :].rearrange(
                                                "p a b -> p b a"),
                                            axis=AX.X)
                                    e.tensor_tensor(accf, accf, accf2, ALU.add)
                                    e.tensor_copy(dst[:, g, :], accf)
                                Ve(vm)
                                M[("acc", pi, g)] = C3.n
                            Ge(gl)
                        M[("mix", pi)] = C1.n
                        M[("accend", pi)] = C3.n

                        if pi == 0:
                            def gv(e):
                                C1.wait(e, M[("mix", 0)])
                                M[("pld", 2)] = load_arena(
                                    e, arena0, pallh["pallv"], SW)
                            Ge(gv)
                        elif pi == 1:
                            def go2(e):
                                C1.wait(e, M[("mix", 1)])
                                ln = SW.group(e)
                                for g4 in range(4):
                                    ins = e.dma_start(
                                        oarena[:, 4 * g4:4 * g4 + 4, :],
                                        oproj_h[:, 4 * g4:4 * g4 + 4, :])
                                    ln.inc(ins, 16)
                                M[("oproj",)] = (ln, ln.n)
                            Ge(go2)

                with nc.psum_tensor("pT3", [128, 512], BF16) as pT3:
                    for src_, dstf in [(qtm, qfm), (ktm2, kfm)]:
                        for gg in range(8):
                            def tt2(e, src_=src_, gg=gg):
                                C3.wait(e)
                                ins = None
                                for u in range(4):
                                    g = 4 * gg + u
                                    ins = e.transpose(
                                        pT3[:, u * 128:(u + 1) * 128],
                                        src_[:, g, :], identb[:])
                                C1.inc(ins)
                            Te(tt2)

                            def vt2(e, dstf=dstf, gg=gg):
                                C1.wait(e)
                                e.tensor_copy(
                                    dstf[:, gg * 512:(gg + 1) * 512], pT3[:])
                            Ve(vt2)

        def phase3b(vtm, qfm, kfm, aosb, oarena):
            def vos(e):
                SW.wait_mark(e, M[("oproj",)])
                for ei in range(E):
                    e.reduce_sum(osumf[:, ei:ei + 1], oarena[:, ei, :],
                                 axis=AX.X)
                e.tensor_copy(osum[:], osumf[:])
            Ve(vos)

            with nc.sbuf_tensor("Ssb", [128, L], F32) as Ssb, \
                 nc.sbuf_tensor("Pb", [128, 2, L], BF16) as Pb, \
                 nc.sbuf_tensor("PTs", [128, 512], BF16) as PTs, \
                 nc.sbuf_tensor("negm", [128, 1], F32) as negm, \
                 nc.psum_tensor("pS", [128, 2, 512], F32) as pS, \
                 nc.psum_tensor("pPT", [128, 512], BF16) as pPT, \
                 nc.psum_tensor("pAO", [128, 128], F32) as pAO:
                b5 = C5.n
                gkb = [0]
                sevd = {}

                def emit_strip_front(i):
                    Lk = (i + 1) * 128
                    nkb = (Lk + 511) // 512
                    gk0 = gkb[0]

                    b3c, b2c = C3.n, C2.n

                    def tS(e, i=i, Lk=Lk, nkb=nkb, gk0=gk0,
                           b3c=b3c, b2c=b2c):
                        for kb in range(nkb):
                            g = gk0 + kb + 1
                            if kb >= 2:
                                kp = kb - 2
                                if kp % 2 == 0:
                                    C3.wait(e, b3c + kp // 2 + 1)
                                else:
                                    C2.wait(e, b2c + (kp - 1) // 2 + 1)
                            elif g > 2:
                                cwait(e, sevd[g - 2])
                            N = min(512, Lk - kb * 512)
                            ins = e.matmul(pS[:, g % 2, 0:N],
                                           qfm[:, i * 128:(i + 1) * 128],
                                           kfm[:, kb * 512:kb * 512 + N],
                                           start=True, stop=True)
                            C5.inc(ins)
                    Te(tS)
                    gkb[0] += nkb

                    def sSv(e, i=i, Lk=Lk, nkb=nkb, gk0=gk0):
                        if i > 0:
                            C2.wait(e, M[("exp", i - 1)])
                        for kb in range(0, nkb, 2):
                            g = gk0 + kb + 1
                            C5.wait(e, b5 + g)
                            N = min(512, Lk - kb * 512)
                            e.tensor_scalar(Ssb[:, kb * 512:kb * 512 + N],
                                            pS[:, g % 2, 0:N],
                                            ISC, None, ALU.mult)
                            sevd[g] = (C3, C3.n)
                    Ve(sSv)

                    def sSs(e, i=i, Lk=Lk, nkb=nkb, gk0=gk0):
                        for kb in range(1, nkb, 2):
                            g = gk0 + kb + 1
                            C5.wait(e, b5 + g)
                            N = min(512, Lk - kb * 512)
                            e.activation(Ssb[:, kb * 512:kb * 512 + N],
                                         pS[:, g % 2, 0:N],
                                         AF.Copy, scale=ISC)
                            sevd[g] = (C2, C2.n)
                    Se(sSs)

                    def vS(e, i=i, Lk=Lk):
                        C2.wait(e)
                        e.tensor_tensor(Ssb[:, i * 128:(i + 1) * 128],
                                        Ssb[:, i * 128:(i + 1) * 128],
                                        causal[:], ALU.add)
                        e.reduce_max(negm[:], Ssb[:, 0:Lk], axis=AX.X,
                                     negate=True)
                    Ve(vS)
                    M[("vS", i)] = C3.n

                    def sE(e, i=i, Lk=Lk):
                        C3.wait(e)
                        if i >= 2:
                            C1.wait(e, M[("tTall", i - 2)])
                        e.activation(Pb[:, i % 2, 0:Lk], Ssb[:, 0:Lk], AF.Exp,
                                     bias=negm[:],
                                     accum_out=Lsum[:, i:i + 1])
                    Se(sE)
                    M[("exp", i)] = C2.n

                def emit_strip_back(i):
                    ngrp = (i + 4) // 4
                    for g in range(ngrp):
                        j0 = 4 * g
                        j1 = min(i, 4 * g + 3)

                        def tT(e, i=i, j0=j0, j1=j1):
                            C2.wait(e, M[("exp", i)])
                            if ("ptev",) in M:
                                C3.wait(e, M[("ptev",)])
                            ins = None
                            for j in range(j0, j1 + 1):
                                ins = e.transpose(
                                    pPT[:, (j - j0) * 128:(j - j0 + 1) * 128],
                                    Pb[:, i % 2, j * 128:(j + 1) * 128],
                                    identb[:])
                            C1.inc(ins)
                        Te(tT)
                        tmark = C1.n

                        def vT(e, j0=j0, j1=j1, tmark=tmark):
                            C1.wait(e, tmark)
                            if ("pvmm",) in M:
                                C1.wait(e, M[("pvmm",)])
                            e.tensor_copy(PTs[:, 0:(j1 - j0 + 1) * 128],
                                          pPT[:, 0:(j1 - j0 + 1) * 128])
                        Ve(vT)
                        M[("ptev",)] = C3.n

                        def tV(e, i=i, j0=j0, j1=j1):
                            C3.wait(e, M[("ptev",)])
                            if j0 == 0 and i > 0:
                                C2.wait(e, M[("aoev", i - 1)])
                            ins = None
                            for j in range(j0, j1 + 1):
                                ins = e.matmul(
                                    pAO[:],
                                    PTs[:, (j - j0) * 128:(j - j0 + 1) * 128],
                                    vtm[:, j, :],
                                    start=(j == 0), stop=(j == i))
                            C1.inc(ins)
                        Te(tV)
                        M[("pvmm",)] = C1.n

                    def sA(e, i=i):
                        C1.wait(e, M[("pvmm",)])
                        e.activation(aosb[:, i, :], pAO[:], AF.Copy)
                    Se(sA)
                    M[("aoev", i)] = C2.n
                    M[("tTall", i)] = C1.n

                for i in range(NT):
                    emit_strip_front(i)
                    if i > 0:
                        emit_strip_back(i - 1)
                emit_strip_back(NT - 1)

                def vw(e):
                    C2.wait(e, M[("aoev", NT - 1)])
                    e.reciprocal(Lsum[:], Lsum[:])
                    for g in range(NT):
                        e.tensor_scalar(wpr[:, g, :], wrt[:, g, :],
                                        Lsum[:, g:g + 1], None, ALU.mult)
                Ve(vw)
                M[("wpr",)] = C3.n

        def phase3c(aosb, oarena, kfm, qfm, vtm):
            T = 256
            nchunks = L // T
            with nc.sbuf_tensor("a2", [128, 2, NCH, T], F32) as a2, \
                 nc.sbuf_tensor("asq2", [128, 4, T], BF16) as asq2, \
                 nc.psum_tensor("pO2", [128, 4, 512], F32) as pO2, \
                 nc.psum_tensor("pS2", [1, 512], F32) as pS2, \
                 nc.psum_tensor("pB1", [128, 512], F32) as pB1, \
                 nc.psum_tensor("pB2", [128, 512], F32) as pB2, \
                 nc.psum_tensor("pWT", [128, 1024], BF16) as pWT:

                wstA = kfm[:].rearrange("p (e t c) -> p e t c", e=E, t=2)
                waoF = qfm[:].rearrange("p (e c) -> p e c", e=E)
                _rows = vtm[:].rearrange("p a b -> p (a b)").bitcast(F32)
                ys = _rows[:, 0:1024].rearrange("p (a b c) -> p a b c",
                                                a=2, b=2)
                qmu = _rows[0:1, 1024:1280]
                qvar = _rows[0:1, 1280:1536]
                qsq = _rows[0:1, 1536:1792]

                def loady(e, t, pg):
                    ln = SD.group(e)
                    ins = e.dma_start(
                        ys[:, pg % 2, :, :],
                        fmap(y32_s)[:, 2 * pg:2 * pg + 2, t * T:(t + 1) * T])
                    ln.inc(ins, 16)
                    M[("ys", t, pg)] = (ln, ln.n)

                def ginit(e):
                    SM.wait_all(e)
                    loady(e, 0, 0)
                    loady(e, 0, 1)
                Ge(ginit)

                def emit_vscA(t):
                    def vscA(e, t=t):
                        if t == 0:
                            C3.wait(e, M[("wpr",)])
                        else:
                            C1.wait(e, M[("o2mm", t - 1)])
                            C1.wait(e, M[("wtrA", t - 1)])
                        e.tensor_tensor(
                            wstA[:, :, :, :],
                            aosb[:, 2 * t:2 * t + 2, :].rearrange(
                                "p (o t) c -> p o t c", o=1)
                            .broadcast_to([128, E, 2, 128]),
                            wpr[:, 2 * t:2 * t + 2, :].rearrange(
                                "p t e -> p e t").rearrange(
                                "p e (t o) -> p e t o", o=1)
                            .broadcast_to([128, E, 2, 128]),
                            ALU.mult)
                    Ve(vscA)
                    M[("wscA", t)] = C3.n

                def emit_wao(t):
                    for g4 in range(4):
                        def ttr(e, t=t, g4=g4):
                            C3.wait(e, M[("wscA", t)])
                            if (t, g4) != (0, 0):
                                C3.wait(e, M[("wevA",)])
                            ins = None
                            for k in range(8):
                                ei, tau = (8 * g4 + k) // 2, (8 * g4 + k) % 2
                                ins = e.transpose(
                                    pWT[:, k * 128:(k + 1) * 128],
                                    wstA[:, ei, tau, :], identb[:])
                            C1.inc(ins)
                        Te(ttr)
                        M[("wtr", t, g4)] = C1.n

                        def vev(e, t=t, g4=g4):
                            C1.wait(e, M[("wtr", t, g4)])
                            if t > 0:
                                C1.wait(e, M[("o2mm", t - 1)])
                            e.tensor_copy(
                                waoF[:, :, :].rearrange(
                                    "p e (t c) -> p e t c", t=2)
                                [:, 4 * g4:4 * g4 + 4, :, :],
                                pWT[:].rearrange(
                                    "p (e t c) -> p e t c", e=4, t=2))
                        Ve(vev)
                        M[("wevA",)] = C3.n
                    M[("wtrA", t)] = C1.n
                    M[("wev", t, E - 1)] = C3.n

                def emit_out2(t):
                    bQ3 = C3.n
                    for q in range(4):
                        def tq(e, t=t, q=q, bQ3=bQ3):
                            C3.wait(e, M[("wev", t, E - 1)])
                            if q > 0:
                                C3.wait(e, bQ3 + q)
                            ins = None
                            for ei in range(E):
                                for s in range(4 * q, 4 * q + 4):
                                    ins = e.matmul(
                                        pO2[:, s % 4, 0:T],
                                        oarena[:, ei, s * 128:(s + 1) * 128],
                                        waoF[:, ei, :],
                                        start=(ei == 0), stop=(ei == E - 1))
                                if q == 0:
                                    ins = e.matmul(
                                        pS2[0:1, 0:T], osum[:, ei:ei + 1],
                                        waoF[:, ei, :],
                                        start=(ei == 0), stop=(ei == E - 1))
                            C1.inc(ins)
                        Te(tq)
                        M[("o2", t, q)] = C1.n

                        def vvQ(e, t=t, q=q):
                            C1.wait(e, M[("o2", t, q)])
                            if q == 0 and t >= 2:
                                SM.wait_mark(e, M[("zstore", t - 2)])
                            SD.wait_mark(e, M[("ys", t, 2 * q)])
                            SD.wait_mark(e, M[("ys", t, 2 * q + 1)])
                            e.tensor_tensor(
                                a2[:, t % 2, 4 * q:4 * q + 4, :],
                                pO2[:, :, 0:T],
                                ys[:, :, :, :].rearrange("p a b c -> p (a b) c"),
                                ALU.add)
                        Ve(vvQ)

                        def ssQ(e, t=t, q=q):
                            e.tensor_tensor(asq2[:, :, :],
                                            a2[:, t % 2, 4 * q:4 * q + 4, :],
                                            a2[:, t % 2, 4 * q:4 * q + 4, :],
                                            ALU.mult)
                        Ve(ssQ)
                        M[("sq3", t, q)] = C3.n

                        def tsQ(e, t=t, q=q):
                            C3.wait(e, M[("sq3", t, q)])
                            if q == 0 and t > 0:
                                C3.wait(e, M[("vz", t - 1)])
                            e.matmul(pB1[0:1, 0:2 * T], onesb[:],
                                     asq2[:, 0:2, :].rearrange(
                                         "p a b -> p (a b)"),
                                     start=True, stop=True)
                            ins = e.matmul(pB2[0:1, 0:2 * T], onesb[:],
                                           asq2[:, 2:4, :].rearrange(
                                               "p a b -> p (a b)"),
                                           start=True, stop=True)
                            C1.inc(ins)
                        Te(tsQ)

                        def vsqQ(e, t=t, q=q):
                            C1.wait(e)
                            if q == 0:
                                e.tensor_copy(qsq[:], pB1[0:1, 0:T])
                            else:
                                e.tensor_tensor(qsq[:], qsq[:],
                                                pB1[0:1, 0:T], ALU.add)
                            e.tensor_tensor(qsq[:], qsq[:], pB1[0:1, T:2 * T],
                                            ALU.add)
                            e.tensor_tensor(qsq[:], qsq[:], pB2[0:1, 0:T],
                                            ALU.add)
                            e.tensor_tensor(qsq[:], qsq[:], pB2[0:1, T:2 * T],
                                            ALU.add)
                        Ve(vsqQ)

                        def gly(e, t=t, q=q):
                            C1.wait(e)
                            if q < 3:
                                loady(e, t, 2 * q + 2)
                                loady(e, t, 2 * q + 3)
                            elif t + 1 < nchunks:
                                loady(e, t + 1, 0)
                                loady(e, t + 1, 1)
                        Ge(gly)
                    M[("o2mm", t)] = C1.n

                def emit_rows(t):
                    def vr(e, t=t):
                        C1.wait(e)
                        if t > 0:
                            C1.wait(e, M[("bc", t - 1)])
                        e.tensor_scalar(qvar[:], qsq[:], 1.0 / D, None,
                                        ALU.mult)
                        e.tensor_scalar(qmu[:], pS2[0:1, 0:T], 1.0 / D, None,
                                        ALU.mult)
                        e.tensor_tensor(qsq[:], qmu[:], qmu[:], ALU.mult)
                        e.tensor_tensor(qvar[:], qvar[:], qsq[:],
                                        ALU.subtract)
                    Ve(vr)

                    def sr(e, t=t):
                        C3.wait(e)
                        e.activation(qvar[:], qvar[:], AF.Sqrt, bias=1e-5)
                    Se(sr)

                    def vr2(e, t=t):
                        C2.wait(e)
                        e.reciprocal(qvar[:], qvar[:])
                    Ve(vr2)
                    M[("rows", t)] = C3.n

                def emit_finish(t):
                    def tb(e, t=t):
                        C3.wait(e, M[("rows", t)])
                        e.matmul(pB1[:, 0:T], o1x128[:], qmu[:], start=True,
                                 stop=True)
                        ins = e.matmul(pB2[:, 0:T], o1x128[:], qvar[:],
                                       start=True, stop=True)
                        C1.inc(ins)
                        M[("bc", t)] = C1.n
                    Te(tb)

                    def vz(e, t=t):
                        C1.wait(e, M[("bc", t)])
                        bc1 = pB1[:, 0:T].rearrange("p (o b) -> p o b", o=1) \
                                         .broadcast_to([128, NCH, T])
                        bc2 = pB2[:, 0:T].rearrange("p (o b) -> p o b", o=1) \
                                         .broadcast_to([128, NCH, T])
                        e.tensor_tensor(a2[:, t % 2, :, :], a2[:, t % 2, :, :],
                                        bc1, ALU.subtract)
                        e.tensor_tensor(a2[:, t % 2, :, :], a2[:, t % 2, :, :],
                                        bc2, ALU.mult)
                    Ve(vz)
                    M[("vz", t)] = C3.n
                    M[("z", t)] = C3.n

                    def yz(e, t=t):
                        C3.wait(e, M[("z", t)])
                        ln = SM.group(e)
                        for grp in range(4):
                            ln.inc(e.dma_start(
                                fmap(z_fm)[:, 4 * grp:4 * grp + 4,
                                           t * T:(t + 1) * T],
                                a2[:, t % 2, 4 * grp:4 * grp + 4, :]), 16)
                        M[("zstore", t)] = (ln, ln.n)
                    Ye(yz)

                emit_vscA(0)
                emit_wao(0)
                for t in range(nchunks):
                    emit_out2(t)
                    if t + 1 < nchunks:
                        emit_vscA(t + 1)
                    emit_rows(t)
                    if t + 1 < nchunks:
                        emit_wao(t + 1)
                    emit_finish(t)

                def yfin(e):
                    SM.wait_all(e)
                    SD.wait_all(e)
                Ye(yfin)

        with contextlib.ExitStack() as B:
            warena1 = B.enter_context(
                nc.sbuf_tensor("warena1", [128, NCH, D], BF16))
            KVb = B.enter_context(nc.sbuf_tensor("KVb", [128, NH, 128], BF16))
            xbf2b = B.enter_context(
                nc.sbuf_tensor("xbf2b", [128, NCH, 512], BF16))
            phase0(warena1)
            barrier()
            phase1(warena1, KVb, xbf2b)
            barrier()
            phase2a(warena1)
            phase2b(warena1, KVb, xbf2b)
            barrier()

        with contextlib.ExitStack() as DD:
            vtm = DD.enter_context(nc.sbuf_tensor("vtm3", [128, NT, 128], BF16))
            aosb = DD.enter_context(nc.sbuf_tensor("aosb", [128, NT, 128], BF16))
            qfm = DD.enter_context(nc.sbuf_tensor("qfm", [128, L], BF16))
            kfm = DD.enter_context(nc.sbuf_tensor("kfm", [128, L], BF16))
            oarena = DD.enter_context(nc.sbuf_tensor("oarena", [128, E, D], BF16))
            phase2c(oarena, kfm, qfm, vtm, aosb)
            barrier()
            phase3a(vtm, qfm, kfm, oarena, aosb)
            barrier()
            phase3b(vtm, qfm, kfm, aosb, oarena)
            barrier()
            phase3c(aosb, oarena, kfm, qfm, vtm)

        for c in chains:
            c.check()
    return nc


_NC_CACHE = {}


def _get_program():
    key = ("prog", DEBUG)
    if key not in _NC_CACHE:
        _NC_CACHE[key] = build_program()
    return _NC_CACHE[key]


def kernel(L0_attn_x, L0_attn_wq, L0_attn_wk, L0_attn_wv, L0_attn_wu, L0_attn_wo,
           L0_ffn_sim_matrix, L0_ffn_gates, L0_ffn_q_proj, L0_ffn_k_proj,
           L0_ffn_v_proj, L0_ffn_o_proj):
    nc_prog = _get_program()
    c = np.ascontiguousarray
    shared = {
        "wqT": c(L0_attn_wq.T), "wkT": c(L0_attn_wk.T), "wvT": c(L0_attn_wv.T),
        "wuT": c(L0_attn_wu.T), "woT": c(L0_attn_wo.T),
        "pallq": c(L0_ffn_q_proj.transpose(1, 0, 2).reshape(D, D)),
        "pallk": c(L0_ffn_k_proj.transpose(1, 0, 2).reshape(D, D)),
        "pallv": c(L0_ffn_v_proj.transpose(1, 0, 2).reshape(D, D)),
        "oprojh": c(L0_ffn_o_proj.transpose(1, 0, 2)),
        "sim": c(L0_ffn_sim_matrix),
        "gates": c(L0_ffn_gates.reshape(E, 1)),
    }
    in_maps = []
    for b in range(8):
        m = dict(shared)
        m["x_fm"] = c(L0_attn_x[b].T)
        in_maps.append(m)
    res = run_bass_kernel_spmd(nc_prog, in_maps, core_ids=list(range(8)))
    out = np.stack([res.results[b]["z_fm"].T for b in range(8)], axis=0)
    return np.ascontiguousarray(out.astype(np.float32))


# version 13 (final)
# speedup vs baseline: 1.2207x; 1.0101x over previous
import contextlib
import math

import numpy as np
import ml_dtypes

import concourse.bass as bass
import concourse.mybir as mybir
from concourse.bass_utils import run_bass_kernel_spmd

F32 = mybir.dt.float32
BF16 = mybir.dt.bfloat16
AF = mybir.ActivationFunctionType
ALU = mybir.AluOpType
AX = mybir.AxisListType

L = 4096
D = 2048
NCH = 16
NH = 16
E = 16
NT = 32
ISC = 1.0 / math.sqrt(128.0)
NEG = -1.0e9

DEBUG = False


class Chain:
    def __init__(self, ctx, nc, name):
        self.h = ctx.enter_context(nc.semaphore(name))
        self.name = name
        self.n = 0
        self.maxwait = 0

    def inc(self, ins, k=1):
        ins.then_inc(self.h, k)
        self.n += k
        return self.n

    def wait(self, eng, tgt=None):
        if tgt is None:
            tgt = self.n
        if tgt <= 0:
            return
        self.maxwait = max(self.maxwait, tgt)
        eng.wait_ge(self.h, tgt)

    def check(self):
        assert self.maxwait <= self.n, \
            f"chain {self.name}: wait {self.maxwait} > final {self.n} (hang!)"


class DmaPool:

    def __init__(self, ctx, nc, name, n=6):
        self.lanes = [Chain(ctx, nc, f"{name}{i}") for i in range(n)]
        self.idx = 0

    def group(self, eng):
        ln = self.lanes[self.idx % len(self.lanes)]
        self.idx += 1
        if ln.n > 0:
            eng.wait_ge(ln.h, ln.n)
        return ln

    def wait_mark(self, eng, mark):
        ln, v = mark
        ln.wait(eng, v)

    def wait_all(self, eng):
        for ln in self.lanes:
            ln.wait(eng, ln.n)

    def check(self):
        for ln in self.lanes:
            ln.check()


def fmap(ap):
    return ap.rearrange("(j p) n -> p j n", p=128)


_SS_OPS = frozenset([
    "tensor_tensor", "tensor_scalar", "tensor_copy", "tensor_add",
    "tensor_sub", "tensor_mul", "tensor_max", "tensor_scalar_add",
    "tensor_scalar_sub", "tensor_scalar_mul", "tensor_scalar_max",
    "tensor_scalar_min", "tensor_relu", "tensor_reduce", "reduce_max",
    "reduce_sum", "scalar_tensor_tensor", "tensor_tensor_scan",
    "reciprocal", "memset", "memzero", "select", "activation",
    "tensor_single_scalar",
])


class SelfSync:

    def __init__(self, eng, chain):
        self._e = eng
        self._c = chain

    def __getattr__(self, name):
        attr = getattr(self._e, name)
        if name in _SS_OPS:
            def wrapped(*a, **k):
                self._c.wait(self._e)
                ins = attr(*a, **k)
                self._c.inc(ins)
                return ins
            return wrapped
        return attr


def build_program():
    nc = bass.Bass("TRN2", target_bir_lowering=False,
                   dynamic_dma_scratch_size=16384)

    x_fm = nc.dram_tensor("x_fm", [D, L], F32, kind="ExternalInput").ap()
    wT = {k: nc.dram_tensor(k, [D, D], F32, kind="ExternalInput").ap()
          for k in ["wqT", "wkT", "wvT", "wuT", "woT"]}
    pallh = {k: nc.dram_tensor(k, [D, D], F32, kind="ExternalInput").ap()
             for k in ["pallq", "pallk", "pallv"]}
    oproj_h = nc.dram_tensor("oprojh", [128, E, D], F32, kind="ExternalInput").ap()
    sim_h = nc.dram_tensor("sim", [D, E], F32, kind="ExternalInput").ap()
    gates_h = nc.dram_tensor("gates", [E, 1], F32, kind="ExternalInput").ap()
    z_fm = nc.dram_tensor("z_fm", [D, L], F32, kind="ExternalOutput").ap()

    SCR = "ExternalOutput" if DEBUG else "Internal"
    qbf_s = nc.dram_tensor("qbf_s", [D, L], BF16, kind=SCR).ap()
    oub_s = nc.dram_tensor("oub_s", [D, L], BF16, kind=SCR).ap()
    y32_s = nc.dram_tensor("y32_s", [D, L], F32, kind=SCR).ap()
    ybf_s = nc.dram_tensor("ybf_s", [D, L], BF16, kind=SCR).ap()
    dbg_arena = nc.dram_tensor("dbg_arena", [D, D], BF16, kind=SCR).ap() if DEBUG else None

    identb_h = nc.inline_tensor(np.eye(128, dtype=ml_dtypes.bfloat16), "c_identb").ap()
    _idf = np.zeros((48, 16), dtype=np.float32)
    _idf[0:16] = np.eye(16); _idf[32:48] = np.eye(16)
    idf32_h = nc.inline_tensor(_idf, "c_idf32").ap()
    causal_np = np.where(np.arange(128)[:, None] >= np.arange(128)[None, :],
                         0.0, NEG).astype(ml_dtypes.bfloat16)
    causal_h = nc.inline_tensor(causal_np, "c_causal").ap()
    onesb_h = nc.inline_tensor(np.ones((128, 1), dtype=ml_dtypes.bfloat16), "c_onesb").ap()
    o1x128_h = nc.inline_tensor(np.ones((1, 128), dtype=np.float32), "c_o1x128").ap()
    o1x16_h = nc.inline_tensor(np.ones((1, 16), dtype=np.float32), "c_o1x16").ap()
    selcol_np = np.zeros((128, 16, 16), dtype=ml_dtypes.bfloat16)
    for h in range(16):
        selcol_np[:, h, h] = 1.0
    selcol_h = nc.inline_tensor(selcol_np.reshape(128, 256), "c_selcol").ap()
    selrow_np = np.zeros((16, 8, 128), dtype=ml_dtypes.bfloat16)
    for h in range(8):
        selrow_np[h, h, :] = 1.0
    selrow_h = nc.inline_tensor(selrow_np.reshape(16, 1024), "c_selrow").ap()
    cv_h = nc.inline_tensor(
        np.broadcast_to(np.array([1e-5], np.float32), (128, 1)).copy(), "c_cv").ap()

    with contextlib.ExitStack() as A:
        sbA = lambda n, s, d=F32: A.enter_context(nc.sbuf_tensor(n, s, d))
        identb = sbA("identb", [128, 128], BF16)
        idf32 = sbA("idf32", [48, 16])
        causal = sbA("causal", [128, 128], BF16)
        onesb = sbA("onesb", [128, 1], BF16)
        onesf = sbA("onesf", [128, 1])
        o1x128 = sbA("o1x128", [1, 128])
        o1x16 = sbA("o1x16", [1, 16])
        selcol = sbA("selcol", [128, 16, 16], BF16)
        selrow = A.enter_context(
            nc.sbuf_tensor("selrow", [16, 8, 128], BF16))
        constv = sbA("constv", [128, 1])
        nc.const_aps.register(F32, 1e-5, constv[:, 0:1])
        arena0 = sbA("arena0", [128, NCH, D], BF16)
        wosum = sbA("wosum", [128, NCH], BF16)
        osum = sbA("osum", [128, E], BF16)
        wosumf = sbA("wosumf", [128, NCH])
        osumf = sbA("osumf", [128, E])
        simn = sbA("simn", [128, NCH, E])
        negcs = sbA("negcs", [1, E])
        sigG = sbA("sigG", [E, 1])
        wrt = sbA("wrt", [128, NT, E])
        wpr = sbA("wpr", [128, NT, E])
        Lsum = sbA("Lsum", [128, NT])
        rt16 = sbA("rt16", [128, 5, 64])
        rtc = sbA("rtc", [128, 12])

        SD = DmaPool(A, nc, "SD", 6)
        SW = DmaPool(A, nc, "SW", 2)
        SM = DmaPool(A, nc, "SM", 4)
        C1 = Chain(A, nc, "C1")
        C2 = Chain(A, nc, "C2")
        C3 = Chain(A, nc, "C3")
        C4 = Chain(A, nc, "C4")
        C5 = Chain(A, nc, "C5")
        SBC = Chain(A, nc, "SBC")
        chains = [SD, SW, SM, C1, C2, C3, C4, C5, SBC]

        block = A.enter_context(nc.Block())
        ENGS = ["tensor", "vector", "scalar", "gpsimd", "sync"]
        Te, Ge, Ye = block.tensor, block.gpsimd, block.sync
        Ve = lambda f: block.vector(lambda e: f(SelfSync(e, C3)))
        Se = lambda f: block.scalar(lambda e: f(SelfSync(e, C2)))

        def barrier():
            def pre(e):
                for ch in (C1, C2, C3, C4, C5):
                    ch.wait(e)
                for pool in (SD, SW, SM):
                    pool.wait_all(e)
                SBC.inc(e.nop())
            for en in ENGS:
                getattr(block, en)(pre)
            tgt = SBC.n
            for en in ENGS:
                getattr(block, en)(lambda e, tgt=tgt: e.wait_ge(SBC.h, tgt))

        def load_arena(e, arena, src, pool):
            ln = pool.group(e)
            for g in range(4):
                ins = e.dma_start(arena[:, 4 * g:4 * g + 4, :],
                                  fmap(src)[:, 4 * g:4 * g + 4, :])
                ln.inc(ins, 16)
            return (ln, ln.n)

        M = {}

        def cwait(e, mark):
            ch, v = mark
            ch.wait(e, v)

        def phase0(warena1):
            def y0(e):
                ln = SM.group(e)
                for dst, src in [
                        (identb[:], identb_h[:]), (idf32[:], idf32_h[:]),
                        (causal[:], causal_h[:]), (onesb[:], onesb_h[:]),
                        (o1x128[:], o1x128_h[:]), (o1x16[:], o1x16_h[:]),
                        (selcol[:].rearrange("p a b -> p (a b)"), selcol_h[:]),
                        (selrow[:].rearrange("p a b -> p (a b)"), selrow_h[:]),
                        (constv[:], cv_h[:]), (sigG[:], gates_h[:]),
                        (simn[:, :, :],
                         sim_h.rearrange("(a p) b -> p a b", p=128))]:
                    ln.inc(e.dma_start(dst, src), 16)
                M[("setup",)] = (ln, ln.n)
            Ye(y0)

            def g0(e):
                M[("wk",)] = load_arena(e, arena0, wT["wkT"], SW)
                M[("wv",)] = load_arena(e, warena1, wT["wvT"], SW)
            Ge(g0)

            def v0(e):
                SM.wait_mark(e, M[("setup",)])
                e.tensor_copy(onesf[:], onesb[:])
            Ve(v0)

            def s0(e):
                C3.wait(e)
                e.activation(sigG[:], sigG[:], AF.Exp, scale=-1.0)
            Se(s0)

            def v1(e):
                C2.wait(e)
                e.tensor_scalar(sigG[:], sigG[:], 1.0, None, ALU.add)
                e.reciprocal(sigG[:], sigG[:])
            Ve(v1)

            with nc.psum_tensor("p0a", [1, 16], F32) as p0a, \
                 nc.psum_tensor("p0b", [128, 16], F32) as p0b, \
                 nc.sbuf_tensor("simsq", [128, 16], F32) as simsq:
                for j in range(NCH):
                    def vq(e, j=j):
                        C3.wait(e)
                        if j > 0:
                            C1.wait(e)
                        e.tensor_tensor(simsq[:], simn[:, j, :],
                                        simn[:, j, :], ALU.mult)
                    Ve(vq)

                    def tq(e, j=j):
                        C3.wait(e)
                        ins = e.matmul(p0a[:], onesf[:], simsq[:],
                                       start=(j == 0), stop=(j == NCH - 1))
                        C1.inc(ins)
                    Te(tq)

                def s1(e):
                    C1.wait(e)
                    e.activation(simsq[0:1, 0:16], p0a[:], AF.Sqrt)
                Se(s1)

                def v2(e):
                    C2.wait(e)
                    e.tensor_scalar(simsq[0:1, 0:16], simsq[0:1, 0:16],
                                    1e-12, None, ALU.max)
                    e.reciprocal(simsq[0:1, 0:16], simsq[0:1, 0:16])
                Ve(v2)

                def t1(e):
                    C3.wait(e)
                    ins = e.matmul(p0b[:], o1x128[:], simsq[0:1, 0:16],
                                   start=True, stop=True)
                    C1.inc(ins)
                Te(t1)

                def v3(e):
                    C1.wait(e)
                    e.tensor_copy(simsq[:], p0b[:])
                    for j in range(NCH):
                        e.tensor_tensor(simn[:, j, :], simn[:, j, :],
                                        simsq[:], ALU.mult)
                Ve(v3)

                for j in range(NCH):
                    def tr(e, j=j):
                        C3.wait(e)
                        ins = e.matmul(p0a[:], onesf[:], simn[:, j, :],
                                       start=(j == 0), stop=(j == NCH - 1))
                        if j == NCH - 1:
                            C1.inc(ins)
                    Te(tr)

                def v4(e):
                    C1.wait(e)
                    e.tensor_scalar(negcs[:], p0a[:], -1.0, None, ALU.mult)
                Ve(v4)

        def phase1(warena1, KVb, xbf2b):
            T = 512
            nchunks = L // T
            KVacc = xbf2b[:].rearrange("p a b -> p (a b)").bitcast(F32) \
                [:, 0:D]
            with nc.sbuf_tensor("xbf1", [128, 2, NCH, T], BF16) as xbf, \
                 nc.sbuf_tensor("ktm", [128, D], BF16) as ktm, \
                 nc.sbuf_tensor("vtm", [128, D], BF16) as vtm, \
                 nc.psum_tensor("pKV", [128, D], F32) as pKV, \
                 nc.psum_tensor("pP1", [128, D], F32) as pP1:
                b1, b2, b3, b4 = C1.n, C2.n, C3.n, C4.n
                xld = {}

                def loadx(e, t):
                    ln = SD.group(e)
                    for grp in range(4):
                        ins = e.dma_start(xbf[:, t % 2, 4 * grp:4 * grp + 4, :],
                                          fmap(x_fm)[:, 4 * grp:4 * grp + 4,
                                                     t * T:(t + 1) * T])
                        ln.inc(ins, 16)
                    xld[t] = (ln, ln.n)

                Ge(lambda e: (loadx(e, 0), loadx(e, 1)))

                for t in range(nchunks):
                    if t >= 2:
                        def gl(e, t=t):
                            C1.wait(e, b1 + 16 * (t - 1))
                            loadx(e, t)
                        Ge(gl)
                    for tau in range(4):
                        tile = t * 4 + tau

                        def tf(e, t=t, tile=tile, tau=tau):
                            SD.wait_mark(e, xld[t])
                            if tile == 0:
                                SW.wait_mark(e, M[("wk",)])
                                SW.wait_mark(e, M[("wv",)])
                            for which, wsrc in [(0, arena0), (1, warena1)]:
                                for half in range(2):
                                    g = 4 * tile + 2 * which + half
                                    if g >= 2:
                                        C2.wait(e, b2 + g - 1)
                                    ins = None
                                    for j in range(NCH):
                                        for nn in range(2):
                                            col = half * 1024 + nn * 512
                                            ins = e.matmul(
                                                pP1[:, col:col + 512],
                                                xbf[:, t % 2, j,
                                                    tau * 128:(tau + 1) * 128],
                                                wsrc[:, j, col:col + 512],
                                                start=(j == 0), stop=(j == NCH - 1))
                                    C1.inc(ins)
                            C2.wait(e, b2 + 4 * tile + 4)
                            if tile > 0:
                                C3.wait(e, b3 + tile)
                            ins = None
                            for h in range(NH):
                                ins = e.matmul(
                                    pKV[:, h * 128:(h + 1) * 128],
                                    ktm[:, h * 128:(h + 1) * 128],
                                    vtm[:, h * 128:(h + 1) * 128],
                                    start=True, stop=True)
                            C4.inc(ins)
                        Te(tf)

                        def vkv(e, tile=tile):
                            C4.wait(e, b4 + tile + 1)
                            if tile == 0:
                                e.tensor_copy(KVacc[:], pKV[:])
                            else:
                                e.tensor_tensor(KVacc[:], KVacc[:], pKV[:],
                                                ALU.add)
                        Ve(vkv)

                        def sf(e, tile=tile):
                            for which, dst in [(0, ktm), (1, vtm)]:
                                for half in range(2):
                                    C1.wait(e, b1 + 4 * tile + 2 * which + half + 1)
                                    if which == 0 and half == 0 and tile > 0:
                                        C4.wait(e, b4 + tile)
                                    func = AF.Relu if which == 0 else AF.Copy
                                    sc = ISC if which == 0 else 1.0
                                    e.activation(
                                        dst[:, half * 1024:(half + 1) * 1024],
                                        pP1[:, half * 1024:(half + 1) * 1024],
                                        func, scale=sc)
                        Se(sf)

                def skv(e):
                    C3.wait(e, b3 + NT)
                    e.activation(KVb[:].rearrange("p a b -> p (a b)"),
                                 KVacc[:], AF.Copy)
                Se(skv)

        def phase2a(warena1):
            T = 512
            nchunks = L // T

            def gw(e):
                M[("wq",)] = load_arena(e, arena0, wT["wqT"], SW)
                M[("wu",)] = load_arena(e, warena1, wT["wuT"], SW)
            Ge(gw)

            with nc.sbuf_tensor("xbf2a", [128, 2, NCH, T], BF16) as xbf, \
                 nc.sbuf_tensor("qst", [128, 2, 8, T], BF16) as qst, \
                 nc.psum_tensor("pQ", [128, 4, T], F32) as pQ:
                b1, b2 = C1.n, C2.n
                xld = {}

                def loadx(e, t):
                    ln = SD.group(e)
                    for grp in range(4):
                        ins = e.dma_start(xbf[:, t % 2, 4 * grp:4 * grp + 4, :],
                                          fmap(x_fm)[:, 4 * grp:4 * grp + 4,
                                                     t * T:(t + 1) * T])
                        ln.inc(ins, 16)
                    xld[t] = (ln, ln.n)

                Ge(lambda e: (loadx(e, 0), loadx(e, 1)))

                for t in range(nchunks):
                    if t >= 2:
                        def gl(e, t=t):
                            C1.wait(e, b1 + 16 * (t - 1))
                            loadx(e, t)
                        Ge(gl)

                    def tf(e, t=t):
                        SD.wait_mark(e, xld[t])
                        if t == 0:
                            SW.wait_mark(e, M[("wq",)])
                        for s in range(16):
                            g = 16 * t + s
                            if g >= 4:
                                C2.wait(e, b2 + g - 3)
                            ins = None
                            for j in range(NCH):
                                ins = e.matmul(pQ[:, s % 4, :],
                                               arena0[:, j, s * 128:(s + 1) * 128],
                                               xbf[:, t % 2, j, :],
                                               start=(j == 0), stop=(j == NCH - 1))
                            C1.inc(ins)
                    Te(tf)

                    def sf(e, t=t):
                        for s in range(16):
                            g = 16 * t + s
                            hglob = 2 * t + s // 8
                            if s % 8 == 0 and hglob >= 2:
                                SM.wait_mark(e, M[("qstore", hglob - 2)])
                            C1.wait(e, b1 + g + 1)
                            e.activation(qst[:, hglob % 2, s % 8, :],
                                         pQ[:, s % 4, :],
                                         AF.Relu, scale=ISC)
                    Se(sf)

                    for hh_ in range(2):
                        def yf(e, t=t, hh_=hh_):
                            hglob = 2 * t + hh_
                            C2.wait(e, b2 + 8 * (hglob + 1))
                            ln = SM.group(e)
                            for grp in range(2):
                                ln.inc(e.dma_start(
                                    fmap(qbf_s)[:, 8 * hh_ + 4 * grp:
                                                8 * hh_ + 4 * grp + 4,
                                                t * T:(t + 1) * T],
                                    qst[:, hglob % 2, 4 * grp:4 * grp + 4, :]),
                                    16)
                            M[("qstore", hglob)] = (ln, ln.n)
                        Ye(yf)
                if DEBUG:
                    def ydbg(e):
                        C1.wait(e)
                        ln = SM.group(e)
                        ln.inc(e.dma_start(
                            dbg_arena.rearrange("(j p) e -> p j e", p=128),
                            arena0[:, :, :]), 16)
                    Ye(ydbg)

        def phase2b(warena1, KVb, xbf):
            T = 512
            nchunks = L // T

            with nc.sbuf_tensor("qc", [128, 8, T], BF16) as qc, \
                 nc.sbuf_tensor("uf", [128, 8, T], F32) as uf, \
                 nc.sbuf_tensor("ofp", [128, 2, T], F32) as ofp, \
                 nc.sbuf_tensor("oust", [128, 8, T], BF16) as oust, \
                 nc.sbuf_tensor("osq", [128, 2, T], BF16) as osq, \
                 nc.sbuf_tensor("invh", [8, T], F32) as invh, \
                 nc.sbuf_tensor("invhb", [8, T], BF16) as invhb, \
                 nc.psum_tensor("pU", [128, 2, T], F32) as pU, \
                 nc.psum_tensor("pO", [128, 2, T], F32) as pO, \
                 nc.psum_tensor("pN", [8, T], F32) as pN, \
                 nc.psum_tensor("pB", [128, 2, T], F32) as pB:
                xld, qld = {}, {}

                def loadx(e, t):
                    ln = SD.group(e)
                    for grp in range(4):
                        ins = e.dma_start(xbf[:, 4 * grp:4 * grp + 4, :],
                                          fmap(x_fm)[:, 4 * grp:4 * grp + 4,
                                                     t * T:(t + 1) * T])
                        ln.inc(ins, 16)
                    xld[t] = (ln, ln.n)

                def loadq(e, t, H):
                    ln = SD.group(e)
                    ins = e.dma_start(qc[:, :, :],
                                      fmap(qbf_s)[:, 8 * H:8 * H + 8,
                                                  t * T:(t + 1) * T])
                    ln.inc(ins, 16)
                    qld[(t, H)] = (ln, ln.n)

                def gpre(e):
                    loadx(e, 0)
                Ge(gpre)
                barrier()

                def gw(e):
                    loadq(e, 0, 0)
                    M[("wo",)] = load_arena(e, arena0, wT["woT"], SW)
                Ge(gw)

                for t in range(nchunks):
                    for H in range(2):
                        hh = 2 * t + H
                        bU1, bU2 = C1.n, C2.n

                        def tU(e, t=t, H=H, hh=hh, bU2=bU2):
                            SD.wait_mark(e, xld[t])
                            if hh == 0:
                                SW.wait_mark(e, M[("wu",)])
                            for s8 in range(8):
                                s = 8 * H + s8
                                if s8 >= 2:
                                    C2.wait(e, bU2 + s8 - 1)
                                elif hh > 0:
                                    C3.wait(e, M[("vB", hh - 1)])
                                ins = None
                                for j in range(NCH):
                                    ins = e.matmul(
                                        pU[:, s8 % 2, :],
                                        warena1[:, j, s * 128:(s + 1) * 128],
                                        xbf[:, j, :],
                                        start=(j == 0), stop=(j == NCH - 1))
                                C1.inc(ins)
                        Te(tU)
                        if H == 1:
                            M[("ux", t)] = C1.n

                        def sU(e, bU1=bU1):
                            for s8 in range(8):
                                C1.wait(e, bU1 + s8 + 1)
                                e.activation(uf[:, s8, :], pU[:, s8 % 2, :],
                                             AF.Silu)
                        Se(sU)

                        bO1, bO2, bO5, bO3 = C1.n, C2.n, C5.n, C3.n

                        def tO(e, t=t, H=H, hh=hh, bO2=bO2):
                            SD.wait_mark(e, qld[(t, H)])
                            for h8 in range(8):
                                if h8 >= 2:
                                    C2.wait(e, bO2 + 2 * h8 - 2)
                                elif hh > 0:
                                    C2.wait(e, M[("ocp", hh - 1)])
                                ins = e.matmul(pO[:, h8 % 2, :],
                                               KVb[:, 8 * H + h8, :],
                                               qc[:, h8, :],
                                               start=True, stop=True)
                                C1.inc(ins)
                                if h8 >= 1:
                                    hprev = h8 - 1
                                    if hprev == 0 and hh > 0:
                                        C2.wait(e, M[("inv", hh - 1)])
                                    C2.wait(e, bO2 + 2 * hprev + 1)
                                    h = 8 * H + hprev
                                    ins = e.matmul(
                                        pN[:], selcol[:, h, 8 * H:8 * H + 8],
                                        osq[:, hprev % 2, :],
                                        start=(hprev == 0), stop=False)
                                    C5.inc(ins)
                            C2.wait(e, bO2 + 15)
                            ins = e.matmul(pN[:], selcol[:, 8 * H + 7,
                                                         8 * H:8 * H + 8],
                                           osq[:, 1, :], start=False, stop=True)
                            C5.inc(ins)
                        Te(tO)
                        M[("oq", t, H)] = C1.n

                        def sO(e, hh=hh, bO1=bO1, bO5=bO5, bO3=bO3):
                            for h8 in range(8):
                                C1.wait(e, bO1 + h8 + 1)
                                if h8 >= 2:
                                    C5.wait(e, bO5 + h8 - 1)
                                    C3.wait(e, bO3 + h8 - 1)
                                elif hh > 0:
                                    C5.wait(e, bO5)
                                    C3.wait(e, bO3)
                                e.activation(osq[:, h8 % 2, :], pO[:, h8 % 2, :],
                                             AF.Square)
                                e.activation(ofp[:, h8 % 2, :],
                                             pO[:, h8 % 2, :], AF.Copy)
                        Se(sO)
                        M[("ocp", hh)] = C2.n

                        def vOU(e, bO2=bO2):
                            for h8 in range(8):
                                C2.wait(e, bO2 + 2 * h8 + 2)
                                e.tensor_tensor(uf[:, h8, :],
                                                ofp[:, h8 % 2, :],
                                                uf[:, h8, :], ALU.mult)
                        Ve(vOU)

                        def sI(e, hh=hh, bO5=bO5):
                            C5.wait(e, bO5 + 8)
                            if hh > 0:
                                C1.wait(e, M[("invmm", hh - 1)])
                            e.activation(invh[:], pN[:], AF.Sqrt,
                                         scale=1.0 / 128.0)
                        Se(sI)
                        M[("inv", hh)] = C2.n

                        def vI(e, hh=hh):
                            C2.wait(e, M[("inv", hh)])
                            e.tensor_scalar(invh[:], invh[:], 1e-12, None, ALU.max)
                            e.reciprocal(invh[:], invh[:])
                            e.tensor_copy(invhb[:], invh[:])
                        Ve(vI)
                        vimark = C3.n

                        bB1, bB3 = C1.n, C3.n

                        def tB(e, vimark=vimark, bB3=bB3):
                            C3.wait(e, vimark)
                            for h8 in range(8):
                                if h8 >= 2:
                                    C3.wait(e, bB3 + h8 - 1)
                                ins = e.matmul(pB[:, h8 % 2, :],
                                               selrow[0:8, h8, :], invhb[:],
                                               start=True, stop=True)
                                C1.inc(ins)
                        Te(tB)
                        M[("invmm", hh)] = C1.n

                        def vB(e, hh=hh, bB1=bB1):
                            for h8 in range(8):
                                C1.wait(e, bB1 + h8 + 1)
                                if h8 == 0 and hh >= 1:
                                    SM.wait_mark(e, M[("oust", hh - 1)])
                                e.tensor_tensor(oust[:, h8, :],
                                                uf[:, h8, :],
                                                pB[:, h8 % 2, :], ALU.mult)
                        Ve(vB)
                        M[("vB", hh)] = C3.n

                        def yS(e, t=t, H=H, hh=hh):
                            C3.wait(e, M[("vB", hh)])
                            ln = SM.group(e)
                            ins = e.dma_start(
                                fmap(oub_s)[:, 8 * H:8 * H + 8, t * T:(t + 1) * T],
                                oust[:, :, :])
                            ln.inc(ins, 16)
                            M[("oust", hh)] = (ln, ln.n)
                        Ye(yS)

                        def gq(e, t=t, H=H):
                            C1.wait(e, M[("oq", t, H)])
                            if H == 0:
                                loadq(e, t, 1)
                            else:
                                if t + 1 < nchunks:
                                    C1.wait(e, M[("ux", t)])
                                    loadx(e, t + 1)
                                    loadq(e, t + 1, 0)
                        Ge(gq)

        def phase2c(oarena, kfm, qfm, vtm, aosb):
            T = 512
            nchunks = L // T

            abuf = oarena[:].rearrange("p a b -> p (a b)").bitcast(F32) \
                .rearrange("p (u j n) -> p u j n", u=2, j=NCH)
            kfv = kfm[:].bitcast(F32)
            r16u = kfv[0:48, 0:512]
            rmu = kfv[0:1, 512:1024]
            rvar = kfv[0:1, 1024:1536]
            rsinv = kfv[0:1, 1536:2048]
            vtv = vtm[:].rearrange("p a b -> p (a b)").bitcast(F32)
            rf = vtv[0:1, 0:512]
            rsq = vtv[0:1, 512:1024]
            ab4ba = vtm[:].rearrange("p a b -> p (a b)")[:, 2048:4096] \
                .rearrange("p (a b) -> p a b", a=4)
            asq4a = qfm[:, 0:2048].rearrange("p (a b) -> p a b", a=4)
            _ou01 = aosb[:].rearrange("p a b -> p (a b)") \
                .rearrange("p (s j n) -> p s j n", s=2, j=4)
            _ou2 = qfm[:, 2048:4096].rearrange("p (j n) -> p j n", j=4)

            def ouv(slot):
                return _ou01[:, slot, :, :] if slot < 2 else _ou2

            with nc.sbuf_tensor("xs", [128, 6, 2, T], F32) as xs, \
                 nc.sbuf_tensor("asq4b", [128, 4, T], BF16) as asq4b, \
                 nc.sbuf_tensor("ab4bb", [128, 4, T], BF16) as ab4bb, \
                 nc.psum_tensor("pW", [128, 4, T], F32) as pW, \
                 nc.psum_tensor("pStat", [1, T], F32) as pStat, \
                 nc.psum_tensor("pR", [16, T], F32) as pR, \
                 nc.psum_tensor("pX", [128, T], F32) as pX, \
                 nc.psum_tensor("pY", [128, T], F32) as pY:

                def loadou(e, t, qf):
                    slot = (4 * t + qf) % 3
                    for h in range(2):
                        ln = SD.group(e)
                        ins = e.dma_start(
                            ouv(slot)[:, 2 * h:2 * h + 2, :],
                            fmap(oub_s)[:, 4 * qf + 2 * h:4 * qf + 2 * h + 2,
                                        t * T:(t + 1) * T])
                        ln.inc(ins, 16)
                        M[("ou", t, qf, h)] = (ln, ln.n)

                def loadxp(e, t, pg):
                    slot = (8 * t + pg) % 6
                    for r in range(2):
                        ln = SD.group(e)
                        ins = e.dma_start(
                            xs[:, slot, r, :],
                            fmap(x_fm)[:, 2 * pg + r:2 * pg + r + 1,
                                       t * T:(t + 1) * T]
                            .rearrange("p o n -> p (o n)"))
                        ln.inc(ins, 16)
                        M[("xp", t, pg, r)] = (ln, ln.n)

                def ginit(e):
                    SM.wait_all(e)
                    loadou(e, 0, 0)
                    loadou(e, 0, 1)
                    loadou(e, 0, 2)
                    for pg in range(4):
                        loadxp(e, 0, pg)
                Ge(ginit)

                def xs4(t, qf):
                    s0 = (8 * t + 2 * qf) % 6
                    return xs[:, s0:s0 + 2, :, :].rearrange(
                        "p a b n -> p (a b) n")

                def emit_group(t, qf, sg):
                    slot = (4 * t + qf) % 3

                    def tqg(e, t=t, qf=qf, sg=sg, slot=slot):
                        if (t, qf, sg) == (0, 0, 0):
                            SW.wait_mark(e, M[("wo",)])
                        elif sg > 0:
                            cwait(e, M[("va", t, qf, sg - 1)])
                        elif qf > 0:
                            cwait(e, M[("va", t, qf - 1, 3)])
                        else:
                            cwait(e, M[("va", t - 1, 3, 3)])
                        if sg == 0:
                            SD.wait_mark(e, M[("ou", t, qf, 0)])
                            SD.wait_mark(e, M[("ou", t, qf, 1)])
                        ins = None
                        for i in range(4):
                            s = 4 * sg + i
                            for j in range(4 * qf, 4 * qf + 4):
                                ins = e.matmul(
                                    pW[:, i, :],
                                    arena0[:, j, s * 128:(s + 1) * 128],
                                    ouv(slot)[:, j - 4 * qf, :],
                                    start=(j == 4 * qf),
                                    stop=(j == 4 * qf + 3))
                        C1.inc(ins)
                    Te(tqg)
                    M[("tg", t, qf, sg)] = C1.n

                    ab4 = abuf[:, t % 2, 4 * sg:4 * sg + 4, :]

                    def vag(e, t=t, qf=qf, sg=sg, ab4=ab4):
                        C1.wait(e, M[("tg", t, qf, sg)])
                        if qf == 0 and t >= 2:
                            SM.wait_mark(e, M[("ystore", t - 2)])
                            if sg == 0:
                                C1.wait(e, M[("tsgm", t - 2, 3)])
                        if sg == qf:
                            for pg in (2 * qf, 2 * qf + 1):
                                SD.wait_mark(e, M[("xp", t, pg, 0)])
                                SD.wait_mark(e, M[("xp", t, pg, 1)])
                        pW4 = pW[:, :, :]
                        if qf == 0:
                            if sg == 0:
                                e.tensor_tensor(ab4, pW4, xs4(t, 0), ALU.add)
                            else:
                                e.tensor_copy(ab4, pW4)
                        else:
                            e.tensor_tensor(ab4, ab4, pW4, ALU.add)
                            if sg == qf:
                                e.tensor_tensor(ab4, ab4, xs4(t, qf),
                                                ALU.add)
                    Ve(vag)
                    M[("va", t, qf, sg)] = (C3, C3.n)

                def emit_stats(t, sg):
                    if True:
                        asq4 = asq4a if sg % 2 == 0 else asq4b[:, :, :]
                        ab4b = ab4ba if sg % 2 == 0 else ab4bb[:, :, :]

                        def ssg(e, t=t, sg=sg, asq4=asq4, ab4b=ab4b):
                            cwait(e, M[("va", t, 3, sg)])
                            if sg >= 2:
                                C1.wait(e, M[("tsgm", t, sg - 2)])
                            elif t > 0:
                                C1.wait(e, M[("tsgm", t - 1, sg + 2)])
                            e.activation(asq4,
                                         abuf[:, t % 2, 4 * sg:4 * sg + 4, :],
                                         AF.Square)
                            e.activation(ab4b,
                                         abuf[:, t % 2, 4 * sg:4 * sg + 4, :],
                                         AF.Copy)
                        Se(ssg)

                        def tsgm(e, t=t, sg=sg):
                            C2.wait(e)
                            if sg >= 2:
                                C3.wait(e, M[("vsq", t, sg - 2)])
                            elif t > 0:
                                C3.wait(e, M[("ln1", t - 1)])
                                C3.wait(e, M[("vud", t - 1)])
                                if sg == 0:
                                    C3.wait(e, M[("vr3", t - 1)])
                            dst = pX if sg % 2 == 0 else pY
                            asq4 = asq4a if sg % 2 == 0 else asq4b[:, :, :]
                            ab4b = ab4ba if sg % 2 == 0 else ab4bb[:, :, :]
                            ins = None
                            for i in range(4):
                                ins = e.matmul(dst[0:1, :], onesb[:],
                                               asq4[:, i, :],
                                               start=(i == 0), stop=(i == 3))
                            for i in range(4):
                                ins = e.matmul(pStat[0:1, :], onesb[:],
                                               ab4b[:, i, :],
                                               start=(sg == 0 and i == 0),
                                               stop=(sg == 3 and i == 3))
                            for i in range(4):
                                s = 4 * sg + i
                                ins = e.matmul(pR[:], simn[:, s, :],
                                               abuf[:, t % 2, s, :],
                                               start=(s == 0), stop=False)
                            C1.inc(ins)
                        Te(tsgm)
                        M[("tsgm", t, sg)] = C1.n

                        def vsqg(e, t=t, sg=sg):
                            C1.wait(e, M[("tsgm", t, sg)])
                            src_ = pX if sg % 2 == 0 else pY
                            if sg == 0:
                                e.tensor_copy(rsq, src_[0:1, :])
                            else:
                                e.tensor_tensor(rsq, rsq, src_[0:1, :],
                                                ALU.add)
                        Ve(vsqg)
                        M[("vsq", t, sg)] = C3.n

                def emit_gld(t, qf):
                    def gld(e, t=t, qf=qf):
                        C1.wait(e, M[("tg", t, qf, 0)])
                        P = 4 * t + qf
                        if P >= 1:
                            tp, qp = divmod(P - 1, 4)
                            cwait(e, M[("va", tp, qp, 3)])
                        if P + 2 < 4 * nchunks:
                            t2, q2 = divmod(P + 2, 4)
                            loadou(e, t2, q2)
                            loadxp(e, t2, 2 * q2)
                            loadxp(e, t2, 2 * q2 + 1)
                    Ge(gld)

                def emit_gpq(qf):
                    def gpq(e, qf=qf):
                        C1.wait(e, M[("tg", nchunks - 1, qf, 3)])
                        for j in range(4 * qf, 4 * qf + 4):
                            ln = (SW if j % 2 == 0 else SD).group(e)
                            ins = e.dma_start(
                                arena0[:, j, :],
                                fmap(pallh["pallq"])[:, j:j + 1, :]
                                .rearrange("p o n -> p (o n)"))
                            ln.inc(ins, 16)
                            M.setdefault(("pqm",), []).append((ln, ln.n))
                    Ge(gpq)

                def emit_vr(t):
                    def vr(e, t=t):
                        C1.wait(e, M[("tsgm", t, 3)])
                        e.tensor_scalar(rmu, pStat[0:1, :], 1.0 / D, None,
                                        ALU.mult)
                        e.tensor_tensor(rf, rmu, rmu, ALU.mult)
                        e.tensor_scalar(rvar, rsq, 1.0 / D, None, ALU.mult)
                        e.tensor_tensor(rvar, rvar, rf, ALU.subtract)
                    Ve(vr)
                    M[("vrd", t)] = C3.n

                    def sr(e):
                        C3.wait(e)
                        e.activation(rsinv, rvar, AF.Sqrt, bias=1e-5)
                        e.activation(rf, rvar, AF.Sqrt, scale=float(D))
                    Se(sr)

                    def vr2(e):
                        C2.wait(e)
                        e.reciprocal(rsinv, rsinv)
                        e.tensor_tensor(rf, rf, rsinv, ALU.mult)
                        e.tensor_scalar(rf, rf, 1e-12, None, ALU.max)
                        e.reciprocal(rf, rf)
                        e.tensor_tensor(rf, rsinv, rf, ALU.mult)
                    Ve(vr2)
                    M[("vr2", t)] = C3.n

                def emit_tb(t):
                    def tb(e, t=t):
                        C3.wait(e, M[("vr2", t)])
                        e.matmul(pX[:], o1x128[:], rmu, start=True, stop=True)
                        ins = e.matmul(pY[:], o1x128[:], rsinv,
                                       start=True, stop=True)
                        C1.inc(ins)
                    Te(tb)
                    M[("tb", t)] = C1.n

                def emit_vy(t, g):
                    def vy(e, t=t, g=g):
                        if g == 0:
                            C1.wait(e, M[("tb", t)])
                        ab4 = abuf[:, t % 2, 4 * g:4 * g + 4, :]
                        bcX = pX[:].rearrange("p (o b) -> p o b", o=1) \
                                   .broadcast_to([128, 4, T])
                        bcY = pY[:].rearrange("p (o b) -> p o b", o=1) \
                                   .broadcast_to([128, 4, T])
                        e.tensor_tensor(ab4, ab4, bcX, ALU.subtract)
                        e.tensor_tensor(ab4, ab4, bcY, ALU.mult)
                    Ve(vy)
                    if g == 3:
                        M[("ln1", t)] = C3.n

                def emit_ys(t):
                    def ys(e, t=t):
                        C3.wait(e, M[("ln1", t)])
                        ln = SM.group(e)
                        for grp in range(4):
                            ln.inc(e.dma_start(
                                fmap(y32_s)[:, 4 * grp:4 * grp + 4,
                                            t * T:(t + 1) * T],
                                abuf[:, t % 2, 4 * grp:4 * grp + 4, :]), 16)
                        M[("ystore", t)] = (ln, ln.n)
                    Ye(ys)

                def emit_tr2(t):
                    def tr2(e, t=t):
                        C3.wait(e, M[("ln1", t)])
                        e.matmul(pR[:], negcs[:], rmu, start=False, stop=True)
                        ins = e.matmul(pX[0:16, :], o1x16[:], rf,
                                       start=True, stop=True)
                        C1.inc(ins)
                    Te(tr2)
                    M[("tr2", t)] = C1.n

                def emit_vr3(t):
                    def vr3(e, t=t):
                        C1.wait(e, M[("tr2", t)])
                        e.tensor_copy(r16u[0:16, :], pX[0:16, :])
                        e.tensor_tensor(r16u[32:48, :], pR[:], r16u[0:16, :],
                                        ALU.mult)
                        e.tensor_scalar(r16u[32:48, :], r16u[32:48, :],
                                        sigG[:], None, ALU.subtract)
                    Ve(vr3)
                    M[("vr3", t)] = C3.n

                def emit_tt(t):
                    def tt(e, t=t):
                        C3.wait(e, M[("vr3", t)])
                        ins = None
                        for u in range(4):
                            ins = e.transpose(
                                pX[:, u * 16:(u + 1) * 16],
                                r16u[32:48, u * 128:(u + 1) * 128],
                                idf32[32:48, :])
                        C1.inc(ins)
                    Te(tt)
                    M[("ttm", t)] = C1.n

                def emit_vu(t):
                    def vu(e, t=t):
                        C1.wait(e, M[("ttm", t)])
                        d3 = lambda ap: ap.rearrange("p (a b) -> p a b", a=4)
                        bc = lambda col: col.rearrange("p (a o) -> p a o", o=1) \
                                            .broadcast_to([128, 4, 16])
                        lt = d3(rt16[:, 0, :])
                        e.tensor_copy(lt, d3(pX[:, 0:64]))
                        m1 = rtc[:, 0:4]
                        e.reduce_max(m1.rearrange("p (a o) -> p a o", o=1), lt,
                                     axis=AX.X)
                        ismax = d3(rt16[:, 1, :])
                        e.tensor_tensor(ismax, lt, bc(m1), ALU.is_ge)
                        l2m = d3(rt16[:, 2, :])
                        e.scalar_tensor_tensor(l2m, ismax, -2e9, lt,
                                               ALU.mult, ALU.add)
                        m2 = rtc[:, 4:8]
                        e.reduce_max(m2.rearrange("p (a o) -> p a o", o=1), l2m,
                                     axis=AX.X)
                        fb = d3(rt16[:, 1, :])
                        e.tensor_tensor(fb, lt, bc(m2), ALU.is_ge)
                        inact = rtc[:, 8:12]
                        e.tensor_single_scalar(inact, m1, 0.0, ALU.is_le)
                        mpos = d3(rt16[:, 3, :])
                        e.tensor_single_scalar(mpos, lt, 0.0, ALU.is_gt)
                        dmask = d3(rt16[:, 4, :])
                        e.tensor_tensor(dmask, fb, mpos, ALU.subtract)
                        tmask = d3(rt16[:, 2, :])
                        e.tensor_tensor(tmask, dmask, bc(inact), ALU.mult)
                        mask = d3(rt16[:, 1, :])
                        e.tensor_tensor(mask, tmask, mpos, ALU.add)
                        gated = d3(rt16[:, 2, :])
                        e.tensor_single_scalar(gated, lt, 0.0, ALU.max)
                        t1_ = d3(rt16[:, 3, :])
                        e.tensor_tensor(t1_, gated, mask, ALU.mult)
                        t2_ = d3(rt16[:, 4, :])
                        e.tensor_scalar(t2_, mask, -1.0, 1e9, ALU.add, ALU.mult)
                        marg = d3(rt16[:, 3, :])
                        e.tensor_tensor(marg, t1_, t2_, ALU.add)
                    Ve(vu)
                    M[("vud", t)] = C3.n

                    def su(e):
                        C3.wait(e)
                        e.activation(rt16[:, 0, :].rearrange(
                                         "p (a b) -> p a b", a=4),
                                     rt16[:, 3, :].rearrange(
                                         "p (a b) -> p a b", a=4),
                                     AF.Exp)
                    Se(su)

                def emit_vu2(t):
                    def vu2(e, t=t):
                        C2.wait(e)
                        d3 = lambda ap: ap.rearrange("p (a b) -> p a b", a=4)
                        bc = lambda col: col.rearrange("p (a o) -> p a o", o=1) \
                                            .broadcast_to([128, 4, 16])
                        p_ = d3(rt16[:, 0, :])
                        S_ = rtc[:, 0:4]
                        e.reduce_sum(S_.rearrange("p (a o) -> p a o", o=1), p_,
                                     axis=AX.X)
                        e.reciprocal(S_, S_)
                        t3 = d3(rt16[:, 2, :])
                        e.tensor_tensor(t3, p_, d3(rt16[:, 1, :]), ALU.mult)
                        e.tensor_tensor(
                            wrt[:, 4 * t:4 * t + 4, :], t3, bc(S_), ALU.mult)
                    Ve(vu2)
                    M[("vu2", t)] = C3.n

                for t in range(nchunks):
                    for qf in range(4):
                        for sg in range(4):
                            emit_group(t, qf, sg)
                            if t > 0:
                                p = t - 1
                                if qf == 0:
                                    emit_stats(p, sg)
                                elif qf == 1:
                                    if sg == 0:
                                        emit_vr(p)
                                    elif sg == 1:
                                        emit_tb(p)
                                    elif sg == 2:
                                        emit_vy(p, 0)
                                        emit_vy(p, 1)
                                    else:
                                        emit_vy(p, 2)
                                        emit_vy(p, 3)
                                        emit_ys(p)
                                elif qf == 2:
                                    if sg == 0:
                                        emit_tr2(p)
                                        emit_vr3(p)
                                    elif sg == 1:
                                        emit_tt(p)
                                        emit_vu(p)
                                    elif sg == 2:
                                        emit_vu2(p)
                        emit_gld(t, qf)
                        if t == nchunks - 1:
                            emit_gpq(qf)
                tl = nchunks - 1
                for sg in range(4):
                    emit_stats(tl, sg)
                emit_vr(tl)
                emit_tb(tl)
                for g in range(4):
                    emit_vy(tl, g)
                emit_ys(tl)
                emit_tr2(tl)
                emit_vr3(tl)
                emit_tt(tl)
                emit_vu(tl)
                emit_vu2(tl)

        def phase3a(vtm, qfm, kfm, oarena, aosb):
            T = 256
            nchunks = L // T
            srcarena = [arena0, oarena, arena0]
            with contextlib.ExitStack() as EE:
                ybf = EE.enter_context(
                    nc.sbuf_tensor("ybf", [128, 2, NCH, T], BF16))
                qtm = EE.enter_context(
                    nc.sbuf_tensor("qtm", [128, NT, 128], BF16))
                ktm2 = EE.enter_context(
                    nc.sbuf_tensor("ktm2", [128, NT, 128], BF16))
                wstage = EE.enter_context(
                    nc.sbuf_tensor("wstage", [128, 8, 128], BF16))
                ybf2 = aosb[:].rearrange("p a b -> p (a b)").rearrange(
                    "p (j n) -> p j n", j=NCH)

                def ybv(gc):
                    sl = gc % 3
                    return ybf[:, sl, :, :] if sl < 2 else ybf2

                def loady3(e, gc):
                    tc = gc % 16
                    ln = SD.group(e)
                    ins = e.dma_start(
                        ybv(gc),
                        fmap(y32_s)[:, :, tc * T:(tc + 1) * T])
                    ln.inc(ins, 16)
                    M[("ybg", gc)] = (ln, ln.n)

                accf = rt16[:, 0:2, :].rearrange("p a b -> p (a b)")
                accf2 = rt16[:, 2:4, :].rearrange("p a b -> p (a b)")
                with nc.psum_tensor("pPE", [128, 2, D], F32) as pPE:
                    def gwk(e):
                        M[("pld", 1)] = load_arena(e, oarena,
                                                   pallh["pallk"], SW)
                        SM.wait_all(e)
                        for gc0 in range(3):
                            loady3(e, gc0)
                    Ge(gwk)
                    for pi, (wname, dst) in enumerate(
                            [("pallq", qtm), ("pallk", ktm2), ("pallv", vtm)]):
                        wsrc = srcarena[pi]
                        for t in range(nchunks):
                            gc = 16 * pi + t

                            def gl(e, gc=gc):
                                if gc + 3 < 48:
                                    C1.wait(e, M[("mmg", gc)])
                                    loady3(e, gc + 3)

                            for tau in range(2):
                                g = 2 * t + tau

                                def tm_(e, pi=pi, t=t, tau=tau, g=g, gc=gc,
                                        wsrc=wsrc):
                                    SD.wait_mark(e, M[("ybg", gc)])
                                    if g == 0:
                                        if pi == 0:
                                            for mk in M[("pqm",)]:
                                                SD.wait_mark(e, mk)
                                        else:
                                            SW.wait_mark(e, M[("pld", pi)])
                                    if g >= 2:
                                        C3.wait(e, M[("acc", pi, g - 2)])
                                    elif pi > 0:
                                        C3.wait(e, M[("accend", pi - 1)])
                                    ins = None
                                    for j in range(NCH):
                                        for nn in range(4):
                                            ins = e.matmul(
                                                pPE[:, g % 2,
                                                    nn * 512:(nn + 1) * 512],
                                                ybv(gc)[:, j,
                                                        tau * 128:(tau + 1) * 128],
                                                wsrc[:, j,
                                                     nn * 512:(nn + 1) * 512],
                                                start=(j == 0),
                                                stop=(j == NCH - 1))
                                    C1.inc(ins)
                                Te(tm_)
                                if tau == 1:
                                    M[("mmg", gc)] = C1.n

                                def vm(e, pi=pi, g=g, dst=dst):
                                    C1.wait(e)
                                    for hf, acc_ in ((0, accf), (1, accf2)):
                                        e.tensor_tensor(
                                            wstage[:, :, :],
                                            pPE[:, g % 2,
                                                hf * 1024:(hf + 1) * 1024]
                                            .rearrange("p (a b) -> p a b", a=8),
                                            wrt[:, g, 8 * hf:8 * hf + 8]
                                            .rearrange("p (a o) -> p a o", o=1)
                                            .broadcast_to([128, 8, 128]),
                                            ALU.mult)
                                        e.reduce_sum(
                                            acc_.rearrange("p (b o) -> p b o",
                                                           o=1),
                                            wstage[:, :, :].rearrange(
                                                "p a b -> p b a"),
                                            axis=AX.X)
                                    e.tensor_tensor(accf, accf, accf2, ALU.add)
                                    e.tensor_copy(dst[:, g, :], accf)
                                Ve(vm)
                                M[("acc", pi, g)] = C3.n
                            Ge(gl)
                        M[("mix", pi)] = C1.n
                        M[("accend", pi)] = C3.n

                        if pi == 0:
                            def gv(e):
                                C1.wait(e, M[("mix", 0)])
                                M[("pld", 2)] = load_arena(
                                    e, arena0, pallh["pallv"], SW)
                            Ge(gv)
                        elif pi == 1:
                            def go2(e):
                                C1.wait(e, M[("mix", 1)])
                                ln = SW.group(e)
                                for g4 in range(4):
                                    ins = e.dma_start(
                                        oarena[:, 4 * g4:4 * g4 + 4, :],
                                        oproj_h[:, 4 * g4:4 * g4 + 4, :])
                                    ln.inc(ins, 16)
                                M[("oproj",)] = (ln, ln.n)
                            Ge(go2)

                with nc.psum_tensor("pT3", [128, 512], BF16) as pT3:
                    for src_, dstf in [(qtm, qfm), (ktm2, kfm)]:
                        for gg in range(8):
                            def tt2(e, src_=src_, gg=gg):
                                C3.wait(e)
                                ins = None
                                for u in range(4):
                                    g = 4 * gg + u
                                    ins = e.transpose(
                                        pT3[:, u * 128:(u + 1) * 128],
                                        src_[:, g, :], identb[:])
                                C1.inc(ins)
                            Te(tt2)

                            def vt2(e, dstf=dstf, gg=gg):
                                C1.wait(e)
                                e.tensor_copy(
                                    dstf[:, gg * 512:(gg + 1) * 512], pT3[:])
                            Ve(vt2)

        def phase3b(vtm, qfm, kfm, aosb, oarena):
            def vos(e):
                SW.wait_mark(e, M[("oproj",)])
                for ei in range(E):
                    e.reduce_sum(osumf[:, ei:ei + 1], oarena[:, ei, :],
                                 axis=AX.X)
                e.tensor_copy(osum[:], osumf[:])
            Ve(vos)

            with nc.sbuf_tensor("Ssb", [128, L], F32) as Ssb, \
                 nc.sbuf_tensor("Pb", [128, 2, L], BF16) as Pb, \
                 nc.sbuf_tensor("PTs", [128, 512], BF16) as PTs, \
                 nc.sbuf_tensor("negm", [128, 1], F32) as negm, \
                 nc.psum_tensor("pS", [128, 4, 512], F32) as pS, \
                 nc.psum_tensor("pPT", [128, 512], BF16) as pPT, \
                 nc.psum_tensor("pAO", [128, 128], F32) as pAO:
                b5 = C5.n
                gkb = [0]
                sevd = {}

                def emit_strip_front(i):
                    Lk = (i + 1) * 128
                    nkb = (Lk + 511) // 512
                    gk0 = gkb[0]

                    b3c, b2c = C3.n, C2.n

                    def tS(e, i=i, Lk=Lk, nkb=nkb, gk0=gk0,
                           b3c=b3c, b2c=b2c):
                        for kb in range(nkb):
                            g = gk0 + kb + 1
                            if kb >= 4:
                                kp = kb - 4
                                if kp % 2 == 0:
                                    C3.wait(e, b3c + kp // 2 + 1)
                                else:
                                    C2.wait(e, b2c + (kp - 1) // 2 + 1)
                            elif g > 4:
                                cwait(e, sevd[g - 4])
                            N = min(512, Lk - kb * 512)
                            ins = e.matmul(pS[:, g % 4, 0:N],
                                           qfm[:, i * 128:(i + 1) * 128],
                                           kfm[:, kb * 512:kb * 512 + N],
                                           start=True, stop=True)
                            C5.inc(ins)
                    Te(tS)
                    gkb[0] += nkb

                    def sSv(e, i=i, Lk=Lk, nkb=nkb, gk0=gk0):
                        if i > 0:
                            C2.wait(e, M[("exp", i - 1)])
                        for kb in range(0, nkb, 2):
                            g = gk0 + kb + 1
                            C5.wait(e, b5 + g)
                            N = min(512, Lk - kb * 512)
                            e.tensor_scalar(Ssb[:, kb * 512:kb * 512 + N],
                                            pS[:, g % 4, 0:N],
                                            ISC, None, ALU.mult)
                            sevd[g] = (C3, C3.n)
                    Ve(sSv)

                    def sSs(e, i=i, Lk=Lk, nkb=nkb, gk0=gk0):
                        for kb in range(1, nkb, 2):
                            g = gk0 + kb + 1
                            C5.wait(e, b5 + g)
                            N = min(512, Lk - kb * 512)
                            e.activation(Ssb[:, kb * 512:kb * 512 + N],
                                         pS[:, g % 4, 0:N],
                                         AF.Copy, scale=ISC)
                            sevd[g] = (C2, C2.n)
                    Se(sSs)

                    def vS(e, i=i, Lk=Lk):
                        C2.wait(e)
                        e.tensor_tensor(Ssb[:, i * 128:(i + 1) * 128],
                                        Ssb[:, i * 128:(i + 1) * 128],
                                        causal[:], ALU.add)
                        e.reduce_max(negm[:], Ssb[:, 0:Lk], axis=AX.X,
                                     negate=True)
                    Ve(vS)
                    M[("vS", i)] = C3.n

                    def sE(e, i=i, Lk=Lk):
                        C3.wait(e)
                        if i >= 2:
                            C1.wait(e, M[("tTall", i - 2)])
                        e.activation(Pb[:, i % 2, 0:Lk], Ssb[:, 0:Lk], AF.Exp,
                                     bias=negm[:],
                                     accum_out=Lsum[:, i:i + 1])
                    Se(sE)
                    M[("exp", i)] = C2.n

                def emit_strip_back(i):
                    ngrp = (i + 4) // 4
                    for g in range(ngrp):
                        j0 = 4 * g
                        j1 = min(i, 4 * g + 3)

                        def tT(e, i=i, j0=j0, j1=j1):
                            C2.wait(e, M[("exp", i)])
                            if ("ptev",) in M:
                                C3.wait(e, M[("ptev",)])
                            ins = None
                            for j in range(j0, j1 + 1):
                                ins = e.transpose(
                                    pPT[:, (j - j0) * 128:(j - j0 + 1) * 128],
                                    Pb[:, i % 2, j * 128:(j + 1) * 128],
                                    identb[:])
                            C1.inc(ins)
                        Te(tT)
                        tmark = C1.n

                        def vT(e, j0=j0, j1=j1, tmark=tmark):
                            C1.wait(e, tmark)
                            if ("pvmm",) in M:
                                C1.wait(e, M[("pvmm",)])
                            e.tensor_copy(PTs[:, 0:(j1 - j0 + 1) * 128],
                                          pPT[:, 0:(j1 - j0 + 1) * 128])
                        Ve(vT)
                        M[("ptev",)] = C3.n

                        def tV(e, i=i, j0=j0, j1=j1):
                            C3.wait(e, M[("ptev",)])
                            if j0 == 0 and i > 0:
                                C2.wait(e, M[("aoev", i - 1)])
                            ins = None
                            for j in range(j0, j1 + 1):
                                ins = e.matmul(
                                    pAO[:],
                                    PTs[:, (j - j0) * 128:(j - j0 + 1) * 128],
                                    vtm[:, j, :],
                                    start=(j == 0), stop=(j == i))
                            C1.inc(ins)
                        Te(tV)
                        M[("pvmm",)] = C1.n

                    def sA(e, i=i):
                        C1.wait(e, M[("pvmm",)])
                        e.activation(aosb[:, i, :], pAO[:], AF.Copy)
                    Se(sA)
                    M[("aoev", i)] = C2.n
                    M[("tTall", i)] = C1.n

                for i in range(NT):
                    emit_strip_front(i)
                    if i > 0:
                        emit_strip_back(i - 1)
                emit_strip_back(NT - 1)

                def vw(e):
                    C2.wait(e, M[("aoev", NT - 1)])
                    e.reciprocal(Lsum[:], Lsum[:])
                    for g in range(NT):
                        e.tensor_scalar(wpr[:, g, :], wrt[:, g, :],
                                        Lsum[:, g:g + 1], None, ALU.mult)
                Ve(vw)
                M[("wpr",)] = C3.n

        def phase3c(aosb, oarena, kfm, qfm, vtm):
            T = 256
            nchunks = L // T
            with nc.sbuf_tensor("a2", [128, 2, NCH, T], F32) as a2, \
                 nc.sbuf_tensor("asq2", [128, 4, T], BF16) as asq2, \
                 nc.psum_tensor("pO2", [128, 4, 512], F32) as pO2, \
                 nc.psum_tensor("pS2", [1, 512], F32) as pS2, \
                 nc.psum_tensor("pB1", [128, 512], F32) as pB1, \
                 nc.psum_tensor("pB2", [128, 512], F32) as pB2, \
                 nc.psum_tensor("pWT", [128, 1024], BF16) as pWT:

                wstA = kfm[:].rearrange("p (e t c) -> p e t c", e=E, t=2)
                waoF = qfm[:].rearrange("p (e c) -> p e c", e=E)
                _rows = vtm[:].rearrange("p a b -> p (a b)").bitcast(F32)
                ys = _rows[:, 0:1024].rearrange("p (a b c) -> p a b c",
                                                a=2, b=2)
                qmu = _rows[0:1, 1024:1280]
                qvar = _rows[0:1, 1280:1536]
                qsq = _rows[0:1, 1536:1792]

                def loady(e, t, pg):
                    ln = SD.group(e)
                    ins = e.dma_start(
                        ys[:, pg % 2, :, :],
                        fmap(y32_s)[:, 2 * pg:2 * pg + 2, t * T:(t + 1) * T])
                    ln.inc(ins, 16)
                    M[("ys", t, pg)] = (ln, ln.n)

                def ginit(e):
                    SM.wait_all(e)
                    loady(e, 0, 0)
                    loady(e, 0, 1)
                Ge(ginit)

                def emit_vscA(t):
                    def vscA(e, t=t):
                        if t == 0:
                            C3.wait(e, M[("wpr",)])
                        else:
                            C1.wait(e, M[("wtrA", t - 1)])
                        e.tensor_tensor(
                            wstA[:, :, :, :],
                            aosb[:, 2 * t:2 * t + 2, :].rearrange(
                                "p (o t) c -> p o t c", o=1)
                            .broadcast_to([128, E, 2, 128]),
                            wpr[:, 2 * t:2 * t + 2, :].rearrange(
                                "p t e -> p e t").rearrange(
                                "p e (t o) -> p e t o", o=1)
                            .broadcast_to([128, E, 2, 128]),
                            ALU.mult)
                    Ve(vscA)
                    M[("wscA", t)] = C3.n

                def emit_wao(t):
                    for g4 in range(4):
                        def ttr(e, t=t, g4=g4):
                            C3.wait(e, M[("wscA", t)])
                            if (t, g4) != (0, 0):
                                C3.wait(e, M[("wevA",)])
                            ins = None
                            for k in range(8):
                                ei, tau = (8 * g4 + k) // 2, (8 * g4 + k) % 2
                                ins = e.transpose(
                                    pWT[:, k * 128:(k + 1) * 128],
                                    wstA[:, ei, tau, :], identb[:])
                            C1.inc(ins)
                        Te(ttr)
                        M[("wtr", t, g4)] = C1.n

                        def vev(e, t=t, g4=g4):
                            C1.wait(e, M[("wtr", t, g4)])
                            if t > 0:
                                C1.wait(e, M[("o2mm", t - 1)])
                            e.tensor_copy(
                                waoF[:, :, :].rearrange(
                                    "p e (t c) -> p e t c", t=2)
                                [:, 4 * g4:4 * g4 + 4, :, :],
                                pWT[:].rearrange(
                                    "p (e t c) -> p e t c", e=4, t=2))
                        Ve(vev)
                        M[("wevA",)] = C3.n
                    M[("wtrA", t)] = C1.n
                    M[("wev", t, E - 1)] = C3.n

                def emit_out2(t):
                    bQ3 = C3.n
                    for q in range(4):
                        def tq(e, t=t, q=q, bQ3=bQ3):
                            C3.wait(e, M[("wev", t, E - 1)])
                            if q > 0:
                                C3.wait(e, bQ3 + q)
                            ins = None
                            for ei in range(E):
                                for s in range(4 * q, 4 * q + 4):
                                    ins = e.matmul(
                                        pO2[:, s % 4, 0:T],
                                        oarena[:, ei, s * 128:(s + 1) * 128],
                                        waoF[:, ei, :],
                                        start=(ei == 0), stop=(ei == E - 1))
                                if q == 0:
                                    ins = e.matmul(
                                        pS2[0:1, 0:T], osum[:, ei:ei + 1],
                                        waoF[:, ei, :],
                                        start=(ei == 0), stop=(ei == E - 1))
                            C1.inc(ins)
                        Te(tq)
                        M[("o2", t, q)] = C1.n

                        def vvQ(e, t=t, q=q):
                            C1.wait(e, M[("o2", t, q)])
                            if q == 0 and t >= 2:
                                SM.wait_mark(e, M[("zstore", t - 2)])
                            SD.wait_mark(e, M[("ys", t, 2 * q)])
                            SD.wait_mark(e, M[("ys", t, 2 * q + 1)])
                            e.tensor_tensor(
                                a2[:, t % 2, 4 * q:4 * q + 4, :],
                                pO2[:, :, 0:T],
                                ys[:, :, :, :].rearrange("p a b c -> p (a b) c"),
                                ALU.add)
                        Ve(vvQ)

                        def ssQ(e, t=t, q=q):
                            e.tensor_tensor(asq2[:, :, :],
                                            a2[:, t % 2, 4 * q:4 * q + 4, :],
                                            a2[:, t % 2, 4 * q:4 * q + 4, :],
                                            ALU.mult)
                        Ve(ssQ)
                        M[("sq3", t, q)] = C3.n

                        def tsQ(e, t=t, q=q):
                            C3.wait(e, M[("sq3", t, q)])
                            if q == 0 and t > 0:
                                C3.wait(e, M[("vz", t - 1)])
                            e.matmul(pB1[0:1, 0:2 * T], onesb[:],
                                     asq2[:, 0:2, :].rearrange(
                                         "p a b -> p (a b)"),
                                     start=True, stop=True)
                            ins = e.matmul(pB2[0:1, 0:2 * T], onesb[:],
                                           asq2[:, 2:4, :].rearrange(
                                               "p a b -> p (a b)"),
                                           start=True, stop=True)
                            C1.inc(ins)
                        Te(tsQ)

                        def vsqQ(e, t=t, q=q):
                            C1.wait(e)
                            if q == 0:
                                e.tensor_copy(qsq[:], pB1[0:1, 0:T])
                            else:
                                e.tensor_tensor(qsq[:], qsq[:],
                                                pB1[0:1, 0:T], ALU.add)
                            e.tensor_tensor(qsq[:], qsq[:], pB1[0:1, T:2 * T],
                                            ALU.add)
                            e.tensor_tensor(qsq[:], qsq[:], pB2[0:1, 0:T],
                                            ALU.add)
                            e.tensor_tensor(qsq[:], qsq[:], pB2[0:1, T:2 * T],
                                            ALU.add)
                        Ve(vsqQ)

                        def gly(e, t=t, q=q):
                            C1.wait(e)
                            if q < 3:
                                loady(e, t, 2 * q + 2)
                                loady(e, t, 2 * q + 3)
                            elif t + 1 < nchunks:
                                loady(e, t + 1, 0)
                                loady(e, t + 1, 1)
                        Ge(gly)
                    M[("o2mm", t)] = C1.n

                def emit_rows(t):
                    def vr(e, t=t):
                        C1.wait(e)
                        if t > 0:
                            C1.wait(e, M[("bc", t - 1)])
                        e.tensor_scalar(qvar[:], qsq[:], 1.0 / D, None,
                                        ALU.mult)
                        e.tensor_scalar(qmu[:], pS2[0:1, 0:T], 1.0 / D, None,
                                        ALU.mult)
                        e.tensor_tensor(qsq[:], qmu[:], qmu[:], ALU.mult)
                        e.tensor_tensor(qvar[:], qvar[:], qsq[:],
                                        ALU.subtract)
                    Ve(vr)

                    def sr(e, t=t):
                        C3.wait(e)
                        e.activation(qvar[:], qvar[:], AF.Sqrt, bias=1e-5)
                    Se(sr)

                    def vr2(e, t=t):
                        C2.wait(e)
                        e.reciprocal(qvar[:], qvar[:])
                    Ve(vr2)
                    M[("rows", t)] = C3.n

                def emit_finish(t):
                    def tb(e, t=t):
                        C3.wait(e, M[("rows", t)])
                        e.matmul(pB1[:, 0:T], o1x128[:], qmu[:], start=True,
                                 stop=True)
                        ins = e.matmul(pB2[:, 0:T], o1x128[:], qvar[:],
                                       start=True, stop=True)
                        C1.inc(ins)
                        M[("bc", t)] = C1.n
                    Te(tb)

                    def vz(e, t=t):
                        C1.wait(e, M[("bc", t)])
                        bc1 = pB1[:, 0:T].rearrange("p (o b) -> p o b", o=1) \
                                         .broadcast_to([128, NCH, T])
                        bc2 = pB2[:, 0:T].rearrange("p (o b) -> p o b", o=1) \
                                         .broadcast_to([128, NCH, T])
                        e.tensor_tensor(a2[:, t % 2, :, :], a2[:, t % 2, :, :],
                                        bc1, ALU.subtract)
                        e.tensor_tensor(a2[:, t % 2, :, :], a2[:, t % 2, :, :],
                                        bc2, ALU.mult)
                    Ve(vz)
                    M[("vz", t)] = C3.n
                    M[("z", t)] = C3.n

                    def yz(e, t=t):
                        C3.wait(e, M[("z", t)])
                        ln = SM.group(e)
                        for grp in range(4):
                            ln.inc(e.dma_start(
                                fmap(z_fm)[:, 4 * grp:4 * grp + 4,
                                           t * T:(t + 1) * T],
                                a2[:, t % 2, 4 * grp:4 * grp + 4, :]), 16)
                        M[("zstore", t)] = (ln, ln.n)
                    Ye(yz)

                emit_vscA(0)
                emit_wao(0)
                for t in range(nchunks):
                    emit_out2(t)
                    if t + 1 < nchunks:
                        emit_vscA(t + 1)
                    emit_rows(t)
                    if t + 1 < nchunks:
                        emit_wao(t + 1)
                    emit_finish(t)

                def yfin(e):
                    SM.wait_all(e)
                    SD.wait_all(e)
                Ye(yfin)

        with contextlib.ExitStack() as B:
            warena1 = B.enter_context(
                nc.sbuf_tensor("warena1", [128, NCH, D], BF16))
            KVb = B.enter_context(nc.sbuf_tensor("KVb", [128, NH, 128], BF16))
            xbf2b = B.enter_context(
                nc.sbuf_tensor("xbf2b", [128, NCH, 512], BF16))
            phase0(warena1)
            barrier()
            phase1(warena1, KVb, xbf2b)
            barrier()
            phase2a(warena1)
            phase2b(warena1, KVb, xbf2b)
            barrier()

        with contextlib.ExitStack() as DD:
            vtm = DD.enter_context(nc.sbuf_tensor("vtm3", [128, NT, 128], BF16))
            aosb = DD.enter_context(nc.sbuf_tensor("aosb", [128, NT, 128], BF16))
            qfm = DD.enter_context(nc.sbuf_tensor("qfm", [128, L], BF16))
            kfm = DD.enter_context(nc.sbuf_tensor("kfm", [128, L], BF16))
            oarena = DD.enter_context(nc.sbuf_tensor("oarena", [128, E, D], BF16))
            phase2c(oarena, kfm, qfm, vtm, aosb)
            barrier()
            phase3a(vtm, qfm, kfm, oarena, aosb)
            barrier()
            phase3b(vtm, qfm, kfm, aosb, oarena)
            barrier()
            phase3c(aosb, oarena, kfm, qfm, vtm)

        for c in chains:
            c.check()
    return nc


_NC_CACHE = {}


def _get_program():
    key = ("prog", DEBUG)
    if key not in _NC_CACHE:
        _NC_CACHE[key] = build_program()
    return _NC_CACHE[key]


def kernel(L0_attn_x, L0_attn_wq, L0_attn_wk, L0_attn_wv, L0_attn_wu, L0_attn_wo,
           L0_ffn_sim_matrix, L0_ffn_gates, L0_ffn_q_proj, L0_ffn_k_proj,
           L0_ffn_v_proj, L0_ffn_o_proj):
    nc_prog = _get_program()
    c = np.ascontiguousarray
    shared = {
        "wqT": c(L0_attn_wq.T), "wkT": c(L0_attn_wk.T), "wvT": c(L0_attn_wv.T),
        "wuT": c(L0_attn_wu.T), "woT": c(L0_attn_wo.T),
        "pallq": c(L0_ffn_q_proj.transpose(1, 0, 2).reshape(D, D)),
        "pallk": c(L0_ffn_k_proj.transpose(1, 0, 2).reshape(D, D)),
        "pallv": c(L0_ffn_v_proj.transpose(1, 0, 2).reshape(D, D)),
        "oprojh": c(L0_ffn_o_proj.transpose(1, 0, 2)),
        "sim": c(L0_ffn_sim_matrix),
        "gates": c(L0_ffn_gates.reshape(E, 1)),
    }
    in_maps = []
    for b in range(8):
        m = dict(shared)
        m["x_fm"] = c(L0_attn_x[b].T)
        in_maps.append(m)
    res = run_bass_kernel_spmd(nc_prog, in_maps, core_ids=list(range(8)))
    out = np.stack([res.results[b]["z_fm"].T for b in range(8)], axis=0)
    return np.ascontiguousarray(out.astype(np.float32))
